# version 06
# baseline (speedup 1.0000x reference)
import math
from contextlib import ExitStack

import numpy as np
import ml_dtypes

import concourse.bass as bass
import concourse.mybir as mybir
from concourse.bass_utils import run_bass_kernel_spmd

F32 = mybir.dt.float32
BF16 = mybir.dt.bfloat16
I32 = mybir.dt.int32
AF = mybir.ActivationFunctionType
ALU = mybir.AluOpType

HID, NH = 2048, 8
T, NCH = 512, 4
TP, NPC = 1024, 2
SPAN = 2048
EPS = 1e-5
_2PI = 2.0 * math.pi
C1 = float(np.float32(_2PI))
C2 = float(np.float32(_2PI - np.float64(np.float32(_2PI))))
C3 = float(np.float32(_2PI - np.float64(np.float32(_2PI))
                      - np.float64(np.float32(_2PI - np.float64(np.float32(_2PI))))))
HALF_PI = float(np.float32(math.pi / 2))
PI = float(np.float32(math.pi))
N2PI = -float(np.float32(_2PI))
INV_2PI = float(np.float32(1.0 / _2PI))
LN1E4 = float(math.log(10000.0))
GAMMAS = [float(v) for v in (1.0 - np.exp(np.linspace(np.log(1 / 32), np.log(1 / 512), 8)))]
LNG = [float(np.float32(math.log(v))) for v in GAMMAS]
LNG_LO = [float(np.float64(math.log(v)) - np.float64(np.float32(math.log(v))))
          for v in GAMMAS]
THS_HI = float(np.float32(-LN1E4 / 127.0))
THS_LO = float(np.float64(-LN1E4 / 127.0) - np.float64(THS_HI))
THB_HI = float(np.float32(-LN1E4 * 64.0 / 127.0))
THB_LO = float(np.float64(-LN1E4 * 64.0 / 127.0) - np.float64(THB_HI))
GT = [float(np.float32(v) ** T) for v in GAMMAS]
GTP = [float(np.float32(v) ** TP) for v in GAMMAS]

ENGINES = ("sync", "gpsimd", "tensor", "vector", "scalar")
DEBUG_TAPS = []
SHUF_MASK = []
for _i in range(0, 32, 2):
    SHUF_MASK += [_i + 1, _i]


class Buf:
    __slots__ = ("name", "w", "rs")

    def __init__(self, name):
        self.name = name
        self.w = None
        self.rs = []


class Builder:
    def __init__(self):
        self.sched = []
        self.cnt = {}
        self.floor = {e: {} for e in ENGINES}
        self.sem_handles = {}
        self.bufs = {}

    def buf(self, name):
        if name not in self.bufs:
            self.bufs[name] = Buf(name)
        return self.bufs[name]

    def step(self, eng, emits, reads=(), writes=(), mark=None, dma=False, extra=()):
        evs = [b.w for b in reads]
        for b in writes:
            evs.append(b.w)
            evs.extend(b.rs)
        evs.extend(extra)
        if dma and mark is not None and self.cnt.get(mark, 0) > 0:
            evs.append((mark, self.cnt[mark], None))
        ws = []
        fl = self.floor[eng]
        for ev in evs:
            if ev is None:
                continue
            s, v, peng = ev
            if v <= 0 or s is None:
                continue
            if fl.get(s, 0) >= v:
                continue
            fl[s] = v
            ws.append((s, v))
        incs = []
        ev = None
        if mark is not None:
            if dma:
                amt = 16 * len(emits)
                incs = [(mark, 16)] * len(emits)
            elif eng in ("vector", "scalar", "gpsimd"):
                amt = len(emits)
                incs = [(mark, 1)] * len(emits)
            else:
                amt = 1
                incs = [(mark, 1)]
            base = self.cnt.get(mark, 0)
            self.cnt[mark] = base + amt
            ev = (mark, self.cnt[mark], None if dma else eng)
        else:
            base = 0
        self.sched.append((eng, ws, emits, incs, dma, base))
        if ev is not None:
            for b in writes:
                b.w = ev
                b.rs = []
            for b in reads:
                b.rs.append(ev)
        else:
            for b in writes:
                if b.w is None:
                    b.w = (None, 0, eng)
        return ev

    def check(self):
        qs = {e: [] for e in ENGINES}
        for it in self.sched:
            qs[it[0]].append(it)
        idx = {e: 0 for e in ENGINES}
        sim = {}
        prog = True
        while prog:
            prog = False
            for e in ENGINES:
                while idx[e] < len(qs[e]):
                    _, ws, emits, incs, dma, _base = qs[e][idx[e]]
                    if all(sim.get(s, 0) >= v for s, v in ws):
                        for s, a in incs:
                            sim[s] = sim.get(s, 0) + a
                        idx[e] += 1
                        prog = True
                    else:
                        break
        stuck = {e: i for e, i in idx.items() if i < len(qs[e])}
        if stuck:
            lines = []
            for e, i in stuck.items():
                _, ws, _, _, _ = qs[e][i]
                lines.append(f"{e}[{i}/{len(qs[e])}] waits " +
                             ",".join(f"{s}>={v}(at {sim.get(s, 0)})" for s, v in ws))
            raise RuntimeError("schedule deadlock:\n" + "\n".join(lines))

    def emit_all(self, block):
        sems = self.sem_handles

        def make(name):
            def f(eng_obj):
                for eng, ws, emits, incs, dma, base in self.sched:
                    if eng != name:
                        continue
                    for s, v in ws:
                        eng_obj.wait_ge(sems[s], v)
                    strict = (not dma) and eng in ("vector", "scalar", "gpsimd") \
                        and incs and len(incs) == len(emits)
                    insts = []
                    for i, em in enumerate(emits):
                        if strict and i > 0:
                            eng_obj.wait_ge(sems[incs[0][0]], base + i)
                        inst = em(eng_obj)
                        insts.append(inst)
                        if incs and len(incs) == len(emits):
                            inst.then_inc(sems[incs[i][0]], incs[i][1])
                    if incs and len(incs) != len(emits):
                        insts[-1].then_inc(sems[incs[0][0]], 1)
            return f
        block.sync(make("sync"))
        block.gpsimd(make("gpsimd"))
        block.tensor(make("tensor"))
        block.vector(make("vector"))
        block.scalar(make("scalar"))


class Arena:
    def __init__(self, nbytes, esz, name):
        self.esz = esz
        self.free = [(0, nbytes)]
        self.hwm = 0
        self.name = name

    def alloc(self, nelem):
        nb = ((nelem * self.esz + 63) // 64) * 64
        for i, (a, b) in enumerate(self.free):
            if b - a >= nb:
                if a + nb == b:
                    self.free.pop(i)
                else:
                    self.free[i] = (a + nb, b)
                self.hwm = max(self.hwm, a + nb)
                return a // self.esz, nb
        raise RuntimeError(f"{self.name} OOM want {nb} free {self.free}")

    def freed(self, h):
        off, nb = h
        a = off * self.esz
        self.free.append((a, a + nb))
        self.free.sort()
        m = []
        for s in self.free:
            if m and m[-1][1] >= s[0]:
                m[-1] = (m[-1][0], max(m[-1][1], s[1]))
            else:
                m.append(list(s))
        self.free = [tuple(x) for x in m]


def build_program(stop_after=4, PH2_SUB=5):
    nc = bass.Bass("TRN2", target_bir_lowering=False)
    B = Builder()
    ctx = ExitStack()
    g = B.buf

    d_x = nc.dram_tensor("x_fm", [HID, SPAN], F32, kind="ExternalInput").ap()
    d_xp = nc.dram_tensor("xp_fm", [HID, SPAN], F32, kind="ExternalInput").ap()
    d_posb = nc.dram_tensor("pos_base", [128, 1], F32, kind="ExternalInput").ap()
    d_pflag = nc.dram_tensor("pflag", [1, TP], F32, kind="ExternalInput").ap()
    WN = ["wq", "wk", "wv", "wg", "wo", "wupT", "wdnT", "we"]
    d_w = {n: nc.dram_tensor(n, [HID, HID], F32, kind="ExternalInput").ap() for n in WN}
    d_wgT = nc.dram_tensor("wgT", [HID, 8], F32, kind="ExternalInput").ap()
    d_b = {n: nc.dram_tensor("b" + n, [1, HID], F32, kind="ExternalInput").ap()
           for n in ["q", "k", "v", "g", "o"]}
    d_gnw = nc.dram_tensor("gnw", [128, 16], F32, kind="ExternalInput").ap()
    d_gnb = nc.dram_tensor("gnb", [128, 16], F32, kind="ExternalInput").ap()
    d_bgc = nc.dram_tensor("bgcol", [128, 16], F32, kind="ExternalInput").ap()
    d_out = nc.dram_tensor("out_fm", [16, 128, SPAN], F32, kind="ExternalOutput").ap()
    d_wbf = {n: nc.dram_tensor("scr_" + n, [8, 16, 128, 256], BF16, kind="Internal").ap()
             for n in WN}
    d_wgTbf = nc.dram_tensor("scr_wgT", [16, 128, 8], BF16, kind="Internal").ap()
    d_mask = nc.dram_tensor("scr_mask", [NH, 4, 128, T], BF16, kind="Internal").ap()
    d_y = nc.dram_tensor("scr_y", [16, 128, SPAN], F32, kind="Internal").ap()
    d_ffn = nc.dram_tensor("scr_ffn", [16, 128, SPAN], BF16, kind="Internal").ap()

    def sb(name, shape, dt):
        return ctx.enter_context(nc.sbuf_tensor(name, shape, dt))

    ident = sb("ident", [128, 128], BF16)
    identF = sb("identF", [128, 128], F32)
    ones_bf = sb("ones_bf", [1, T], BF16)
    ones_colbf = sb("ones_colbf", [128, 1], BF16)
    ones_bc = sb("ones_bc", [1, 128], BF16)
    oneh8 = sb("oneh8", [128, 64], BF16)
    oneh8T = sb("oneh8T", [8, 1024], BF16)
    rw8 = sb("rw8", [8, 128], F32)
    thcol = sb("thcol", [128, 2], F32)
    altsign = sb("altsign", [128, 1], F32)
    posb = sb("posb", [128, 1], F32)
    gnwc = sb("gnwc", [128, 16], F32)
    gnbc = sb("gnbc", [128, 16], F32)
    bgc = sb("bgc", [128, 16], F32)
    kdec = sb("kdec", [128, 32], F32)
    kdecP = sb("kdecP", [128, 64], F32)
    pfl = sb("pfl", [1, TP], BF16)
    rcols = sb("rcols", [128, 16], F32)
    S_fp = sb("S_fp", [128, 4096], F32)
    S_bf = sb("S_bf", [128, 512], BF16)
    aug_all = sb("aug_all", [2, 5 * HID], BF16)
    AUG = {"wq": 0, "wk": HID, "wv": 2 * HID, "wg": 3 * HID, "wo": 4 * HID,
           "wupT": 0, "we": HID}
    augw = sb("augw", [1, 256], F32)
    aug8 = sb("aug8", [1, 64], BF16)
    mrows = sb("mrows", [2, TP], BF16)
    s_sum = sb("s_sum", [1, 512], F32)
    s_sq = sb("s_sq", [1, 512], F32)
    s_mean = sb("s_mean", [1, 512], F32)
    s_tmp = sb("s_tmp", [1, 512], F32)
    s_r = sb("s_r", [1, 512], F32)
    s_t3 = sb("s_t3", [1, 512], F32)
    r2row = sb("r2row", [1, TP], F32)
    bcm = sb("bcm", [1, TP], BF16)
    bcm2 = sb("bcm2", [1, TP], BF16)
    grows = sb("grows", [8, 3 * T], F32)
    gtmp = sb("gtmp", [8, T], F32)
    gbf = sb("gbf", [8, T], BF16)
    gbf2 = sb("gbf2", [8, T], BF16)
    lgsb = sb("lgsb", [8, TP], BF16)
    lgt = sb("lgt", [128, 64], BF16)
    lgtf = sb("lgtf", [128, 64], F32)
    m8t = sb("m8t", [128, 8], F32)
    rwc = sb("rwc", [128, 8], F32)
    csi_t = sb("csi", [128, 512], I32)
    cvals = sb("cvals", [128, 23], F32)
    cvi1 = sb("cvi1", [128, 1], I32)
    CV_THS, CV_THB, CV_EPS, CV_256 = 0, 1, 2, 3
    CV_LNG = 4
    CV_ONE = 12
    CV_THS2, CV_THB2 = 13, 14
    CV_LNG2 = 15

    ABF = 101 * 1024
    AFW = 23 * 1024
    arb = Arena(ABF, 2, "bf16")
    arf = Arena(AFW, 4, "f32")
    t_bf = sb("arena_bf", [128, ABF // 2], BF16)
    t_f32 = sb("arena_f32", [128, AFW // 4], F32)

    def abf(n):
        o, nb = arb.alloc(n)
        return t_bf[:, o:o + n], (o, nb)

    def af32(n):
        o, nb = arf.alloc(n)
        return t_f32[:, o:o + n], (o, nb)

    ps = ctx.enter_context(nc.psum_tensor("ps", [128, 3584], F32))
    psb = ctx.enter_context(nc.psum_tensor("psb", [128, 1024], BF16))

    def bk(i, parts=128, width=512, off=0):
        return ps[0:parts, 512 * i + off:512 * i + off + width]

    pb = [g(f"ps{i}") for i in range(7)]

    for s in ["dg", "ds", "te", "ve", "se", "gp", "sy"]:
        B.sem_handles[s] = ctx.enter_context(nc.semaphore(s))

    def dma(eng, sem, out_ap, in_ap, reads=(), writes=()):
        return B.step(eng, [lambda e, o=out_ap, i=in_ap: e.dma_start(out=o, in_=i)],
                      reads=reads, writes=writes, mark=sem, dma=True)

    ENG_MARK = {"sync": "sy", "gpsimd": "gp", "tensor": "te", "vector": "ve",
                "scalar": "se"}

    def barrier():
        snap = [(s, v) for s, v in B.cnt.items() if v > 0]
        for e in ENGINES:
            B.step(e, [lambda eng: eng.nop()],
                   extra=[(s, v, None) for s, v in snap], mark=ENG_MARK[e])

    taps_made = {}

    def tap(name, ap, readbufs):
        if name in DEBUG_TAPS and name not in taps_made:
            dd = nc.dram_tensor("tap_" + name, list(ap.shape), ap.dtype,
                                kind="ExternalOutput").ap()
            dma("sync", "ds", dd, ap, reads=list(readbufs))
            taps_made[name] = True

    csc, _ = af32(512)
    csn, _ = af32(512)
    csr, _ = af32(512)
    csS, _ = af32(512)
    csC, _ = af32(512)

    B.step("gpsimd", [lambda e: e.memset(ident[:], 1.0),
                      lambda e: e.memset(identF[:], 1.0)],
           writes=[g("ident")], mark="gp")
    B.step("gpsimd", [lambda e: e.affine_select(ident[:], ident[:], pattern=[[1, 128]],
                                                compare_op=ALU.is_equal, fill=0.0,
                                                base=0, channel_multiplier=-1),
                      lambda e: e.affine_select(identF[:], identF[:], pattern=[[1, 128]],
                                                compare_op=ALU.is_equal, fill=0.0,
                                                base=0, channel_multiplier=-1)],
           writes=[g("ident")], mark="gp")
    B.step("vector", [lambda e: e.memset(cvi1[:], 1),
                      lambda e: e.memset(augw[:], 0.0),
                      lambda e: e.memset(cvals[:, CV_THS:CV_THS + 1], -LN1E4 / 127.0),
                      lambda e: e.memset(cvals[:, CV_THB:CV_THB + 1], -LN1E4 * 64.0 / 127.0),
                      lambda e: e.memset(cvals[:, CV_EPS:CV_EPS + 1], EPS),
                      lambda e: e.memset(cvals[:, CV_256:CV_256 + 1], 1.0 / 256.0),
                      lambda e: e.memset(cvals[:, CV_ONE:CV_ONE + 1], 1.0),
                      lambda e: e.memset(cvals[:, CV_THS2:CV_THS2 + 1], THS_LO),
                      lambda e: e.memset(cvals[:, CV_THB2:CV_THB2 + 1], THB_LO)] +
           [(lambda h: (lambda e: e.memset(cvals[:, CV_LNG2 + h:CV_LNG2 + h + 1],
                                           LNG_LO[h])))(h) for h in range(NH)] +
           [(lambda h: (lambda e: e.memset(cvals[:, CV_LNG + h:CV_LNG + h + 1],
                                           LNG[h])))(h) for h in range(NH)] +
           [lambda e: e.memset(ones_bf[:], 1.0),
                      lambda e: e.memset(ones_colbf[:], 1.0),
                      lambda e: e.memset(ones_bc[:], 1.0),
                      lambda e: e.memset(S_fp[:], 0.0)],
           writes=[g("const1"), g("S_fp"), g("cvals")], mark="ve")
    B.step("gpsimd", [lambda e: e.memset(oneh8[:], 1.0),
                      lambda e: e.memset(oneh8T[:], 1.0)],
           writes=[g("oneh8")], mark="gp")
    B.step("gpsimd", [lambda e: e.affine_select(oneh8[:], oneh8[:],
                                                pattern=[[-1, 8], [1, 8]],
                                                compare_op=ALU.is_equal, fill=0.0,
                                                base=0, channel_multiplier=0),
                      lambda e: e.affine_select(oneh8T[:], oneh8T[:],
                                                pattern=[[1, 8], [0, 128]],
                                                compare_op=ALU.is_equal, fill=0.0,
                                                base=0, channel_multiplier=-1)],
           writes=[g("oneh8")], mark="gp")
    B.step("gpsimd", [lambda e: e.iota(csc[:, 0:64], pattern=[[0, 64]], base=0,
                                       channel_multiplier=1,
                                       allow_small_or_imprecise_dtypes=True)],
           writes=[g("csc")], mark="gp")
    B.step("vector", [
        lambda e: e.tensor_copy(csi_t[:, 0:64], csc[:, 0:64]),
        lambda e: e.tensor_single_scalar(csi_t[:, 64:128], csi_t[:, 0:64],
                                         cvi1[:, 0:1], op=ALU.bitwise_and),
        lambda e: e.tensor_copy(csr[:, 0:64], csi_t[:, 64:128]),
        lambda e: e.scalar_tensor_tensor(csn[:, 0:64], csr[:, 0:64], -1.0,
                                         csc[:, 0:64], op0=ALU.mult, op1=ALU.add),
        lambda e: e.tensor_scalar(csn[:, 0:64], csn[:, 0:64], 0.5, None,
                                  op0=ALU.mult),
    ], reads=[g("csc")], writes=[g("csn"), g("csi"), g("csr")], mark="ve")
    B.step("scalar", [
        lambda e: e.activation(thcol[:, 0:1], csn[:, 0:1], AF.Exp,
                               scale=cvals[:, CV_THS:CV_THS + 1]),
        lambda e: e.activation(thcol[:, 1:2], csn[:, 0:1], AF.Exp,
                               bias=cvals[:, CV_THB:CV_THB + 1],
                               scale=cvals[:, CV_THS:CV_THS + 1]),
        lambda e: e.activation(csS[:, 0:1], csn[:, 0:1], AF.Exp,
                               scale=cvals[:, CV_THS2:CV_THS2 + 1]),
        lambda e: e.activation(csS[:, 1:2], csn[:, 0:1], AF.Exp,
                               bias=cvals[:, CV_THB2:CV_THB2 + 1],
                               scale=cvals[:, CV_THS2:CV_THS2 + 1]),
    ], reads=[g("csn"), g("cvals")], writes=[g("thcol"), g("csS")], mark="se")
    B.step("vector", [
        lambda e: e.tensor_tensor(thcol[:, 0:1], thcol[:, 0:1], csS[:, 0:1],
                                  op=ALU.mult),
        lambda e: e.tensor_tensor(thcol[:, 1:2], thcol[:, 1:2], csS[:, 1:2],
                                  op=ALU.mult),
    ], reads=[g("csS")], writes=[g("thcol")], mark="ve")
    B.step("vector", [
        lambda e: e.tensor_scalar(altsign[:], csr[:, 0:1], 2.0, -1.0,
                                  op0=ALU.mult, op1=ALU.add),
    ], reads=[g("csr")], writes=[g("altsign")], mark="ve")
    B.step("gpsimd", [lambda e: e.iota(kdec[:], pattern=[[0, 8], [-128, 4]], base=T - 1,
                                       channel_multiplier=-1,
                                       allow_small_or_imprecise_dtypes=True),
                      lambda e: e.iota(kdecP[:], pattern=[[0, 8], [-128, 8]], base=TP - 1,
                                       channel_multiplier=-1,
                                       allow_small_or_imprecise_dtypes=True),
                      lambda e: e.iota(csc[:, 0:512], pattern=[[1, 512]], base=1,
                                       channel_multiplier=0,
                                       allow_small_or_imprecise_dtypes=True)],
           reads=[g("csc")], writes=[g("kdecr"), g("csc")], mark="gp")
    B.step("scalar",
           [(lambda h: (lambda e: e.activation(csS[:, 4 * h:4 * h + 4],
                                               kdec[:, 4 * h:4 * h + 4], AF.Exp,
                                               scale=cvals[:, CV_LNG2 + h:CV_LNG2 + h + 1])))(h)
            for h in range(NH)] +
           [(lambda h: (lambda e: e.activation(csC[:, 8 * h:8 * h + 8],
                                               kdecP[:, 8 * h:8 * h + 8], AF.Exp,
                                               scale=cvals[:, CV_LNG2 + h:CV_LNG2 + h + 1])))(h)
            for h in range(NH)] +
           [(lambda h: (lambda e: e.activation(kdec[:, 4 * h:4 * h + 4],
                                               kdec[:, 4 * h:4 * h + 4], AF.Exp,
                                               scale=cvals[:, CV_LNG + h:CV_LNG + h + 1])))(h)
            for h in range(NH)] +
           [(lambda h: (lambda e: e.activation(kdecP[:, 8 * h:8 * h + 8],
                                               kdecP[:, 8 * h:8 * h + 8], AF.Exp,
                                               scale=cvals[:, CV_LNG + h:CV_LNG + h + 1])))(h)
            for h in range(NH)],
           reads=[g("kdecr"), g("csc"), g("cvals")],
           writes=[g("kdec"), g("csS"), g("csC")], mark="se")
    B.step("vector", [
        lambda e: e.tensor_tensor(kdec[:], kdec[:], csS[:, 0:32], op=ALU.mult),
        lambda e: e.tensor_tensor(kdecP[:], kdecP[:], csC[:, 0:64], op=ALU.mult),
    ], reads=[g("csS"), g("csC"), g("kdec")], writes=[g("kdec")], mark="ve")
    gqb, _ = af32(512)
    B.step("vector", [lambda e: e.tensor_copy(gqb[:, 0:512], csc[:, 0:512])],
           reads=[g("csc")], writes=[g("gqb")], mark="ve")
    dma("sync", "ds", gnwc[:], d_gnw, writes=[g("gnw")])
    dma("sync", "ds", gnbc[:], d_gnb, writes=[g("gnb")])
    dma("sync", "ds", posb[:], d_posb, writes=[g("posb")])
    dma("sync", "ds", bgc[:], d_bgc, writes=[g("bgc")])
    dma("gpsimd", "dg", pfl[:], d_pflag, writes=[g("pfl")])
    for n, blk in [("q", AUG["wq"]), ("k", AUG["wk"]), ("v", AUG["wv"])]:
        dma("gpsimd", "dg", aug_all[1:2, blk:blk + HID], d_b[n], writes=[g("augw" + n)])
    dma("gpsimd", "dg", aug_all[0:1, AUG["wo"]:AUG["wo"] + HID], d_b["o"],
        writes=[g("augwo")])
    for n in WN:
        for eh in range(8):
            dma("gpsimd", "dg", d_wbf[n][eh].rearrange("dc p m -> (dc p) m"),
                d_w[n][:, eh * 256:(eh + 1) * 256],
                writes=[g("scr_" + n)])
    dma("gpsimd", "dg", d_wgTbf.rearrange("dc p m -> (dc p) m"), d_wgT,
        writes=[g("scr_wgT")])
    sqt, _ = abf(T)
    e0, _ = abf(T)
    e1, _ = abf(T)
    e2, _ = abf(T)
    e3, _ = abf(T)
    gqs, gqs_h = abf(T)
    mtmp, mtmp_h = abf(T)
    for h in range(NH):
        for ss in range(4):
            B.step("gpsimd", [(lambda sv: (lambda e: e.iota(
                csc[:, 0:T], pattern=[[1, T]], base=-128 * sv,
                channel_multiplier=-1, allow_small_or_imprecise_dtypes=True)))(ss)],
                reads=[g("csc")], writes=[g("csc")], mark="gp")
            B.step("scalar", [
                (lambda hv: (lambda e: e.activation(
                    csn[:, 0:T], csc[:, 0:T], AF.Exp,
                    scale=cvals[:, CV_LNG + hv:CV_LNG + hv + 1])))(h),
                (lambda hv: (lambda e: e.activation(
                    csS[:, 0:T], csc[:, 0:T], AF.Exp,
                    scale=cvals[:, CV_LNG2 + hv:CV_LNG2 + hv + 1])))(h),
            ], reads=[g("csc"), g("cvals")], writes=[g("csn"), g("csS")],
                mark="se")
            B.step("vector", [lambda e: e.tensor_tensor(
                csn[:, 0:T], csn[:, 0:T], csS[:, 0:T], op=ALU.mult)],
                reads=[g("csS")], writes=[g("csn")], mark="ve")
            B.step("gpsimd", [(lambda sv: (lambda e: e.affine_select(
                mtmp, csn[:, 0:T], pattern=[[1, T]],
                compare_op=ALU.is_ge, fill=0.0, base=-128 * sv,
                channel_multiplier=-1)))(ss)],
                reads=[g("csn")], writes=[g("mtmp")], mark="gp")
            dma("sync", "ds", d_mask[h, ss], mtmp, reads=[g("mtmp")])


    def stats_mms(xb, xbbuf, W, bs, bq):
        ntg = W // 512
        for dc in range(16):
            sq_aps = [sqt[:, 0:512], e0[:, 0:512]]
            B.step("scalar", [(lambda d, t: (lambda e: e.activation(
                sq_aps[t], xb[:, d * W + t * 512:d * W + (t + 1) * 512],
                AF.Square)))(dc, tg) for tg in range(ntg)],
                reads=[xbbuf], writes=[g("sqt"), g("e0")], mark="se")
            ems = []
            for tg in range(ntg):
                ems.append((lambda d, t: (lambda e: e.matmul(
                    bk(bs + t, parts=1), ones_colbf[:],
                    xb[:, d * W + t * 512:d * W + (t + 1) * 512],
                    start=(d == 0), stop=(d == 15))))(dc, tg))
                ems.append((lambda d, t, sa=sq_aps[tg]: (lambda e: e.matmul(
                    bk(bq + t, parts=1), ones_colbf[:], sa,
                    start=(d == 0), stop=(d == 15))))(dc, tg))
            B.step("tensor", ems, reads=[xbbuf, g("sqt"), g("e0"), g("const1")],
                   writes=[pb[bs + t] for t in range(ntg)] +
                          [pb[bq + t] for t in range(ntg)], mark="te")

    def strip_chain(W, bs, bq, flagged=False, want_aug=True, div=HID,
                    r2_mul=False, r2_save=False, want_rcols=False):
        for tg in range(W // 512):
            c0 = tg * 512
            B.step("scalar", [
                (lambda t: (lambda e: e.copy(s_sum[0:1, :], bk(bs + t, parts=1))))(tg),
                (lambda t: (lambda e: e.copy(s_sq[0:1, :], bk(bq + t, parts=1))))(tg),
            ], reads=[pb[bs + tg], pb[bq + tg]], writes=[g("strip")], mark="se")
            B.step("vector", [lambda e: e.tensor_scalar(
                s_mean[0:1, :], s_sum[0:1, :], 1.0 / div, None, op0=ALU.mult)],
                reads=[g("strip")], writes=[g("strip")], mark="ve")
            B.step("scalar", [lambda e: e.activation(s_tmp[0:1, :], s_mean[0:1, :],
                                                     AF.Square)],
                   reads=[g("strip")], writes=[g("strip")], mark="se")
            ems = [lambda e: e.scalar_tensor_tensor(
                s_sq[0:1, :], s_sq[0:1, :], 1.0 / div, s_tmp[0:1, :],
                op0=ALU.mult, op1=ALU.subtract)]
            if r2_mul:
                ems.append((lambda c: (lambda e: e.tensor_tensor(
                    s_t3[0:1, :], r2row[0:1, c:c + 512], r2row[0:1, c:c + 512],
                    op=ALU.mult)))(c0))
                ems.append(lambda e: e.tensor_tensor(
                    s_sq[0:1, :], s_sq[0:1, :], s_t3[0:1, :], op=ALU.mult))
            B.step("vector", ems, reads=[g("strip"), g("r2row")],
                   writes=[g("strip")], mark="ve")
            B.step("scalar", [lambda e: e.activation(
                s_tmp[0:1, :], s_sq[0:1, :], AF.Sqrt,
                bias=cvals[0:1, CV_EPS:CV_EPS + 1])],
                reads=[g("strip"), g("cvals")], writes=[g("strip")], mark="se")
            B.step("vector", [lambda e: e.reciprocal(s_sum[0:1, :],
                                                     s_tmp[0:1, :])],
                   reads=[g("strip")], writes=[g("strip")], mark="ve")
            B.step("scalar", [lambda e: e.activation(s_r[0:1, :], s_sum[0:1, :],
                                                     AF.Identity, bias=0.0)],
                   reads=[g("strip")], writes=[g("strip")], mark="se")
            ems = []
            if want_aug:
                ems.append((lambda c: (lambda e: e.tensor_scalar(
                    mrows[0:1, c:c + 512], s_mean[0:1, :], -1.0, None,
                    op0=ALU.mult)))(c0))
                if flagged:
                    ems.append((lambda c: (lambda e: e.tensor_tensor(
                        s_t3[0:1, :], s_tmp[0:1, :], pfl[0:1, c:c + 512],
                        op=ALU.mult)))(c0))
                    ems.append((lambda c: (lambda e: e.tensor_tensor(
                        mrows[0:1, c:c + 512], mrows[0:1, c:c + 512],
                        pfl[0:1, c:c + 512], op=ALU.mult)))(c0))
                else:
                    ems.append(lambda e: e.tensor_copy(s_t3[0:1, :], s_tmp[0:1, :]))
            ems.append((lambda c: (lambda e: e.tensor_copy(
                bcm[0:1, c:c + 512], s_r[0:1, :])))(c0))
            ems.append((lambda c: (lambda e: e.tensor_scalar(
                bcm2[0:1, c:c + 512], s_r[0:1, :], 1.0 / 16.0, None,
                op0=ALU.mult)))(c0))
            if r2_mul:
                ems.append((lambda c: (lambda e: e.tensor_tensor(
                    s_r[0:1, :], s_r[0:1, :], r2row[0:1, c:c + 512],
                    op=ALU.mult)))(c0))
                ems.append((lambda c: (lambda e: e.scalar_tensor_tensor(
                    s_tmp[0:1, :], s_mean[0:1, :], -1.0, s_r[0:1, :],
                    op0=ALU.mult, op1=ALU.mult)))(c0))
                ems.append((lambda c: (lambda e: e.tensor_copy(
                    bcm[0:1, c:c + 512], s_r[0:1, :])))(c0))
                ems.append((lambda c: (lambda e: e.tensor_copy(
                    bcm2[0:1, c:c + 512], s_tmp[0:1, :])))(c0))
            B.step("vector", ems,
                   reads=[g("strip"), g("pfl"), g("r2sq"), g("r2row")],
                   writes=[g("strip"), g("mrows"), g("bcm"), g("bcm2")], mark="ve")
            if want_aug:
                dma("gpsimd", "dg", mrows[1:2, c0:c0 + 512], s_t3[0:1, :],
                    reads=[g("strip")], writes=[g("mrows")])
            if r2_save:
                B.step("vector", [(lambda c: (lambda e: e.tensor_copy(
                    r2row[0:1, c:c + 512], s_r[0:1, :])))(c0)],
                    reads=[g("strip")], writes=[g("r2row")], mark="ve")
            if want_rcols:
                for ts in range(4):
                    b = 4 + (ts % 3)
                    B.step("tensor", [(lambda t, bb: (lambda e: e.transpose(
                        bk(bb, width=1), s_r[0:1, t * 128:(t + 1) * 128],
                        identF[0:1, 0:1])))(ts, b)],
                        reads=[g("strip"), g("ident")], writes=[pb[b]], mark="te")
                    B.step("scalar", [(lambda t, bb, d0=tg * 4: (lambda e: e.copy(
                        rcols[:, d0 + t:d0 + t + 1], bk(bb, width=1))))(ts, b)],
                        reads=[pb[b]], writes=[g("rcols")], mark="se")

    def bcast(row_ap, row_buf, W, bankidx, out_ap, out_buf, parts_k=1, st_ap=None,
              st_buf=None):
        stat = st_ap if st_ap is not None else ones_bc[:]
        ems = []
        for tg in range(W // 512):
            ems.append((lambda t, sa=stat: (lambda e: e.matmul(
                bk(bankidx + t), sa, row_ap[0:parts_k, t * 512:(t + 1) * 512],
                start=True, stop=True)))(tg))
        B.step("tensor", ems,
               reads=[row_buf, g("const1")] + ([st_buf] if st_buf else []),
               writes=[pb[bankidx + t] for t in range(W // 512)], mark="te")
        B.step("scalar", [(lambda t: (lambda e: e.copy(
            out_ap[:, t * 512:(t + 1) * 512], bk(bankidx + t))))(tg)
            for tg in range(W // 512)],
            reads=[pb[bankidx + t] for t in range(W // 512)],
            writes=[out_buf], mark="se")

    def cs_tiles(dst, dstbuf, W, iota_base, use_posb, folds):
        for hf in range(2):
            for sub in range(W // 512):
                o = sub * 512
                B.step("gpsimd", [(lambda ib: (lambda e: e.iota(
                    csc[:, 0:512], pattern=[[1, 512]], base=ib, channel_multiplier=0,
                    allow_small_or_imprecise_dtypes=True)))(iota_base + o)],
                    reads=[g("csc")], writes=[g("csc")], mark="gp")
                ems = []
                if use_posb:
                    ems.append(lambda e: e.tensor_scalar(csc[:, 0:512], csc[:, 0:512],
                                                         posb[:, 0:1], None,
                                                         op0=ALU.add))
                ems += [
                    (lambda h2: (lambda e: e.tensor_scalar(
                        csc[:, 0:512], csc[:, 0:512], thcol[:, h2:h2 + 1], None,
                        op0=ALU.mult)))(hf),
                    lambda e: e.tensor_scalar(csn[:, 0:512], csc[:, 0:512], INV_2PI,
                                              None, op0=ALU.mult),
                    lambda e: e.tensor_copy(csi_t[:, 0:512], csn[:, 0:512]),
                    lambda e: e.tensor_copy(csn[:, 0:512], csi_t[:, 0:512]),
                    lambda e: e.scalar_tensor_tensor(csr[:, 0:512], csn[:, 0:512], -C1,
                                                     csc[:, 0:512], op0=ALU.mult,
                                                     op1=ALU.add),
                    lambda e: e.scalar_tensor_tensor(csr[:, 0:512], csn[:, 0:512], -C2,
                                                     csr[:, 0:512], op0=ALU.mult,
                                                     op1=ALU.add),
                    lambda e: e.scalar_tensor_tensor(csr[:, 0:512], csn[:, 0:512], -C3,
                                                     csr[:, 0:512], op0=ALU.mult,
                                                     op1=ALU.add),
                    lambda e: e.tensor_single_scalar(csc[:, 0:512], csr[:, 0:512], PI,
                                                     op=ALU.is_gt),
                    lambda e: e.scalar_tensor_tensor(csr[:, 0:512], csc[:, 0:512], N2PI,
                                                     csr[:, 0:512], op0=ALU.mult,
                                                     op1=ALU.add),
                    lambda e: e.tensor_single_scalar(csc[:, 0:512], csr[:, 0:512], -PI,
                                                     op=ALU.is_lt),
                    lambda e: e.scalar_tensor_tensor(csr[:, 0:512], csc[:, 0:512], -N2PI,
                                                     csr[:, 0:512], op0=ALU.mult,
                                                     op1=ALU.add),
                ]
                B.step("vector", ems, reads=[g("csc"), g("posb"), g("thcol")],
                       writes=[g("csc"), g("csn"), g("csr"), g("csi")], mark="ve")
                B.step("scalar", [lambda e: e.activation(csS[:, 0:512], csr[:, 0:512],
                                                         AF.Sin)],
                       reads=[g("csr")], writes=[g("csS")], mark="se")
                B.step("vector", [
                    lambda e: e.tensor_scalar(csn[:, 0:512], csr[:, 0:512], HALF_PI,
                                              None, op0=ALU.add),
                    lambda e: e.tensor_single_scalar(csc[:, 0:512], csn[:, 0:512], PI,
                                                     op=ALU.is_gt),
                    lambda e: e.scalar_tensor_tensor(csn[:, 0:512], csc[:, 0:512], N2PI,
                                                     csn[:, 0:512], op0=ALU.mult,
                                                     op1=ALU.add),
                    lambda e: e.tensor_scalar(csS[:, 0:512], csS[:, 0:512],
                                              altsign[:, 0:1], None, op0=ALU.mult),
                ], reads=[g("csr"), g("altsign")],
                    writes=[g("csn"), g("csc"), g("csS")], mark="ve")
                B.step("scalar", [lambda e: e.activation(csC[:, 0:512], csn[:, 0:512],
                                                         AF.Sin)],
                       reads=[g("csn")], writes=[g("csC")], mark="se")
                ems = []
                for fi, (tile, tilebuf) in enumerate(folds):
                    bb = fi * 4 * W + hf * 2 * W
                    ems.append((lambda b2, ta, o2=o: (lambda e: e.tensor_tensor(
                        dst[:, b2 + o2:b2 + o2 + 512], csC[:, 0:512],
                        ta[:, o2:o2 + 512], op=ALU.mult)))(bb, tile))
                    ems.append((lambda b2, ta, o2=o: (lambda e: e.tensor_tensor(
                        dst[:, b2 + W + o2:b2 + W + o2 + 512], csS[:, 0:512],
                        ta[:, o2:o2 + 512], op=ALU.mult)))(bb, tile))
                B.step("vector", ems,
                       reads=[g("csC"), g("csS")] + [fb for _, fb in folds],
                       writes=[dstbuf], mark="ve")

    def stream_w8(n, eh, slot_ap, slot_buf):
        return dma("sync", "ds", slot_ap.rearrange("p (dc m) -> p dc m", m=256),
                   d_wbf[n][eh].rearrange("dc p m -> p dc m"),
                   reads=[g("scr_" + n)], writes=[slot_buf])

    def wsum(n, eh, slot_ap, slot_buf, bankidx):
        ems = []
        for dc in range(16):
            ems.append((lambda d: (lambda e: e.matmul(
                bk(bankidx, parts=1, width=256), ones_colbf[:],
                slot_ap[:, d * 256:(d + 1) * 256],
                start=(d == 0), stop=(d == 15))))(dc))
        B.step("tensor", ems, reads=[slot_buf, g("const1")], writes=[pb[bankidx]],
               mark="te")
        B.step("scalar", [lambda e: e.copy(augw[0:1, 0:256],
                                           bk(bankidx, parts=1, width=256))],
               reads=[pb[bankidx]], writes=[g("augw")], mark="se")
        B.step("vector", [(lambda e2, nn: (lambda e: e.tensor_copy(
            aug_all[0:1, AUG[nn] + e2 * 256:AUG[nn] + (e2 + 1) * 256],
            augw[0:1, 0:256])))(eh, n)],
            reads=[g("augw")], writes=[g("augw" + n[1:])], mark="ve")

    def shift_epi(src_bank, Cap, Sap, csbuf, dst_ap, dstbuf):
        B.step("scalar", [lambda e: e.copy(e0[:, 0:T], bk(src_bank))],
               reads=[pb[src_bank]], writes=[g("e0")], mark="se")
        B.step("vector", [
            lambda e: e.stream_shuffle(e1[:, 0:T], e0[:, 0:T], mask=SHUF_MASK),
            (lambda ca: lambda e: e.tensor_tensor(e2[:, 0:T], e0[:, 0:T], ca,
                                                  op=ALU.mult))(Cap),
        ], reads=[g("e0"), csbuf], writes=[g("e1"), g("e2")], mark="ve")
        B.step("vector", [
            (lambda sa: lambda e: e.tensor_tensor(e3[:, 0:T], e1[:, 0:T], sa,
                                                  op=ALU.mult))(Sap),
            (lambda da: lambda e: e.tensor_tensor(da, e2[:, 0:T], e3[:, 0:T],
                                                  op=ALU.add))(dst_ap),
        ], reads=[g("e1"), g("e2"), csbuf], writes=[g("e3"), dstbuf], mark="ve")

    w8s = [abf(4096) for _ in range(2)]
    xbfP, xbfP_h = abf(16 * TP)
    cskP, cskP_h = abf(4 * TP)
    rkbP, rkbP_h = abf(TP)
    ktP, ktP_h = abf(2 * TP)
    ktmP, ktmP_h = abf(8 * 256)
    vtmP, vtmP_h = abf(8 * 256)

    for pc in range(NPC if stop_after >= 1 else 0):
        cols = slice(pc * TP, (pc + 1) * TP)
        for q4 in range(4):
            dma("gpsimd", "dg",
                xbfP.rearrange("p (dc w) -> p dc w", w=TP)[:, q4 * 4:(q4 + 1) * 4],
                d_xp[:, cols].rearrange("(dc p) w -> p dc w", p=128)[:, q4 * 4:(q4 + 1) * 4],
                reads=[g("xbfP")], writes=[g("xbfP")])
        if PH2_SUB < 1:
            continue
        stats_mms(xbfP, g("xbfP"), TP, 0, 2)
        strip_chain(TP, 0, 2, flagged=True, want_rcols=True)
        if PH2_SUB < 2:
            continue
        bcast(bcm, g("bcm"), TP, 4, rkbP, g("rkbP"))
        cs_tiles(cskP, g("cskP"), TP, pc * TP, False, [(rkbP, g("rkbP"))])
        if PH2_SUB < 3:
            continue
        for h in range(NH if PH2_SUB >= 3 else 0):
            slv = w8s[0]
            slk = w8s[1]
            stream_w8("wv", h, slv[0], g("w8_0"))
            if pc == 0:
                wsum("wv", h, slv[0], g("w8_0"), 6)
            for ts in range(8):
                b = ts % 4
                ems = []
                for dc in range(16):
                    ems.append((lambda d, t, oap=bk(b, width=256): (lambda e: e.matmul(
                        oap,
                        xbfP[:, d * TP + t * 128:d * TP + (t + 1) * 128],
                        slv[0][:, d * 256:(d + 1) * 256],
                        start=(d == 0), stop=False)))(dc, ts))
                ems.append((lambda t, hh, oap=bk(b, width=256): (lambda e: e.matmul(
                    oap, mrows[0:2, t * 128:(t + 1) * 128],
                    aug_all[0:2, AUG["wv"] + hh * 256:AUG["wv"] + (hh + 1) * 256],
                    start=False, stop=True)))(ts, h))
                B.step("tensor", ems,
                       reads=[g("xbfP"), g("w8_0"), g("mrows"), g("augwv")],
                       writes=[pb[b]], mark="te")
                B.step("scalar", [(lambda t, bb: (lambda e: e.activation(
                    vtmP[:, t * 256:(t + 1) * 256], bk(bb, width=256),
                    AF.Identity, bias=0.0, scale=rcols[:, t:t + 1])))(ts, b)],
                    reads=[pb[b], g("rcols")], writes=[g("vtmP")], mark="se")
            if PH2_SUB < 4:
                continue
            stream_w8("wk", h, slk[0], g("w8_1"))
            if pc == 0:
                wsum("wk", h, slk[0], g("w8_1"), 6)
            for m2 in range(2):
                bp = 3 if m2 == 0 else 5
                for tg in range(2):
                    ems = []
                    for dc in range(16):
                        ems.append((lambda d, t, m, oap=bk(bp + tg): (lambda e: e.matmul(
                            oap,
                            slk[0][:, d * 256 + m * 128:d * 256 + (m + 1) * 128],
                            xbfP[:, d * TP + t * 512:d * TP + (t + 1) * 512],
                            start=(d == 0), stop=False)))(dc, tg, m2))
                    ems.append((lambda t, m, hh, oap=bk(bp + tg): (lambda e: e.matmul(
                        oap,
                        aug_all[0:2, AUG["wk"] + hh * 256 + m * 128:
                                AUG["wk"] + hh * 256 + (m + 1) * 128],
                        mrows[0:2, t * 512:(t + 1) * 512],
                        start=False, stop=True)))(tg, m2, h))
                    B.step("tensor", ems,
                           reads=[g("xbfP"), g("w8_1"), g("mrows"), g("augwk")],
                           writes=[pb[bp + tg]], mark="te")
                    co = m2 * 2 * TP + tg * 512
                    shift_epi(bp + tg, cskP[:, co:co + 512],
                              cskP[:, co + TP:co + TP + 512], g("cskP"),
                              ktP[:, m2 * TP + tg * 512:m2 * TP + (tg + 1) * 512],
                              g("ktP"))
            if PH2_SUB < 5:
                continue
            for m2 in range(2):
                for ssb in range(8):
                    B.step("tensor", [(lambda m, s2: (lambda e: e.transpose(
                        psb[:, 0:128],
                        ktP[:, m * TP + s2 * 128:m * TP + (s2 + 1) * 128],
                        ident[:])))(m2, ssb)],
                        reads=[g("ktP"), g("ident")], writes=[g("psb")], mark="te")
                    B.step("scalar", [(lambda m, s2, hh: (lambda e: e.activation(
                        ktmP[:, s2 * 256 + m * 128:s2 * 256 + (m + 1) * 128],
                        psb[:, 0:128], AF.Identity, bias=0.0,
                        scale=kdecP[:, 8 * hh + s2:8 * hh + s2 + 1])))(m2, ssb, h)],
                        reads=[g("psb"), g("kdec")], writes=[g("ktmP")], mark="se")
            for dk2 in range(2):
                b = dk2
                ems = []
                for ssb in range(8):
                    ems.append((lambda s2, d2, oap=bk(b, width=256): (lambda e: e.matmul(
                        oap,
                        ktmP[:, s2 * 256 + d2 * 128:s2 * 256 + (d2 + 1) * 128],
                        vtmP[:, s2 * 256:(s2 + 1) * 256],
                        start=(s2 == 0), stop=(s2 == 7))))(ssb, dk2))
                B.step("tensor", ems, reads=[g("ktmP"), g("vtmP")],
                       writes=[pb[b]], mark="te")
                B.step("vector", [(lambda d2, bb, hh: (lambda e: e.scalar_tensor_tensor(
                    S_fp[:, hh * 512 + d2 * 256:hh * 512 + (d2 + 1) * 256],
                    S_fp[:, hh * 512 + d2 * 256:hh * 512 + (d2 + 1) * 256],
                    GTP[hh], bk(bb, width=256), op0=ALU.mult, op1=ALU.add)))(dk2, b, h)],
                    reads=[pb[b]], writes=[g("S_fp")], mark="ve")

    tap("ktmP", ktmP[:, 0:2048], [g("ktmP")])
    tap("vtmP", vtmP[:, 0:2048], [g("vtmP")])
    tap("ktP", ktP[:, 0:2048], [g("ktP")])
    barrier()
    for hh in (xbfP_h, cskP_h, rkbP_h, ktP_h, ktmP_h, vtmP_h):
        arb.freed(hh)
    tap("S_prior", S_fp[:], [g("S_fp")])

    xbf, xbf_h = abf(16 * T)
    csq, csq_h = abf(8 * T)
    rqb, rqb_h = abf(T)
    rkb, rkb_h = abf(T)
    r1bc, r1bc_h = af32(T)
    qt, qt_h = abf(2 * T)
    ktl, ktl_h = abf(2 * T)
    ktm, ktm_h = abf(4 * 256)
    vtm, vtm_h = abf(4 * 256)
    qhb, qhb_h = abf(2 * T)
    simb, simb_h = abf(4 * T)
    mskb, mskb_h = abf(4 * T)
    retb, retb_h = abf(16 * T)
    sgb, sgb_h = abf(16 * T)
    aob = sgb
    bct0, bct0_h = abf(T)
    bct1, bct1_h = abf(T)
    x32 = [af32(T) for _ in range(2)]
    yt32 = [af32(T) for _ in range(2)]

    def CSQ(m2, kind):
        return csq[:, (m2 * 2 + kind) * T:(m2 * 2 + kind + 1) * T]

    def CSK(m2, kind):
        return csq[:, 4 * T + (m2 * 2 + kind) * T:4 * T + (m2 * 2 + kind + 1) * T]

    n_ch = 0 if stop_after < 2 else (1 if stop_after == 2 else NCH)
    for c in range(n_ch):
        cols = slice(c * T, (c + 1) * T)
        for q4 in range(4):
            dma("gpsimd", "dg",
                xbf.rearrange("p (dc w) -> p dc w", w=T)[:, q4 * 4:(q4 + 1) * 4],
                d_x[:, cols].rearrange("(dc p) w -> p dc w", p=128)[:, q4 * 4:(q4 + 1) * 4],
                reads=[g("xbf")], writes=[g("xbf")])
        stats_mms(xbf, g("xbf"), T, 0, 1)
        strip_chain(T, 0, 1, want_rcols=True)
        bcast(bcm2, g("bcm2"), T, 4, rqb, g("rqb"))
        bcast(bcm, g("bcm"), T, 5, rkb, g("rkb"))
        B.step("tensor", [lambda e: e.matmul(bk(6), ones_bc[:], bcm[0:1, 0:T],
                                             start=True, stop=True)],
               reads=[g("bcm"), g("const1")], writes=[pb[6]], mark="te")
        B.step("scalar", [lambda e: e.copy(r1bc, bk(6))],
               reads=[pb[6]], writes=[g("r1bc")], mark="se")
        cs_tiles(csq, g("csq"), T, c * T, True, [(rqb, g("rqb")), (rkb, g("rkb"))])
        if c == 0:
            tap("csq", csq[:, 0:8 * T], [g("csq")])
            tap("thcol", thcol[:], [g("thcol")])
            tap("altsign", altsign[:], [g("altsign")])
            tap("csS", csS[:, 0:512], [g("csS")])
            tap("csC", csC[:, 0:512], [g("csC")])
            tap("rqb", rqb[:, 0:T], [g("rqb")])
            tap("rkb", rkb[:, 0:T], [g("rkb")])

        for h in range(NH):
            dma("sync", "ds",
                mskb[:, 0:4 * T].rearrange("p (s t) -> p s t", s=4),
                d_mask[h].rearrange("s p t -> p s t"),
                reads=[g("mskslot")], writes=[g("mskslot")])
            B.step("vector", [(lambda hh: (lambda e: e.tensor_copy(
                S_bf[:], S_fp[:, hh * 512:(hh + 1) * 512])))(h)],
                reads=[g("S_fp")], writes=[g("S_bf")], mark="ve")
            stream_w8("wv", h, w8s[0][0], g("w8_0"))
            for ts in range(4):
                b = ts
                ems = []
                for dc in range(16):
                    ems.append((lambda d, t, oap=bk(b, width=256): (lambda e: e.matmul(
                        oap,
                        xbf[:, d * T + t * 128:d * T + (t + 1) * 128],
                        w8s[0][0][:, d * 256:(d + 1) * 256],
                        start=(d == 0), stop=False)))(dc, ts))
                ems.append((lambda t, hh, oap=bk(b, width=256): (lambda e: e.matmul(
                    oap, mrows[0:2, t * 128:(t + 1) * 128],
                    aug_all[0:2, AUG["wv"] + hh * 256:AUG["wv"] + (hh + 1) * 256],
                    start=False, stop=True)))(ts, h))
                B.step("tensor", ems,
                       reads=[g("xbf"), g("w8_0"), g("mrows"), g("augwv")],
                       writes=[pb[b]], mark="te")
                B.step("scalar", [(lambda t, bb: (lambda e: e.activation(
                    vtm[:, t * 256:(t + 1) * 256], bk(bb, width=256),
                    AF.Identity, bias=0.0, scale=rcols[:, t:t + 1])))(ts, b)],
                    reads=[pb[b], g("rcols")], writes=[g("vtm")], mark="se")
            stream_w8("wq", h, w8s[1][0], g("w8_1"))
            if c == 0:
                wsum("wq", h, w8s[1][0], g("w8_1"), 4)
            for m2 in range(2):
                b = 3 + m2
                ems = []
                for dc in range(16):
                    ems.append((lambda d, m, oap=bk(b): (lambda e: e.matmul(
                        oap, w8s[1][0][:, d * 256 + m * 128:d * 256 + (m + 1) * 128],
                        xbf[:, d * T:(d + 1) * T],
                        start=(d == 0), stop=False)))(dc, m2))
                ems.append((lambda m, hh, oap=bk(b): (lambda e: e.matmul(
                    oap, aug_all[0:2, AUG["wq"] + hh * 256 + m * 128:
                                 AUG["wq"] + hh * 256 + (m + 1) * 128],
                    mrows[0:2, 0:T], start=False, stop=True)))(m2, h))
                B.step("tensor", ems,
                       reads=[g("xbf"), g("w8_1"), g("mrows"), g("augwq")],
                       writes=[pb[b]], mark="te")
                shift_epi(b, CSQ(m2, 0), CSQ(m2, 1), g("csq"),
                          qt[:, m2 * T:(m2 + 1) * T], g("qt"))
            stream_w8("wk", h, w8s[0][0], g("w8_0"))
            for m2 in range(2):
                b = 0 + m2
                ems = []
                for dc in range(16):
                    ems.append((lambda d, m, oap=bk(b): (lambda e: e.matmul(
                        oap, w8s[0][0][:, d * 256 + m * 128:d * 256 + (m + 1) * 128],
                        xbf[:, d * T:(d + 1) * T],
                        start=(d == 0), stop=False)))(dc, m2))
                ems.append((lambda m, hh, oap=bk(b): (lambda e: e.matmul(
                    oap, aug_all[0:2, AUG["wk"] + hh * 256 + m * 128:
                                 AUG["wk"] + hh * 256 + (m + 1) * 128],
                    mrows[0:2, 0:T], start=False, stop=True)))(m2, h))
                B.step("tensor", ems,
                       reads=[g("xbf"), g("w8_0"), g("mrows"), g("augwk")],
                       writes=[pb[b]], mark="te")
                shift_epi(b, CSK(m2, 0), CSK(m2, 1), g("csq"),
                          ktl[:, m2 * T:(m2 + 1) * T], g("ktl"))
            for m2 in range(2):
                for ssb in range(4):
                    B.step("tensor", [(lambda m, s2: (lambda e: e.transpose(
                        psb[:, 0:128],
                        ktl[:, m * T + s2 * 128:m * T + (s2 + 1) * 128],
                        ident[:])))(m2, ssb)],
                        reads=[g("ktl"), g("ident")], writes=[g("psb")], mark="te")
                    B.step("scalar", [(lambda m, s2, hh: (lambda e: e.activation(
                        ktm[:, s2 * 256 + m * 128:s2 * 256 + (m + 1) * 128],
                        psb[:, 0:128], AF.Identity, bias=0.0,
                        scale=kdec[:, 4 * hh + s2:4 * hh + s2 + 1])))(m2, ssb, h)],
                        reads=[g("psb"), g("kdec")], writes=[g("ktm")], mark="se")
            B.step("scalar", [(lambda hh: (lambda e: e.activation(
                gqs, gqb[:, 0:512], AF.Exp,
                scale=cvals[:, CV_LNG + hh:CV_LNG + hh + 1])))(h)],
                reads=[g("gqb"), g("cvals")], writes=[g("gqs")], mark="se")
            B.step("vector", [
                (lambda m: (lambda e: e.tensor_tensor(
                    qhb[:, m * T:(m + 1) * T], qt[:, m * T:(m + 1) * T],
                    gqs, op=ALU.mult)))(m2)
                for m2 in range(2)],
                reads=[g("qt"), g("gqs")], writes=[g("qhb")], mark="ve")
            for ssb in range(4):
                b = 3 + (ssb % 2)
                ems = []
                for dh2 in range(2):
                    ems.append((lambda s2, d2, oap=bk(b): (lambda e: e.matmul(
                        oap, ktl[:, d2 * T + s2 * 128:d2 * T + (s2 + 1) * 128],
                        qt[:, d2 * T:(d2 + 1) * T],
                        start=(d2 == 0), stop=(d2 == 1))))(ssb, dh2))
                B.step("tensor", ems, reads=[g("ktl"), g("qt")], writes=[pb[b]],
                       mark="te")
                B.step("vector", [(lambda s2, bb: (lambda e: e.tensor_tensor(
                    simb[:, s2 * T:(s2 + 1) * T], bk(bb),
                    mskb[:, s2 * T:(s2 + 1) * T],
                    op=ALU.mult)))(ssb, b)],
                    reads=[pb[b], g("mskslot")], writes=[g("simb")], mark="ve")
            for dvh in range(2):
                b = 0 if dvh == 0 else 2
                ems = []
                for ssb in range(4):
                    ems.append((lambda s2, dv, oap=bk(b): (lambda e: e.matmul(
                        oap, vtm[:, s2 * 256 + dv * 128:s2 * 256 + (dv + 1) * 128],
                        simb[:, s2 * T:(s2 + 1) * T],
                        start=(s2 == 0), stop=False)))(ssb, dvh))
                for dh2 in range(2):
                    ems.append((lambda d2, dv, oap=bk(b): (lambda e: e.matmul(
                        oap, S_bf[:, d2 * 256 + dv * 128:d2 * 256 + (dv + 1) * 128],
                        qhb[:, d2 * T:(d2 + 1) * T],
                        start=False, stop=(d2 == 1))))(dh2, dvh))
                B.step("tensor", ems, reads=[g("vtm"), g("simb"), g("S_bf"), g("qhb")],
                       writes=[pb[b]], mark="te")
            for dvh in range(2):
                b = 0 if dvh == 0 else 2
                B.step("scalar", [
                    (lambda bb, hh, dv: (lambda e: e.copy(
                        retb[:, (2 * hh + dv) * T:(2 * hh + dv + 1) * T], bk(bb))))(b, h, dvh),
                    (lambda bb: (lambda e: e.activation(
                        sqt[:, 0:T], bk(bb), AF.Square)))(b),
                ], reads=[pb[b]], writes=[g("retb"), g("sqt")], mark="se")
                B.step("tensor", [
                    (lambda hh, dv: (lambda e: e.matmul(
                        bk(5, parts=8), oneh8[:, hh * 8:hh * 8 + 8],
                        retb[:, (2 * hh + dv) * T:(2 * hh + dv + 1) * T],
                        start=(hh == 0 and dv == 0), stop=(hh == 7 and dv == 1))))(h, dvh),
                    (lambda hh, dv: (lambda e: e.matmul(
                        bk(6, parts=8), oneh8[:, hh * 8:hh * 8 + 8], sqt[:, 0:T],
                        start=(hh == 0 and dv == 0), stop=(hh == 7 and dv == 1))))(h, dvh),
                ], reads=[g("retb"), g("sqt"), g("oneh8")],
                    writes=[pb[5], pb[6]], mark="te")
            for dk2 in range(2):
                b = 1 if dk2 == 0 else 3
                ems = []
                for ssb in range(4):
                    ems.append((lambda s2, d2, oap=bk(b, width=256): (lambda e: e.matmul(
                        oap,
                        ktm[:, s2 * 256 + d2 * 128:s2 * 256 + (d2 + 1) * 128],
                        vtm[:, s2 * 256:(s2 + 1) * 256],
                        start=(s2 == 0), stop=(s2 == 3))))(ssb, dk2))
                B.step("tensor", ems, reads=[g("ktm"), g("vtm")], writes=[pb[b]],
                       mark="te")
                B.step("vector", [(lambda d2, bb, hh: (lambda e: e.scalar_tensor_tensor(
                    S_fp[:, hh * 512 + d2 * 256:hh * 512 + (d2 + 1) * 256],
                    S_fp[:, hh * 512 + d2 * 256:hh * 512 + (d2 + 1) * 256],
                    GT[hh], bk(bb, width=256), op0=ALU.mult, op1=ALU.add)))(dk2, b, h)],
                    reads=[pb[b]], writes=[g("S_fp")], mark="ve")
        if c == 0:
            tap("qt0", qt[:, 0:2 * T], [g("qt")])
            tap("vtm0", vtm[:, 0:1024], [g("vtm")])
            tap("retb", retb[:, 0:16 * T], [g("retb")])

        for eh in range(NH):
            stream_w8("wg", eh, w8s[0][0], g("w8_0"))
            if c == 0:
                wsum("wg", eh, w8s[0][0], g("w8_0"), 4)
            for m2 in range(2):
                m = 2 * eh + m2
                b = m2
                ems = []
                for dc in range(16):
                    ems.append((lambda d, mm2, oap=bk(b): (lambda e: e.matmul(
                        oap, w8s[0][0][:, d * 256 + mm2 * 128:d * 256 + (mm2 + 1) * 128],
                        xbf[:, d * T:(d + 1) * T],
                        start=(d == 0), stop=False)))(dc, m2))
                ems.append((lambda mm, oap=bk(b): (lambda e: e.matmul(
                    oap, aug_all[0:1, AUG["wg"] + mm * 128:AUG["wg"] + (mm + 1) * 128],
                    mrows[0:1, 0:T], start=False, stop=True)))(m))
                B.step("tensor", ems,
                       reads=[g("xbf"), g("w8_0"), g("mrows"), g("augwg")],
                       writes=[pb[b]], mark="te")
                B.step("vector", [(lambda bb: (lambda e: e.tensor_tensor(
                    csr[:, 0:T], bk(bb), r1bc, op=ALU.mult)))(b)],
                    reads=[pb[b], g("r1bc")], writes=[g("csr")], mark="ve")
                B.step("scalar", [(lambda mm: (lambda e: e.activation(
                    sgb[:, mm * T:(mm + 1) * T], csr[:, 0:T], AF.Silu,
                    bias=bgc[:, mm:mm + 1], scale=1.0)))(m)],
                    reads=[g("csr"), g("bgc")], writes=[g("sgb")], mark="se")

        GA = grows[0:8, 0:T]
        GB = grows[0:8, T:2 * T]
        GC = grows[0:8, 2 * T:3 * T]
        B.step("scalar", [
            lambda e: e.activation(GA, bk(5, parts=8), AF.Identity,
                                   bias=0.0, scale=cvals[0:8, CV_256:CV_256 + 1]),
            lambda e: e.activation(GB, bk(6, parts=8), AF.Identity,
                                   bias=0.0, scale=cvals[0:8, CV_256:CV_256 + 1]),
        ], reads=[pb[5], pb[6], g("cvals")], writes=[g("grows")], mark="se")
        B.step("scalar", [lambda e: e.activation(GC, GA, AF.Square)],
               reads=[g("grows")], writes=[g("grows")], mark="se")
        B.step("vector", [lambda e: e.tensor_tensor(GB, GB, GC, op=ALU.subtract)],
               reads=[g("grows")], writes=[g("grows")], mark="ve")
        B.step("scalar", [lambda e: e.activation(GC, GB, AF.Sqrt,
                                                 bias=cvals[0:8, CV_EPS:CV_EPS + 1])],
               reads=[g("grows"), g("cvals")], writes=[g("grows")], mark="se")
        B.step("vector", [
            lambda e: e.reciprocal(gtmp[0:8, 0:T], GC),
        ], reads=[g("grows"), g("gtmp")], writes=[g("gtmp")], mark="ve")
        B.step("scalar", [lambda e: e.activation(GB, gtmp[0:8, 0:T],
                                                 AF.Identity, bias=0.0)],
               reads=[g("gtmp")], writes=[g("grows")], mark="se")
        B.step("vector", [
            lambda e: e.scalar_tensor_tensor(GC, GA, -1.0, GB, op0=ALU.mult,
                                             op1=ALU.mult),
            lambda e: e.tensor_copy(gbf[0:8, 0:T], GB),
            lambda e: e.tensor_copy(gbf2[0:8, 0:T], GC),
        ], reads=[g("grows")], writes=[g("grows"), g("gbf")], mark="ve")
        for h in range(NH):
            B.step("tensor", [
                (lambda hh: (lambda e: e.matmul(
                    bk(4), oneh8T[0:8, hh * 128:hh * 128 + 128], gbf[0:8, 0:T],
                    start=True, stop=True)))(h),
                (lambda hh: (lambda e: e.matmul(
                    bk(5), oneh8T[0:8, hh * 128:hh * 128 + 128], gbf2[0:8, 0:T],
                    start=True, stop=True)))(h),
            ], reads=[g("gbf"), g("oneh8")], writes=[pb[4], pb[5]], mark="te")
            B.step("scalar", [lambda e: e.copy(bct0, bk(4)),
                              lambda e: e.copy(bct1, bk(5))],
                   reads=[pb[4], pb[5]], writes=[g("bct")], mark="se")
            for dvh in range(2):
                m = 2 * h + dvh
                B.step("vector", [
                    (lambda mm: (lambda e: e.tensor_tensor(
                        e2[:, 0:T], retb[:, mm * T:(mm + 1) * T], bct0,
                        op=ALU.mult)))(m),
                    (lambda mm: (lambda e: e.tensor_tensor(
                        e3[:, 0:T], e2[:, 0:T], bct1, op=ALU.add)))(m),
                    (lambda mm: (lambda e: e.tensor_scalar(
                        e2[:, 0:T], e3[:, 0:T], gnwc[:, mm:mm + 1],
                        gnbc[:, mm:mm + 1], op0=ALU.mult, op1=ALU.add)))(m),
                    (lambda mm: (lambda e: e.tensor_tensor(
                        retb[:, mm * T:(mm + 1) * T], e2[:, 0:T],
                        sgb[:, mm * T:(mm + 1) * T], op=ALU.mult)))(m),
                ], reads=[g("retb"), g("bct"), g("gnw"), g("gnb"), g("sgb")],
                    writes=[g("retb"), g("e2"), g("e3")], mark="ve")
        if c == 0:
            tap("gate", retb[:, 0:16 * T], [g("retb")])

        for eh in range(NH):
            stream_w8("wo", eh, w8s[1][0], g("w8_1"))
            for m2 in range(2):
                m = 2 * eh + m2
                b = m % 4
                ems = []
                for dc in range(16):
                    ems.append((lambda d, mm2, oap=bk(b): (lambda e: e.matmul(
                        oap, w8s[1][0][:, d * 256 + mm2 * 128:d * 256 + (mm2 + 1) * 128],
                        retb[:, d * T:(d + 1) * T],
                        start=(d == 0), stop=False)))(dc, m2))
                ems.append((lambda mm, oap=bk(b): (lambda e: e.matmul(
                    oap, aug_all[0:1, AUG["wo"] + mm * 128:AUG["wo"] + (mm + 1) * 128],
                    ones_bf[0:1, 0:T], start=False, stop=True)))(m))
                B.step("tensor", ems,
                       reads=[g("retb"), g("w8_1"), g("const1"), g("augwo")],
                       writes=[pb[b]], mark="te")
                B.step("scalar", [
                    (lambda bb, mm: (lambda e: e.copy(
                        aob[:, mm * T:(mm + 1) * T], bk(bb))))(b, m),
                    (lambda bb: (lambda e: e.activation(
                        sqt[:, 0:T], bk(bb), AF.Square)))(b),
                ], reads=[pb[b]], writes=[g("sgb"), g("sqt")], mark="se")
                B.step("tensor", [
                    (lambda mm: (lambda e: e.matmul(
                        bk(5, parts=1), ones_colbf[:], aob[:, mm * T:(mm + 1) * T],
                        start=(mm == 0), stop=(mm == 15))))(m),
                    (lambda mm: (lambda e: e.matmul(
                        bk(6, parts=1), ones_colbf[:], sqt[:, 0:T],
                        start=(mm == 0), stop=(mm == 15))))(m),
                ], reads=[g("sgb"), g("sqt"), g("const1")], writes=[pb[5], pb[6]],
                    mark="te")
        strip_chain(T, 5, 6, want_aug=False)
        B.step("vector", [
            lambda e: e.scalar_tensor_tensor(s_tmp[0:1, :], s_mean[0:1, :], -1.0,
                                             s_r[0:1, :], op0=ALU.mult, op1=ALU.mult),
            lambda e: e.tensor_copy(bcm2[0:1, 0:T], s_tmp[0:1, :]),
        ], reads=[g("strip")], writes=[g("strip"), g("bcm2")], mark="ve")
        bcast(bcm, g("bcm"), T, 4, bct0, g("bct"))
        bcast(bcm2, g("bcm2"), T, 5, bct1, g("bct"))
        for dc in range(16):
            sl = dc % 2
            dma("sync", "ds", x32[sl][0],
                d_x[dc * 128:(dc + 1) * 128, cols],
                reads=[g(f"x32_{sl}")], writes=[g(f"x32_{sl}")])
            B.step("vector", [
                (lambda d: (lambda e: e.tensor_tensor(
                    e2[:, 0:T], aob[:, d * T:(d + 1) * T], bct0, op=ALU.mult)))(dc),
                lambda e: e.tensor_tensor(e3[:, 0:T], e2[:, 0:T], bct1, op=ALU.add),
                (lambda s: (lambda e: e.tensor_tensor(
                    yt32[s][0], e3[:, 0:T], x32[s][0], op=ALU.add)))(sl),
            ], reads=[g("sgb"), g("bct"), g(f"x32_{sl}"), g(f"yt32_{sl}")],
                writes=[g("e2"), g("e3"), g(f"yt32_{sl}")], mark="ve")
            dma("sync", "ds", d_y[dc, :, cols], yt32[sl][0],
                reads=[g(f"yt32_{sl}")])
        if c == 0:
            tap("y0", yt32[0][0], [g("yt32_0")])

    barrier()
    for hh in (xbf_h, csq_h, rqb_h, rkb_h, qt_h, ktl_h, ktm_h, vtm_h, qhb_h,
               simb_h, mskb_h, retb_h, sgb_h, bct0_h, bct1_h, gqs_h, mtmp_h):
        arb.freed(hh)
    arf.freed(r1bc_h)
    for a in x32 + yt32:
        arf.freed(a[1])

    ybf, ybf_h = abf(16 * TP)
    unb, unb_h = abf(16 * TP)
    ffns = [abf(TP) for _ in range(2)]
    ffnr = ffns
    rwbt, rwbt_h = abf(TP)
    pbt, pbt_h = abf(TP)
    bbt, bbt_h = abf(TP)
    fA, fA_h = abf(TP)
    fB, fB_h = abf(TP)
    wgt8, wgt8_h = abf(128)
    y32 = [af32(512) for _ in range(2)]
    o32 = [af32(512) for _ in range(2)]

    for hf4 in range(2 if stop_after >= 4 else 0):
        cols = slice(hf4 * TP, (hf4 + 1) * TP)
        for q4 in range(4):
            dma("gpsimd", "dg",
                ybf.rearrange("p (dc w) -> p dc w", w=TP)[:, q4 * 4:(q4 + 1) * 4],
                d_y[:, :, cols].rearrange("dc p w -> p dc w")[:, q4 * 4:(q4 + 1) * 4],
                reads=[g("ybf")], writes=[g("ybf")])
        stats_mms(ybf, g("ybf"), TP, 0, 2)
        strip_chain(TP, 0, 2, r2_save=True, want_rcols=True)
        dma("sync", "ds", wgt8.rearrange("p (dc m) -> p dc m", m=8),
            d_wgTbf.rearrange("dc p m -> p dc m"),
            reads=[g("scr_wgT")], writes=[g("wgt8")])
        if hf4 == 0:
            ems = []
            for dc in range(16):
                ems.append((lambda d: (lambda e: e.matmul(
                    bk(2, parts=1, width=8), ones_colbf[:], wgt8[:, d * 8:(d + 1) * 8],
                    start=(d == 0), stop=(d == 15))))(dc))
            B.step("tensor", ems, reads=[g("wgt8"), g("const1")], writes=[pb[2]],
                   mark="te")
            B.step("scalar", [lambda e: e.copy(augw[0:1, 0:8],
                                               bk(2, parts=1, width=8))],
                   reads=[pb[2]], writes=[g("augw")], mark="se")
            B.step("vector", [lambda e: e.tensor_copy(aug8[0:1, 0:64],
                                                      augw[0:1, 0:64])],
                   reads=[g("augw")], writes=[g("aug8")], mark="ve")
        for tg in range(2):
            b = 2 + tg
            ems = []
            for dc in range(16):
                ems.append((lambda d, t, oap=bk(b, parts=8): (lambda e: e.matmul(
                    oap, wgt8[:, d * 8:(d + 1) * 8],
                    ybf[:, d * TP + t * 512:d * TP + (t + 1) * 512],
                    start=(d == 0), stop=False)))(dc, tg))
            ems.append((lambda t, oap=bk(b, parts=8): (lambda e: e.matmul(
                oap, aug8[0:1, 0:8], mrows[0:1, t * 512:(t + 1) * 512],
                start=False, stop=True)))(tg))
            B.step("tensor", ems, reads=[g("ybf"), g("wgt8"), g("aug8"), g("mrows")],
                   writes=[pb[b]], mark="te")
            B.step("scalar", [(lambda t, bb: (lambda e: e.copy(
                lgsb[0:8, t * 512:(t + 1) * 512], bk(bb, parts=8))))(tg, b)],
                reads=[pb[b]], writes=[g("lgsb")], mark="se")
        for tb in range(8):
            B.step("tensor", [(lambda t: (lambda e: e.transpose(
                psb[:, 0:8], lgsb[0:8, t * 128:(t + 1) * 128],
                ident[0:8, 0:8])))(tb)],
                reads=[g("lgsb"), g("ident")], writes=[g("psb")], mark="te")
            B.step("scalar", [(lambda t: (lambda e: e.copy(
                lgt[:, t * 8:(t + 1) * 8], psb[:, 0:8])))(tb)],
                reads=[g("psb")], writes=[g("lgt")], mark="se")
        if tb == 7:
            pass
        B.step("vector", [lambda e: e.tensor_copy(lgtf[:], lgt[:])],
               reads=[g("lgt")], writes=[g("lgtf")], mark="ve")
        for tb in range(8):
            B.step("vector", [
                (lambda t: (lambda e: e.max(out=m8t[:], in_=lgtf[:, t * 8:(t + 1) * 8])))(tb),
                lambda e: e.tensor_tensor(m8t[:, 2:3], m8t[:, 1:2], m8t[:, 0:1],
                                          op=ALU.subtract),
            ], reads=[g("lgtf")], writes=[g("m8t")], mark="ve")
            B.step("scalar", [
                (lambda t: (lambda e: e.activation(
                    m8t[:, 3:4], m8t[:, 2:3], AF.Exp, scale=rcols[:, t:t + 1])))(tb),
                lambda e: e.activation(m8t[:, 4:5], m8t[:, 3:4], AF.Identity,
                                       bias=cvals[:, CV_ONE:CV_ONE + 1]),
            ], reads=[g("m8t"), g("rcols"), g("cvals")], writes=[g("m8t")],
                mark="se")
            B.step("vector", [
                lambda e: e.reciprocal(m8t[:, 5:6], m8t[:, 4:5]),
            ], reads=[g("m8t")], writes=[g("m8t")], mark="ve")
            B.step("scalar", [(lambda t: (lambda e: e.activation(
                rwc[:, t:t + 1], m8t[:, 4:5], AF.Identity, bias=0.0,
                scale=m8t[:, 5:6])))(tb)],
                reads=[g("m8t")], writes=[g("rwc")], mark="se")
        B.step("tensor", [lambda e: e.transpose(bk(2, parts=8, width=128),
                                                rwc[:, 0:8], identF[:])],
               reads=[g("rwc"), g("ident")], writes=[pb[2]], mark="te")
        B.step("scalar", [lambda e: e.copy(rw8[0:8, 0:128],
                                           bk(2, parts=8, width=128))],
               reads=[pb[2]], writes=[g("rw8")], mark="se")
        B.step("vector", [lambda e: e.tensor_copy(lgsb[0:8, 0:128], rw8[0:8, 0:128])],
               reads=[g("rw8")], writes=[g("lgsb")], mark="ve")
        for tb in range(8):
            B.step("tensor", [(lambda t: (lambda e: e.matmul(
                bk(2, width=128), oneh8T[0:8, t * 128:t * 128 + 128],
                lgsb[0:8, 0:128], start=True, stop=True)))(tb)],
                reads=[g("lgsb"), g("oneh8")], writes=[pb[2]], mark="te")
            B.step("scalar", [(lambda t: (lambda e: e.copy(
                rwbt[:, t * 128:(t + 1) * 128], bk(2, width=128))))(tb)],
                reads=[pb[2]], writes=[g("rwbt")], mark="se")
        for eh in range(NH):
            stream_w8("wupT", eh, w8s[0][0], g("w8_0"))
            if hf4 == 0:
                wsum("wupT", eh, w8s[0][0], g("w8_0"), 2)
            for m2 in range(2):
                m = 2 * eh + m2
                bb0 = 3
                ems = []
                for dc in range(16):
                    for tg in range(2):
                        ems.append((lambda d, t, mm2: (lambda e: e.matmul(
                            bk(bb0 + t),
                            w8s[0][0][:, d * 256 + mm2 * 128:d * 256 + (mm2 + 1) * 128],
                            ybf[:, d * TP + t * 512:d * TP + (t + 1) * 512],
                            start=(d == 0), stop=False)))(dc, tg, m2))
                for tg in range(2):
                    ems.append((lambda t, mm: (lambda e: e.matmul(
                        bk(bb0 + t),
                        aug_all[0:1, AUG["wupT"] + mm * 128:AUG["wupT"] + (mm + 1) * 128],
                        mrows[0:1, t * 512:(t + 1) * 512],
                        start=False, stop=True)))(tg, m))
                B.step("tensor", ems,
                       reads=[g("ybf"), g("w8_0"), g("mrows"), g("augwupT")],
                       writes=[pb[bb0], pb[bb0 + 1]], mark="te")
                B.step("scalar", [(lambda t, mm, bb: (lambda e: e.activation(
                    unb[:, mm * TP + t * 512:mm * TP + (t + 1) * 512],
                    bk(bb + t), AF.Relu)))(tg, m, bb0) for tg in range(2)],
                    reads=[pb[bb0], pb[bb0 + 1]], writes=[g("unb")], mark="se")
        for dc in range(16):
            B.step("vector", [(lambda d: (lambda e: e.tensor_tensor(
                ybf[:, d * TP:(d + 1) * TP], ybf[:, d * TP:(d + 1) * TP],
                rwbt, op=ALU.mult)))(dc)],
                reads=[g("ybf"), g("rwbt")], writes=[g("ybf")], mark="ve")
        B.step("vector", [lambda e: e.tensor_tensor(mrows[0:1, 0:TP], mrows[0:1, 0:TP],
                                                    rwbt[0:1, 0:TP], op=ALU.mult)],
               reads=[g("mrows"), g("rwbt")], writes=[g("mrows")], mark="ve")
        for eh in range(NH):
            stream_w8("wdnT", eh, w8s[0][0], g("w8_0"))
            stream_w8("we", eh, w8s[1][0], g("w8_1"))
            if hf4 == 0:
                wsum("we", eh, w8s[1][0], g("w8_1"), 2)
            for m2 in range(2):
                m = 2 * eh + m2
                bb0 = 3
                ems = []
                for oc in range(16):
                    for tg in range(2):
                        ems.append((lambda d, t, mm2: (lambda e: e.matmul(
                            bk(bb0 + t),
                            w8s[0][0][:, d * 256 + mm2 * 128:d * 256 + (mm2 + 1) * 128],
                            unb[:, d * TP + t * 512:d * TP + (t + 1) * 512],
                            start=(d == 0), stop=False)))(oc, tg, m2))
                for dc in range(16):
                    for tg in range(2):
                        ems.append((lambda d, t, mm2: (lambda e: e.matmul(
                            bk(bb0 + t),
                            w8s[1][0][:, d * 256 + mm2 * 128:d * 256 + (mm2 + 1) * 128],
                            ybf[:, d * TP + t * 512:d * TP + (t + 1) * 512],
                            start=False, stop=False)))(dc, tg, m2))
                for tg in range(2):
                    ems.append((lambda t, mm: (lambda e: e.matmul(
                        bk(bb0 + t),
                        aug_all[0:1, AUG["we"] + mm * 128:AUG["we"] + (mm + 1) * 128],
                        mrows[0:1, t * 512:(t + 1) * 512],
                        start=False, stop=True)))(tg, m))
                B.step("tensor", ems,
                       reads=[g("ybf"), g("unb"), g("w8_0"), g("w8_1"), g("mrows"),
                              g("augwe")],
                       writes=[pb[bb0], pb[bb0 + 1]], mark="te")
                fsl = m % 2
                B.step("scalar", [
                    (lambda t, bb, s: (lambda e: e.copy(
                        ffns[s][0][:, t * 512:(t + 1) * 512],
                        bk(bb + t))))(0, bb0, fsl),
                    (lambda bb: (lambda e: e.activation(
                        sqt[:, 0:T], bk(bb), AF.Square)))(bb0),
                    (lambda t, bb, s: (lambda e: e.copy(
                        ffns[s][0][:, t * 512:(t + 1) * 512],
                        bk(bb + t))))(1, bb0, fsl),
                    (lambda bb: (lambda e: e.activation(
                        e0[:, 0:T], bk(bb + 1), AF.Square)))(bb0),
                ], reads=[pb[bb0], pb[bb0 + 1]]
                    + [g(f"ffns{fsl}")],
                    writes=[g(f"ffns{fsl}"), g("sqt"), g("e0")], mark="se")
                dma("sync", "ds", d_ffn[m, :, cols], ffns[fsl][0],
                    reads=[g(f"ffns{fsl}")])
                B.step("tensor", [
                    (lambda mm, s: (lambda e: e.matmul(
                        bk(0, parts=1), ones_colbf[:], ffns[s][0][:, 0:512],
                        start=(mm == 0), stop=(mm == 15))))(m, fsl),
                    (lambda mm: (lambda e: e.matmul(
                        bk(5, parts=1), ones_colbf[:], sqt[:, 0:T],
                        start=(mm == 0), stop=(mm == 15))))(m),
                    (lambda mm, s: (lambda e: e.matmul(
                        bk(1, parts=1), ones_colbf[:], ffns[s][0][:, 512:1024],
                        start=(mm == 0), stop=(mm == 15))))(m, fsl),
                    (lambda mm: (lambda e: e.matmul(
                        bk(6, parts=1), ones_colbf[:], e0[:, 0:T],
                        start=(mm == 0), stop=(mm == 15))))(m),
                ], reads=[g(f"ffns{fsl}"), g("sqt"), g("e0"), g("const1")],
                    writes=[pb[0], pb[1], pb[5], pb[6]], mark="te")
        strip_chain(TP, 0, 5, want_aug=False, r2_mul=True)
        bcast(bcm, g("bcm"), TP, 2, pbt, g("pbt"))
        bcast(bcm2, g("bcm2"), TP, 2, bbt, g("bbt"))
        for dc in range(16):
            for tg in range(2):
                sl = (2 * dc + tg) % 2
                c2 = slice(hf4 * TP + tg * 512, hf4 * TP + (tg + 1) * 512)
                dma("sync", "ds", y32[sl][0], d_y[dc, :, c2],
                    reads=[g(f"y32_{sl}")], writes=[g(f"y32_{sl}")])
                dma("sync", "ds", ffnr[sl][0][:, 0:512], d_ffn[dc, :, c2],
                    reads=[g(f"ffns{sl}")], writes=[g(f"ffns{sl}")])
                B.step("vector", [
                    (lambda s, t: (lambda e: e.tensor_tensor(
                        fA[:, 0:512], ffnr[s][0][:, 0:512],
                        pbt[:, t * 512:(t + 1) * 512], op=ALU.mult)))(sl, tg),
                    (lambda t: (lambda e: e.tensor_tensor(
                        fB[:, 0:512], fA[:, 0:512],
                        bbt[:, t * 512:(t + 1) * 512], op=ALU.add)))(tg),
                    (lambda s: (lambda e: e.tensor_tensor(
                        o32[s][0], fB[:, 0:512], y32[s][0], op=ALU.add)))(sl),
                ], reads=[g(f"ffns{sl}"), g("pbt"), g("bbt"), g(f"y32_{sl}"),
                          g(f"o32_{sl}")],
                    writes=[g("fA"), g("fB"), g(f"o32_{sl}")], mark="ve")
                dma("sync", "ds", d_out[dc, :, c2], o32[sl][0],
                    reads=[g(f"o32_{sl}")])

    if stop_after >= 4:
        tap("ybf4", ybf[:, 0:2 * TP], [g("ybf")])
        tap("lgtf", lgtf[:], [g("lgtf")])
        tap("rwc", rwc[:, 0:8], [g("rwc")])
        tap("rwbt", rwbt[:], [g("rwbt")])
        tap("pbt", pbt[:], [g("pbt")])
        tap("bbt", bbt[:], [g("bbt")])
        tap("r2row", r2row[0:1, :], [g("r2row")])
        tap("unb4", unb[:, 0:2 * TP], [g("unb")])
        tap("mrows4", mrows[0:2, 0:TP], [g("mrows")])
        tap("o32l", o32[1][0][:], [g("o32_1")])

    B.step("sync", [lambda e: e.nop()], extra=[("ds", B.cnt.get("ds", 0), None)])

    globals()["LAST_B"] = B
    B.check()
    with nc.Block() as block:
        B.emit_all(block)
    ctx.close()
    return nc


_CACHE = {}


STOP_AFTER = 4
PH2SUB = 5


def _get_nc():
    key = ("nc", STOP_AFTER, PH2SUB)
    if key not in _CACHE:
        _CACHE[key] = build_program(STOP_AFTER, PH2SUB)
    return _CACHE[key]


def kernel(**inputs):
    x = np.ascontiguousarray(np.asarray(inputs["L0_attn_x"], dtype=np.float32))
    w = {k: np.ascontiguousarray(np.asarray(inputs[k], dtype=np.float32))
         for k in inputs}
    nc = _get_nc()

    wq = w["L0_attn_w_q"]; wk = w["L0_attn_w_k"]; wv = w["L0_attn_w_v"]
    wg = w["L0_attn_w_g"]; wo = w["L0_attn_w_o"]
    wupT = np.ascontiguousarray(w["L0_ffn_W_up"].T)
    wdnT = np.ascontiguousarray(w["L0_ffn_W_down"].T)
    wgT = np.ascontiguousarray(w["L0_ffn_W_gate"].T)
    we = w["L0_ffn_W_expert"]
    colify = lambda v: np.ascontiguousarray(v.reshape(16, 128).T)
    common = {
        "wq": wq, "wk": wk, "wv": wv, "wg": wg, "wo": wo,
        "wupT": wupT, "wdnT": wdnT, "we": we, "wgT": wgT,
        "bq": w["L0_attn_b_q"].reshape(1, HID),
        "bk": w["L0_attn_b_k"].reshape(1, HID),
        "bv": w["L0_attn_b_v"].reshape(1, HID),
        "bg": w["L0_attn_b_g"].reshape(1, HID),
        "bo": w["L0_attn_b_o"].reshape(1, HID),
        "gnw": colify(w["L0_attn_gn_weight"]),
        "gnb": colify(w["L0_attn_gn_bias"]),
        "bgcol": colify(w["L0_attn_b_g"]),
    }
    zeros_fm = np.zeros((HID, SPAN), dtype=np.float32)
    in_maps = []
    for core in range(8):
        b, hf = core // 2, core % 2
        xs = np.ascontiguousarray(x[b, hf * SPAN:(hf + 1) * SPAN].T)
        xp = (np.ascontiguousarray(x[b, 0:SPAN].T) if hf == 1 else zeros_fm)
        im = dict(common)
        im["x_fm"] = xs
        im["xp_fm"] = xp
        im["pos_base"] = np.full((128, 1), float(hf * SPAN), dtype=np.float32)
        im["pflag"] = np.full((1, TP), 1.0 if hf == 1 else 0.0, dtype=np.float32)
        in_maps.append(im)

    res = run_bass_kernel_spmd(nc, in_maps, core_ids=list(range(8)))
    out = np.empty((4, 4096, HID), dtype=np.float32)
    for core in range(8):
        b, hf = core // 2, core % 2
        o = np.asarray(res.results[core]["out_fm"])
        out[b, hf * SPAN:(hf + 1) * SPAN] = o.reshape(HID, SPAN).T
    return out


# version 07
# speedup vs baseline: 1.0105x; 1.0105x over previous
import math
from contextlib import ExitStack

import numpy as np
import ml_dtypes

import concourse.bass as bass
import concourse.mybir as mybir
from concourse.bass_utils import run_bass_kernel_spmd

F32 = mybir.dt.float32
BF16 = mybir.dt.bfloat16
I32 = mybir.dt.int32
AF = mybir.ActivationFunctionType
ALU = mybir.AluOpType

HID, NH = 2048, 8
T, NCH = 512, 4
TP, NPC = 1024, 2
SPAN = 2048
EPS = 1e-5
_2PI = 2.0 * math.pi
C1 = float(np.float32(_2PI))
C2 = float(np.float32(_2PI - np.float64(np.float32(_2PI))))
C3 = float(np.float32(_2PI - np.float64(np.float32(_2PI))
                      - np.float64(np.float32(_2PI - np.float64(np.float32(_2PI))))))
HALF_PI = float(np.float32(math.pi / 2))
PI = float(np.float32(math.pi))
N2PI = -float(np.float32(_2PI))
INV_2PI = float(np.float32(1.0 / _2PI))
LN1E4 = float(math.log(10000.0))
GAMMAS = [float(v) for v in (1.0 - np.exp(np.linspace(np.log(1 / 32), np.log(1 / 512), 8)))]
LNG = [float(np.float32(math.log(v))) for v in GAMMAS]
LNG_LO = [float(np.float64(math.log(v)) - np.float64(np.float32(math.log(v))))
          for v in GAMMAS]
THS_HI = float(np.float32(-LN1E4 / 127.0))
THS_LO = float(np.float64(-LN1E4 / 127.0) - np.float64(THS_HI))
THB_HI = float(np.float32(-LN1E4 * 64.0 / 127.0))
THB_LO = float(np.float64(-LN1E4 * 64.0 / 127.0) - np.float64(THB_HI))
GT = [float(np.float32(v) ** T) for v in GAMMAS]
GTP = [float(np.float32(v) ** TP) for v in GAMMAS]

ENGINES = ("sync", "gpsimd", "tensor", "vector", "scalar")
DEBUG_TAPS = []
SHUF_MASK = []
for _i in range(0, 32, 2):
    SHUF_MASK += [_i + 1, _i]


class Buf:
    __slots__ = ("name", "w", "rs")

    def __init__(self, name):
        self.name = name
        self.w = None
        self.rs = []


class Builder:
    def __init__(self):
        self.sched = []
        self.cnt = {}
        self.floor = {e: {} for e in ENGINES}
        self.sem_handles = {}
        self.bufs = {}
        self.last_dma = {}

    def buf(self, name):
        if name not in self.bufs:
            self.bufs[name] = Buf(name)
        return self.bufs[name]

    def step(self, eng, emits, reads=(), writes=(), mark=None, dma=False, extra=()):
        evs = [b.w for b in reads]
        for b in writes:
            evs.append(b.w)
            evs.extend(b.rs)
        evs.extend(extra)
        if dma and mark is not None and self.cnt.get(mark, 0) > 0:
            wkey = tuple(sorted(id(b) for b in writes))
            prev = self.last_dma.get(mark)
            if not (wkey and prev == (wkey, len(self.sched))):
                evs.append((mark, self.cnt[mark], None))
        if dma and mark is not None:
            self.last_dma[mark] = (tuple(sorted(id(b) for b in writes)),
                                   len(self.sched) + 1)
        ws = []
        fl = self.floor[eng]
        for ev in evs:
            if ev is None:
                continue
            s, v, peng = ev
            if v <= 0 or s is None:
                continue
            if fl.get(s, 0) >= v:
                continue
            fl[s] = v
            ws.append((s, v))
        incs = []
        ev = None
        if mark is not None:
            if dma:
                amt = 16 * len(emits)
                incs = [(mark, 16)] * len(emits)
            elif eng in ("vector", "scalar", "gpsimd"):
                amt = len(emits)
                incs = [(mark, 1)] * len(emits)
            else:
                amt = 1
                incs = [(mark, 1)]
            base = self.cnt.get(mark, 0)
            self.cnt[mark] = base + amt
            ev = (mark, self.cnt[mark], None if dma else eng)
        else:
            base = 0
        self.sched.append((eng, ws, emits, incs, dma, base))
        if ev is not None:
            for b in writes:
                b.w = ev
                b.rs = []
            for b in reads:
                b.rs.append(ev)
        else:
            for b in writes:
                if b.w is None:
                    b.w = (None, 0, eng)
        return ev

    def check(self):
        qs = {e: [] for e in ENGINES}
        for it in self.sched:
            qs[it[0]].append(it)
        idx = {e: 0 for e in ENGINES}
        sim = {}
        prog = True
        while prog:
            prog = False
            for e in ENGINES:
                while idx[e] < len(qs[e]):
                    _, ws, emits, incs, dma, _base = qs[e][idx[e]]
                    if all(sim.get(s, 0) >= v for s, v in ws):
                        for s, a in incs:
                            sim[s] = sim.get(s, 0) + a
                        idx[e] += 1
                        prog = True
                    else:
                        break
        stuck = {e: i for e, i in idx.items() if i < len(qs[e])}
        if stuck:
            lines = []
            for e, i in stuck.items():
                _, ws, _, _, _ = qs[e][i]
                lines.append(f"{e}[{i}/{len(qs[e])}] waits " +
                             ",".join(f"{s}>={v}(at {sim.get(s, 0)})" for s, v in ws))
            raise RuntimeError("schedule deadlock:\n" + "\n".join(lines))

    def emit_all(self, block):
        sems = self.sem_handles

        def make(name):
            def f(eng_obj):
                for eng, ws, emits, incs, dma, base in self.sched:
                    if eng != name:
                        continue
                    for s, v in ws:
                        eng_obj.wait_ge(sems[s], v)
                    strict = (not dma) and eng in ("vector", "scalar", "gpsimd") \
                        and incs and len(incs) == len(emits)
                    insts = []
                    for i, em in enumerate(emits):
                        if strict and i > 0:
                            eng_obj.wait_ge(sems[incs[0][0]], base + i)
                        inst = em(eng_obj)
                        insts.append(inst)
                        if incs and len(incs) == len(emits):
                            inst.then_inc(sems[incs[i][0]], incs[i][1])
                    if incs and len(incs) != len(emits):
                        insts[-1].then_inc(sems[incs[0][0]], 1)
            return f
        block.sync(make("sync"))
        block.gpsimd(make("gpsimd"))
        block.tensor(make("tensor"))
        block.vector(make("vector"))
        block.scalar(make("scalar"))


class Arena:
    def __init__(self, nbytes, esz, name):
        self.esz = esz
        self.free = [(0, nbytes)]
        self.hwm = 0
        self.name = name

    def alloc(self, nelem):
        nb = ((nelem * self.esz + 63) // 64) * 64
        for i, (a, b) in enumerate(self.free):
            if b - a >= nb:
                if a + nb == b:
                    self.free.pop(i)
                else:
                    self.free[i] = (a + nb, b)
                self.hwm = max(self.hwm, a + nb)
                return a // self.esz, nb
        raise RuntimeError(f"{self.name} OOM want {nb} free {self.free}")

    def freed(self, h):
        off, nb = h
        a = off * self.esz
        self.free.append((a, a + nb))
        self.free.sort()
        m = []
        for s in self.free:
            if m and m[-1][1] >= s[0]:
                m[-1] = (m[-1][0], max(m[-1][1], s[1]))
            else:
                m.append(list(s))
        self.free = [tuple(x) for x in m]


def build_program(stop_after=4, PH2_SUB=5):
    nc = bass.Bass("TRN2", target_bir_lowering=False)
    B = Builder()
    ctx = ExitStack()
    g = B.buf

    d_x = nc.dram_tensor("x_fm", [HID, SPAN], F32, kind="ExternalInput").ap()
    d_xp = nc.dram_tensor("xp_fm", [HID, SPAN], F32, kind="ExternalInput").ap()
    d_posb = nc.dram_tensor("pos_base", [128, 1], F32, kind="ExternalInput").ap()
    d_pflag = nc.dram_tensor("pflag", [1, TP], F32, kind="ExternalInput").ap()
    WN = ["wq", "wk", "wv", "wg", "wo", "wupT", "wdnT", "we"]
    d_w = {n: nc.dram_tensor(n, [HID, HID], F32, kind="ExternalInput").ap() for n in WN}
    d_wgT = nc.dram_tensor("wgT", [HID, 8], F32, kind="ExternalInput").ap()
    d_b = {n: nc.dram_tensor("b" + n, [1, HID], F32, kind="ExternalInput").ap()
           for n in ["q", "k", "v", "g", "o"]}
    d_gnw = nc.dram_tensor("gnw", [128, 16], F32, kind="ExternalInput").ap()
    d_gnb = nc.dram_tensor("gnb", [128, 16], F32, kind="ExternalInput").ap()
    d_bgc = nc.dram_tensor("bgcol", [128, 16], F32, kind="ExternalInput").ap()
    d_out = nc.dram_tensor("out_fm", [16, 128, SPAN], F32, kind="ExternalOutput").ap()
    d_wbf = {n: nc.dram_tensor("scr_" + n, [8, 16, 128, 256], BF16, kind="Internal").ap()
             for n in WN}
    d_wgTbf = nc.dram_tensor("scr_wgT", [16, 128, 8], BF16, kind="Internal").ap()
    d_mask = nc.dram_tensor("scr_mask", [NH, 4, 128, T], BF16, kind="Internal").ap()
    d_y = nc.dram_tensor("scr_y", [16, 128, SPAN], F32, kind="Internal").ap()
    d_ffn = nc.dram_tensor("scr_ffn", [16, 128, SPAN], BF16, kind="Internal").ap()

    def sb(name, shape, dt):
        return ctx.enter_context(nc.sbuf_tensor(name, shape, dt))

    ident = sb("ident", [128, 128], BF16)
    identF = sb("identF", [128, 128], F32)
    ones_bf = sb("ones_bf", [1, T], BF16)
    ones_colbf = sb("ones_colbf", [128, 1], BF16)
    ones_bc = sb("ones_bc", [1, 128], BF16)
    oneh8 = sb("oneh8", [128, 64], BF16)
    oneh8T = sb("oneh8T", [8, 1024], BF16)
    rw8 = sb("rw8", [8, 128], F32)
    thcol = sb("thcol", [128, 2], F32)
    altsign = sb("altsign", [128, 1], F32)
    posb = sb("posb", [128, 1], F32)
    gnwc = sb("gnwc", [128, 16], F32)
    gnbc = sb("gnbc", [128, 16], F32)
    bgc = sb("bgc", [128, 16], F32)
    kdec = sb("kdec", [128, 32], F32)
    kdecP = sb("kdecP", [128, 64], F32)
    pfl = sb("pfl", [1, TP], BF16)
    rcols = sb("rcols", [128, 16], F32)
    S_fp = sb("S_fp", [128, 4096], F32)
    S_bf = sb("S_bf", [128, 512], BF16)
    aug_all = sb("aug_all", [2, 5 * HID], BF16)
    AUG = {"wq": 0, "wk": HID, "wv": 2 * HID, "wg": 3 * HID, "wo": 4 * HID,
           "wupT": 0, "we": HID}
    augw = sb("augw", [1, 256], F32)
    aug8 = sb("aug8", [1, 64], BF16)
    mrows = sb("mrows", [2, TP], BF16)
    s_sum = sb("s_sum", [1, 512], F32)
    s_sq = sb("s_sq", [1, 512], F32)
    s_mean = sb("s_mean", [1, 512], F32)
    s_tmp = sb("s_tmp", [1, 512], F32)
    s_r = sb("s_r", [1, 512], F32)
    s_t3 = sb("s_t3", [1, 512], F32)
    r2row = sb("r2row", [1, TP], F32)
    bcm = sb("bcm", [1, TP], BF16)
    bcm2 = sb("bcm2", [1, TP], BF16)
    grows = sb("grows", [8, 3 * T], F32)
    gtmp = sb("gtmp", [8, T], F32)
    gbf = sb("gbf", [8, T], BF16)
    gbf2 = sb("gbf2", [8, T], BF16)
    lgsb = sb("lgsb", [8, TP], BF16)
    lgt = sb("lgt", [128, 64], BF16)
    lgtf = sb("lgtf", [128, 64], F32)
    m8t = sb("m8t", [128, 8], F32)
    rwc = sb("rwc", [128, 8], F32)
    csi_t = sb("csi", [128, 512], I32)
    cvals = sb("cvals", [128, 23], F32)
    cvi1 = sb("cvi1", [128, 1], I32)
    CV_THS, CV_THB, CV_EPS, CV_256 = 0, 1, 2, 3
    CV_LNG = 4
    CV_ONE = 12
    CV_THS2, CV_THB2 = 13, 14
    CV_LNG2 = 15

    ABF = 101 * 1024
    AFW = 23 * 1024
    arb = Arena(ABF, 2, "bf16")
    arf = Arena(AFW, 4, "f32")
    t_bf = sb("arena_bf", [128, ABF // 2], BF16)
    t_f32 = sb("arena_f32", [128, AFW // 4], F32)

    def abf(n):
        o, nb = arb.alloc(n)
        return t_bf[:, o:o + n], (o, nb)

    def af32(n):
        o, nb = arf.alloc(n)
        return t_f32[:, o:o + n], (o, nb)

    ps = ctx.enter_context(nc.psum_tensor("ps", [128, 3584], F32))
    psb = ctx.enter_context(nc.psum_tensor("psb", [128, 1024], BF16))

    def bk(i, parts=128, width=512, off=0):
        return ps[0:parts, 512 * i + off:512 * i + off + width]

    pb = [g(f"ps{i}") for i in range(7)]

    for s in ["dg", "ds", "te", "ve", "se", "gp", "sy"]:
        B.sem_handles[s] = ctx.enter_context(nc.semaphore(s))

    def dma(eng, sem, out_ap, in_ap, reads=(), writes=()):
        return B.step(eng, [lambda e, o=out_ap, i=in_ap: e.dma_start(out=o, in_=i)],
                      reads=reads, writes=writes, mark=sem, dma=True)

    ENG_MARK = {"sync": "sy", "gpsimd": "gp", "tensor": "te", "vector": "ve",
                "scalar": "se"}

    def barrier():
        snap = [(s, v) for s, v in B.cnt.items() if v > 0]
        for e in ENGINES:
            B.step(e, [lambda eng: eng.nop()],
                   extra=[(s, v, None) for s, v in snap], mark=ENG_MARK[e])

    taps_made = {}

    def tap(name, ap, readbufs):
        if name in DEBUG_TAPS and name not in taps_made:
            dd = nc.dram_tensor("tap_" + name, list(ap.shape), ap.dtype,
                                kind="ExternalOutput").ap()
            dma("sync", "ds", dd, ap, reads=list(readbufs))
            taps_made[name] = True

    csc, _ = af32(512)
    csn, _ = af32(512)
    csr, _ = af32(512)
    csS, _ = af32(512)
    csC, _ = af32(512)

    B.step("gpsimd", [lambda e: e.memset(ident[:], 1.0),
                      lambda e: e.memset(identF[:], 1.0)],
           writes=[g("ident")], mark="gp")
    B.step("gpsimd", [lambda e: e.affine_select(ident[:], ident[:], pattern=[[1, 128]],
                                                compare_op=ALU.is_equal, fill=0.0,
                                                base=0, channel_multiplier=-1),
                      lambda e: e.affine_select(identF[:], identF[:], pattern=[[1, 128]],
                                                compare_op=ALU.is_equal, fill=0.0,
                                                base=0, channel_multiplier=-1)],
           writes=[g("ident")], mark="gp")
    B.step("vector", [lambda e: e.memset(cvi1[:], 1),
                      lambda e: e.memset(augw[:], 0.0),
                      lambda e: e.memset(cvals[:, CV_THS:CV_THS + 1], -LN1E4 / 127.0),
                      lambda e: e.memset(cvals[:, CV_THB:CV_THB + 1], -LN1E4 * 64.0 / 127.0),
                      lambda e: e.memset(cvals[:, CV_EPS:CV_EPS + 1], EPS),
                      lambda e: e.memset(cvals[:, CV_256:CV_256 + 1], 1.0 / 256.0),
                      lambda e: e.memset(cvals[:, CV_ONE:CV_ONE + 1], 1.0),
                      lambda e: e.memset(cvals[:, CV_THS2:CV_THS2 + 1], THS_LO),
                      lambda e: e.memset(cvals[:, CV_THB2:CV_THB2 + 1], THB_LO)] +
           [(lambda h: (lambda e: e.memset(cvals[:, CV_LNG2 + h:CV_LNG2 + h + 1],
                                           LNG_LO[h])))(h) for h in range(NH)] +
           [(lambda h: (lambda e: e.memset(cvals[:, CV_LNG + h:CV_LNG + h + 1],
                                           LNG[h])))(h) for h in range(NH)] +
           [lambda e: e.memset(ones_bf[:], 1.0),
                      lambda e: e.memset(ones_colbf[:], 1.0),
                      lambda e: e.memset(ones_bc[:], 1.0),
                      lambda e: e.memset(S_fp[:], 0.0)],
           writes=[g("const1"), g("S_fp"), g("cvals")], mark="ve")
    B.step("gpsimd", [lambda e: e.memset(oneh8[:], 1.0),
                      lambda e: e.memset(oneh8T[:], 1.0)],
           writes=[g("oneh8")], mark="gp")
    B.step("gpsimd", [lambda e: e.affine_select(oneh8[:], oneh8[:],
                                                pattern=[[-1, 8], [1, 8]],
                                                compare_op=ALU.is_equal, fill=0.0,
                                                base=0, channel_multiplier=0),
                      lambda e: e.affine_select(oneh8T[:], oneh8T[:],
                                                pattern=[[1, 8], [0, 128]],
                                                compare_op=ALU.is_equal, fill=0.0,
                                                base=0, channel_multiplier=-1)],
           writes=[g("oneh8")], mark="gp")
    B.step("gpsimd", [lambda e: e.iota(csc[:, 0:64], pattern=[[0, 64]], base=0,
                                       channel_multiplier=1,
                                       allow_small_or_imprecise_dtypes=True)],
           writes=[g("csc")], mark="gp")
    B.step("vector", [
        lambda e: e.tensor_copy(csi_t[:, 0:64], csc[:, 0:64]),
        lambda e: e.tensor_single_scalar(csi_t[:, 64:128], csi_t[:, 0:64],
                                         cvi1[:, 0:1], op=ALU.bitwise_and),
        lambda e: e.tensor_copy(csr[:, 0:64], csi_t[:, 64:128]),
        lambda e: e.scalar_tensor_tensor(csn[:, 0:64], csr[:, 0:64], -1.0,
                                         csc[:, 0:64], op0=ALU.mult, op1=ALU.add),
        lambda e: e.tensor_scalar(csn[:, 0:64], csn[:, 0:64], 0.5, None,
                                  op0=ALU.mult),
    ], reads=[g("csc")], writes=[g("csn"), g("csi"), g("csr")], mark="ve")
    B.step("scalar", [
        lambda e: e.activation(thcol[:, 0:1], csn[:, 0:1], AF.Exp,
                               scale=cvals[:, CV_THS:CV_THS + 1]),
        lambda e: e.activation(thcol[:, 1:2], csn[:, 0:1], AF.Exp,
                               bias=cvals[:, CV_THB:CV_THB + 1],
                               scale=cvals[:, CV_THS:CV_THS + 1]),
        lambda e: e.activation(csS[:, 0:1], csn[:, 0:1], AF.Exp,
                               scale=cvals[:, CV_THS2:CV_THS2 + 1]),
        lambda e: e.activation(csS[:, 1:2], csn[:, 0:1], AF.Exp,
                               bias=cvals[:, CV_THB2:CV_THB2 + 1],
                               scale=cvals[:, CV_THS2:CV_THS2 + 1]),
    ], reads=[g("csn"), g("cvals")], writes=[g("thcol"), g("csS")], mark="se")
    B.step("vector", [
        lambda e: e.tensor_tensor(thcol[:, 0:1], thcol[:, 0:1], csS[:, 0:1],
                                  op=ALU.mult),
        lambda e: e.tensor_tensor(thcol[:, 1:2], thcol[:, 1:2], csS[:, 1:2],
                                  op=ALU.mult),
    ], reads=[g("csS")], writes=[g("thcol")], mark="ve")
    B.step("vector", [
        lambda e: e.tensor_scalar(altsign[:], csr[:, 0:1], 2.0, -1.0,
                                  op0=ALU.mult, op1=ALU.add),
    ], reads=[g("csr")], writes=[g("altsign")], mark="ve")
    B.step("gpsimd", [lambda e: e.iota(kdec[:], pattern=[[0, 8], [-128, 4]], base=T - 1,
                                       channel_multiplier=-1,
                                       allow_small_or_imprecise_dtypes=True),
                      lambda e: e.iota(kdecP[:], pattern=[[0, 8], [-128, 8]], base=TP - 1,
                                       channel_multiplier=-1,
                                       allow_small_or_imprecise_dtypes=True),
                      lambda e: e.iota(csc[:, 0:512], pattern=[[1, 512]], base=1,
                                       channel_multiplier=0,
                                       allow_small_or_imprecise_dtypes=True)],
           reads=[g("csc")], writes=[g("kdecr"), g("csc")], mark="gp")
    B.step("scalar",
           [(lambda h: (lambda e: e.activation(csS[:, 4 * h:4 * h + 4],
                                               kdec[:, 4 * h:4 * h + 4], AF.Exp,
                                               scale=cvals[:, CV_LNG2 + h:CV_LNG2 + h + 1])))(h)
            for h in range(NH)] +
           [(lambda h: (lambda e: e.activation(csC[:, 8 * h:8 * h + 8],
                                               kdecP[:, 8 * h:8 * h + 8], AF.Exp,
                                               scale=cvals[:, CV_LNG2 + h:CV_LNG2 + h + 1])))(h)
            for h in range(NH)] +
           [(lambda h: (lambda e: e.activation(kdec[:, 4 * h:4 * h + 4],
                                               kdec[:, 4 * h:4 * h + 4], AF.Exp,
                                               scale=cvals[:, CV_LNG + h:CV_LNG + h + 1])))(h)
            for h in range(NH)] +
           [(lambda h: (lambda e: e.activation(kdecP[:, 8 * h:8 * h + 8],
                                               kdecP[:, 8 * h:8 * h + 8], AF.Exp,
                                               scale=cvals[:, CV_LNG + h:CV_LNG + h + 1])))(h)
            for h in range(NH)],
           reads=[g("kdecr"), g("csc"), g("cvals")],
           writes=[g("kdec"), g("csS"), g("csC")], mark="se")
    B.step("vector", [
        lambda e: e.tensor_tensor(kdec[:], kdec[:], csS[:, 0:32], op=ALU.mult),
        lambda e: e.tensor_tensor(kdecP[:], kdecP[:], csC[:, 0:64], op=ALU.mult),
    ], reads=[g("csS"), g("csC"), g("kdec")], writes=[g("kdec")], mark="ve")
    gqb, _ = af32(512)
    B.step("vector", [lambda e: e.tensor_copy(gqb[:, 0:512], csc[:, 0:512])],
           reads=[g("csc")], writes=[g("gqb")], mark="ve")
    dma("sync", "ds", gnwc[:], d_gnw, writes=[g("gnw")])
    dma("sync", "ds", gnbc[:], d_gnb, writes=[g("gnb")])
    dma("sync", "ds", posb[:], d_posb, writes=[g("posb")])
    dma("sync", "ds", bgc[:], d_bgc, writes=[g("bgc")])
    dma("gpsimd", "dg", pfl[:], d_pflag, writes=[g("pfl")])
    for n, blk in [("q", AUG["wq"]), ("k", AUG["wk"]), ("v", AUG["wv"])]:
        dma("gpsimd", "dg", aug_all[1:2, blk:blk + HID], d_b[n], writes=[g("augw" + n)])
    dma("gpsimd", "dg", aug_all[0:1, AUG["wo"]:AUG["wo"] + HID], d_b["o"],
        writes=[g("augwo")])
    for n in WN:
        for eh in range(8):
            dma("gpsimd", "dg", d_wbf[n][eh].rearrange("dc p m -> (dc p) m"),
                d_w[n][:, eh * 256:(eh + 1) * 256],
                writes=[g("scr_" + n)])
    dma("gpsimd", "dg", d_wgTbf.rearrange("dc p m -> (dc p) m"), d_wgT,
        writes=[g("scr_wgT")])
    sqt, _ = abf(T)
    e0, _ = abf(T)
    e1, _ = abf(T)
    e2, _ = abf(T)
    e3, _ = abf(T)
    gqs, gqs_h = abf(T)
    mtmp, mtmp_h = abf(T)
    for h in range(NH):
        for ss in range(4):
            B.step("gpsimd", [(lambda sv: (lambda e: e.iota(
                csc[:, 0:T], pattern=[[1, T]], base=-128 * sv,
                channel_multiplier=-1, allow_small_or_imprecise_dtypes=True)))(ss)],
                reads=[g("csc")], writes=[g("csc")], mark="gp")
            B.step("scalar", [
                (lambda hv: (lambda e: e.activation(
                    csn[:, 0:T], csc[:, 0:T], AF.Exp,
                    scale=cvals[:, CV_LNG + hv:CV_LNG + hv + 1])))(h),
                (lambda hv: (lambda e: e.activation(
                    csS[:, 0:T], csc[:, 0:T], AF.Exp,
                    scale=cvals[:, CV_LNG2 + hv:CV_LNG2 + hv + 1])))(h),
            ], reads=[g("csc"), g("cvals")], writes=[g("csn"), g("csS")],
                mark="se")
            B.step("vector", [lambda e: e.tensor_tensor(
                csn[:, 0:T], csn[:, 0:T], csS[:, 0:T], op=ALU.mult)],
                reads=[g("csS")], writes=[g("csn")], mark="ve")
            B.step("gpsimd", [(lambda sv: (lambda e: e.affine_select(
                mtmp, csn[:, 0:T], pattern=[[1, T]],
                compare_op=ALU.is_ge, fill=0.0, base=-128 * sv,
                channel_multiplier=-1)))(ss)],
                reads=[g("csn")], writes=[g("mtmp")], mark="gp")
            dma("sync", "ds", d_mask[h, ss], mtmp, reads=[g("mtmp")])


    def stats_mms(xb, xbbuf, W, bs, bq):
        ntg = W // 512
        for dc in range(16):
            sq_aps = [sqt[:, 0:512], e0[:, 0:512]]
            B.step("scalar", [(lambda d, t: (lambda e: e.activation(
                sq_aps[t], xb[:, d * W + t * 512:d * W + (t + 1) * 512],
                AF.Square)))(dc, tg) for tg in range(ntg)],
                reads=[xbbuf], writes=[g("sqt"), g("e0")], mark="se")
            ems = []
            for tg in range(ntg):
                ems.append((lambda d, t: (lambda e: e.matmul(
                    bk(bs + t, parts=1), ones_colbf[:],
                    xb[:, d * W + t * 512:d * W + (t + 1) * 512],
                    start=(d == 0), stop=(d == 15))))(dc, tg))
                ems.append((lambda d, t, sa=sq_aps[tg]: (lambda e: e.matmul(
                    bk(bq + t, parts=1), ones_colbf[:], sa,
                    start=(d == 0), stop=(d == 15))))(dc, tg))
            B.step("tensor", ems, reads=[xbbuf, g("sqt"), g("e0"), g("const1")],
                   writes=[pb[bs + t] for t in range(ntg)] +
                          [pb[bq + t] for t in range(ntg)], mark="te")

    def strip_chain(W, bs, bq, flagged=False, want_aug=True, div=HID,
                    r2_mul=False, r2_save=False, want_rcols=False):
        for tg in range(W // 512):
            c0 = tg * 512
            B.step("scalar", [
                (lambda t: (lambda e: e.copy(s_sum[0:1, :], bk(bs + t, parts=1))))(tg),
                (lambda t: (lambda e: e.copy(s_sq[0:1, :], bk(bq + t, parts=1))))(tg),
            ], reads=[pb[bs + tg], pb[bq + tg]], writes=[g("strip")], mark="se")
            B.step("vector", [lambda e: e.tensor_scalar(
                s_mean[0:1, :], s_sum[0:1, :], 1.0 / div, None, op0=ALU.mult)],
                reads=[g("strip")], writes=[g("strip")], mark="ve")
            B.step("scalar", [lambda e: e.activation(s_tmp[0:1, :], s_mean[0:1, :],
                                                     AF.Square)],
                   reads=[g("strip")], writes=[g("strip")], mark="se")
            ems = [lambda e: e.scalar_tensor_tensor(
                s_sq[0:1, :], s_sq[0:1, :], 1.0 / div, s_tmp[0:1, :],
                op0=ALU.mult, op1=ALU.subtract)]
            if r2_mul:
                ems.append((lambda c: (lambda e: e.tensor_tensor(
                    s_t3[0:1, :], r2row[0:1, c:c + 512], r2row[0:1, c:c + 512],
                    op=ALU.mult)))(c0))
                ems.append(lambda e: e.tensor_tensor(
                    s_sq[0:1, :], s_sq[0:1, :], s_t3[0:1, :], op=ALU.mult))
            B.step("vector", ems, reads=[g("strip"), g("r2row")],
                   writes=[g("strip")], mark="ve")
            B.step("scalar", [lambda e: e.activation(
                s_tmp[0:1, :], s_sq[0:1, :], AF.Sqrt,
                bias=cvals[0:1, CV_EPS:CV_EPS + 1])],
                reads=[g("strip"), g("cvals")], writes=[g("strip")], mark="se")
            B.step("vector", [lambda e: e.reciprocal(s_sum[0:1, :],
                                                     s_tmp[0:1, :])],
                   reads=[g("strip")], writes=[g("strip")], mark="ve")
            B.step("scalar", [lambda e: e.activation(s_r[0:1, :], s_sum[0:1, :],
                                                     AF.Identity, bias=0.0)],
                   reads=[g("strip")], writes=[g("strip")], mark="se")
            ems = []
            if want_aug:
                ems.append((lambda c: (lambda e: e.tensor_scalar(
                    mrows[0:1, c:c + 512], s_mean[0:1, :], -1.0, None,
                    op0=ALU.mult)))(c0))
                if flagged:
                    ems.append((lambda c: (lambda e: e.tensor_tensor(
                        s_t3[0:1, :], s_tmp[0:1, :], pfl[0:1, c:c + 512],
                        op=ALU.mult)))(c0))
                    ems.append((lambda c: (lambda e: e.tensor_tensor(
                        mrows[0:1, c:c + 512], mrows[0:1, c:c + 512],
                        pfl[0:1, c:c + 512], op=ALU.mult)))(c0))
                else:
                    ems.append(lambda e: e.tensor_copy(s_t3[0:1, :], s_tmp[0:1, :]))
            ems.append((lambda c: (lambda e: e.tensor_copy(
                bcm[0:1, c:c + 512], s_r[0:1, :])))(c0))
            ems.append((lambda c: (lambda e: e.tensor_scalar(
                bcm2[0:1, c:c + 512], s_r[0:1, :], 1.0 / 16.0, None,
                op0=ALU.mult)))(c0))
            if r2_mul:
                ems.append((lambda c: (lambda e: e.tensor_tensor(
                    s_r[0:1, :], s_r[0:1, :], r2row[0:1, c:c + 512],
                    op=ALU.mult)))(c0))
                ems.append((lambda c: (lambda e: e.scalar_tensor_tensor(
                    s_tmp[0:1, :], s_mean[0:1, :], -1.0, s_r[0:1, :],
                    op0=ALU.mult, op1=ALU.mult)))(c0))
                ems.append((lambda c: (lambda e: e.tensor_copy(
                    bcm[0:1, c:c + 512], s_r[0:1, :])))(c0))
                ems.append((lambda c: (lambda e: e.tensor_copy(
                    bcm2[0:1, c:c + 512], s_tmp[0:1, :])))(c0))
            B.step("vector", ems,
                   reads=[g("strip"), g("pfl"), g("r2sq"), g("r2row")],
                   writes=[g("strip"), g("mrows"), g("bcm"), g("bcm2")], mark="ve")
            if want_aug:
                dma("gpsimd", "dg", mrows[1:2, c0:c0 + 512], s_t3[0:1, :],
                    reads=[g("strip")], writes=[g("mrows")])
            if r2_save:
                B.step("vector", [(lambda c: (lambda e: e.tensor_copy(
                    r2row[0:1, c:c + 512], s_r[0:1, :])))(c0)],
                    reads=[g("strip")], writes=[g("r2row")], mark="ve")
            if want_rcols:
                for ts in range(4):
                    b = 4 + (ts % 3)
                    B.step("tensor", [(lambda t, bb: (lambda e: e.transpose(
                        bk(bb, width=1), s_r[0:1, t * 128:(t + 1) * 128],
                        identF[0:1, 0:1])))(ts, b)],
                        reads=[g("strip"), g("ident")], writes=[pb[b]], mark="te")
                    B.step("scalar", [(lambda t, bb, d0=tg * 4: (lambda e: e.copy(
                        rcols[:, d0 + t:d0 + t + 1], bk(bb, width=1))))(ts, b)],
                        reads=[pb[b]], writes=[g("rcols")], mark="se")

    def bcast(row_ap, row_buf, W, bankidx, out_ap, out_buf, parts_k=1, st_ap=None,
              st_buf=None):
        stat = st_ap if st_ap is not None else ones_bc[:]
        ems = []
        for tg in range(W // 512):
            ems.append((lambda t, sa=stat: (lambda e: e.matmul(
                bk(bankidx + t), sa, row_ap[0:parts_k, t * 512:(t + 1) * 512],
                start=True, stop=True)))(tg))
        B.step("tensor", ems,
               reads=[row_buf, g("const1")] + ([st_buf] if st_buf else []),
               writes=[pb[bankidx + t] for t in range(W // 512)], mark="te")
        B.step("scalar", [(lambda t: (lambda e: e.copy(
            out_ap[:, t * 512:(t + 1) * 512], bk(bankidx + t))))(tg)
            for tg in range(W // 512)],
            reads=[pb[bankidx + t] for t in range(W // 512)],
            writes=[out_buf], mark="se")

    def cs_tiles(dst, dstbuf, W, iota_base, use_posb, folds):
        for hf in range(2):
            for sub in range(W // 512):
                o = sub * 512
                B.step("gpsimd", [(lambda ib: (lambda e: e.iota(
                    csc[:, 0:512], pattern=[[1, 512]], base=ib, channel_multiplier=0,
                    allow_small_or_imprecise_dtypes=True)))(iota_base + o)],
                    reads=[g("csc")], writes=[g("csc")], mark="gp")
                ems = []
                if use_posb:
                    ems.append(lambda e: e.tensor_scalar(csc[:, 0:512], csc[:, 0:512],
                                                         posb[:, 0:1], None,
                                                         op0=ALU.add))
                ems += [
                    (lambda h2: (lambda e: e.tensor_scalar(
                        csc[:, 0:512], csc[:, 0:512], thcol[:, h2:h2 + 1], None,
                        op0=ALU.mult)))(hf),
                    lambda e: e.tensor_scalar(csn[:, 0:512], csc[:, 0:512], INV_2PI,
                                              None, op0=ALU.mult),
                    lambda e: e.tensor_copy(csi_t[:, 0:512], csn[:, 0:512]),
                    lambda e: e.tensor_copy(csn[:, 0:512], csi_t[:, 0:512]),
                    lambda e: e.scalar_tensor_tensor(csr[:, 0:512], csn[:, 0:512], -C1,
                                                     csc[:, 0:512], op0=ALU.mult,
                                                     op1=ALU.add),
                    lambda e: e.scalar_tensor_tensor(csr[:, 0:512], csn[:, 0:512], -C2,
                                                     csr[:, 0:512], op0=ALU.mult,
                                                     op1=ALU.add),
                    lambda e: e.scalar_tensor_tensor(csr[:, 0:512], csn[:, 0:512], -C3,
                                                     csr[:, 0:512], op0=ALU.mult,
                                                     op1=ALU.add),
                    lambda e: e.tensor_single_scalar(csc[:, 0:512], csr[:, 0:512], PI,
                                                     op=ALU.is_gt),
                    lambda e: e.scalar_tensor_tensor(csr[:, 0:512], csc[:, 0:512], N2PI,
                                                     csr[:, 0:512], op0=ALU.mult,
                                                     op1=ALU.add),
                    lambda e: e.tensor_single_scalar(csc[:, 0:512], csr[:, 0:512], -PI,
                                                     op=ALU.is_lt),
                    lambda e: e.scalar_tensor_tensor(csr[:, 0:512], csc[:, 0:512], -N2PI,
                                                     csr[:, 0:512], op0=ALU.mult,
                                                     op1=ALU.add),
                ]
                B.step("vector", ems, reads=[g("csc"), g("posb"), g("thcol")],
                       writes=[g("csc"), g("csn"), g("csr"), g("csi")], mark="ve")
                B.step("scalar", [lambda e: e.activation(csS[:, 0:512], csr[:, 0:512],
                                                         AF.Sin)],
                       reads=[g("csr")], writes=[g("csS")], mark="se")
                B.step("vector", [
                    lambda e: e.tensor_scalar(csn[:, 0:512], csr[:, 0:512], HALF_PI,
                                              None, op0=ALU.add),
                    lambda e: e.tensor_single_scalar(csc[:, 0:512], csn[:, 0:512], PI,
                                                     op=ALU.is_gt),
                    lambda e: e.scalar_tensor_tensor(csn[:, 0:512], csc[:, 0:512], N2PI,
                                                     csn[:, 0:512], op0=ALU.mult,
                                                     op1=ALU.add),
                    lambda e: e.tensor_scalar(csS[:, 0:512], csS[:, 0:512],
                                              altsign[:, 0:1], None, op0=ALU.mult),
                ], reads=[g("csr"), g("altsign")],
                    writes=[g("csn"), g("csc"), g("csS")], mark="ve")
                B.step("scalar", [lambda e: e.activation(csC[:, 0:512], csn[:, 0:512],
                                                         AF.Sin)],
                       reads=[g("csn")], writes=[g("csC")], mark="se")
                ems = []
                for fi, (tile, tilebuf) in enumerate(folds):
                    bb = fi * 4 * W + hf * 2 * W
                    ems.append((lambda b2, ta, o2=o: (lambda e: e.tensor_tensor(
                        dst[:, b2 + o2:b2 + o2 + 512], csC[:, 0:512],
                        ta[:, o2:o2 + 512], op=ALU.mult)))(bb, tile))
                    ems.append((lambda b2, ta, o2=o: (lambda e: e.tensor_tensor(
                        dst[:, b2 + W + o2:b2 + W + o2 + 512], csS[:, 0:512],
                        ta[:, o2:o2 + 512], op=ALU.mult)))(bb, tile))
                B.step("vector", ems,
                       reads=[g("csC"), g("csS")] + [fb for _, fb in folds],
                       writes=[dstbuf], mark="ve")

    def stream_w8(n, eh, slot_ap, slot_buf):
        return dma("sync", "ds", slot_ap.rearrange("p (dc m) -> p dc m", m=256),
                   d_wbf[n][eh].rearrange("dc p m -> p dc m"),
                   reads=[g("scr_" + n)], writes=[slot_buf])

    def wsum(n, eh, slot_ap, slot_buf, bankidx):
        ems = []
        for dc in range(16):
            ems.append((lambda d: (lambda e: e.matmul(
                bk(bankidx, parts=1, width=256), ones_colbf[:],
                slot_ap[:, d * 256:(d + 1) * 256],
                start=(d == 0), stop=(d == 15))))(dc))
        B.step("tensor", ems, reads=[slot_buf, g("const1")], writes=[pb[bankidx]],
               mark="te")
        B.step("scalar", [lambda e: e.copy(augw[0:1, 0:256],
                                           bk(bankidx, parts=1, width=256))],
               reads=[pb[bankidx]], writes=[g("augw")], mark="se")
        B.step("vector", [(lambda e2, nn: (lambda e: e.tensor_copy(
            aug_all[0:1, AUG[nn] + e2 * 256:AUG[nn] + (e2 + 1) * 256],
            augw[0:1, 0:256])))(eh, n)],
            reads=[g("augw")], writes=[g("augw" + n[1:])], mark="ve")

    def shift_epi(src_bank, Cap, Sap, csbuf, dst_ap, dstbuf):
        B.step("scalar", [lambda e: e.copy(e0[:, 0:T], bk(src_bank))],
               reads=[pb[src_bank]], writes=[g("e0")], mark="se")
        B.step("vector", [
            lambda e: e.stream_shuffle(e1[:, 0:T], e0[:, 0:T], mask=SHUF_MASK),
            (lambda ca: lambda e: e.tensor_tensor(e2[:, 0:T], e0[:, 0:T], ca,
                                                  op=ALU.mult))(Cap),
        ], reads=[g("e0"), csbuf], writes=[g("e1"), g("e2")], mark="ve")
        B.step("vector", [
            (lambda sa: lambda e: e.tensor_tensor(e3[:, 0:T], e1[:, 0:T], sa,
                                                  op=ALU.mult))(Sap),
            (lambda da: lambda e: e.tensor_tensor(da, e2[:, 0:T], e3[:, 0:T],
                                                  op=ALU.add))(dst_ap),
        ], reads=[g("e1"), g("e2"), csbuf], writes=[g("e3"), dstbuf], mark="ve")

    w8s = [abf(4096) for _ in range(2)]
    xbfP, xbfP_h = abf(16 * TP)
    cskP, cskP_h = abf(4 * TP)
    rkbP, rkbP_h = abf(TP)
    ktP, ktP_h = abf(2 * TP)
    ktmP, ktmP_h = abf(8 * 256)
    vtmP, vtmP_h = abf(8 * 256)

    for pc in range(NPC if stop_after >= 1 else 0):
        cols = slice(pc * TP, (pc + 1) * TP)
        for q4 in range(4):
            dma("gpsimd", "dg",
                xbfP.rearrange("p (dc w) -> p dc w", w=TP)[:, q4 * 4:(q4 + 1) * 4],
                d_xp[:, cols].rearrange("(dc p) w -> p dc w", p=128)[:, q4 * 4:(q4 + 1) * 4],
                reads=[g("xbfP")], writes=[g("xbfP")])
        if PH2_SUB < 1:
            continue
        stats_mms(xbfP, g("xbfP"), TP, 0, 2)
        strip_chain(TP, 0, 2, flagged=True, want_rcols=True)
        if PH2_SUB < 2:
            continue
        bcast(bcm, g("bcm"), TP, 4, rkbP, g("rkbP"))
        cs_tiles(cskP, g("cskP"), TP, pc * TP, False, [(rkbP, g("rkbP"))])
        if PH2_SUB < 3:
            continue
        for h in range(NH if PH2_SUB >= 3 else 0):
            slv = w8s[0]
            slk = w8s[1]
            stream_w8("wv", h, slv[0], g("w8_0"))
            if pc == 0:
                wsum("wv", h, slv[0], g("w8_0"), 6)
            for ts in range(8):
                b = ts % 4
                ems = []
                for dc in range(16):
                    ems.append((lambda d, t, oap=bk(b, width=256): (lambda e: e.matmul(
                        oap,
                        xbfP[:, d * TP + t * 128:d * TP + (t + 1) * 128],
                        slv[0][:, d * 256:(d + 1) * 256],
                        start=(d == 0), stop=False)))(dc, ts))
                ems.append((lambda t, hh, oap=bk(b, width=256): (lambda e: e.matmul(
                    oap, mrows[0:2, t * 128:(t + 1) * 128],
                    aug_all[0:2, AUG["wv"] + hh * 256:AUG["wv"] + (hh + 1) * 256],
                    start=False, stop=True)))(ts, h))
                B.step("tensor", ems,
                       reads=[g("xbfP"), g("w8_0"), g("mrows"), g("augwv")],
                       writes=[pb[b]], mark="te")
                B.step("scalar", [(lambda t, bb: (lambda e: e.activation(
                    vtmP[:, t * 256:(t + 1) * 256], bk(bb, width=256),
                    AF.Identity, bias=0.0, scale=rcols[:, t:t + 1])))(ts, b)],
                    reads=[pb[b], g("rcols")], writes=[g("vtmP")], mark="se")
            if PH2_SUB < 4:
                continue
            stream_w8("wk", h, slk[0], g("w8_1"))
            if pc == 0:
                wsum("wk", h, slk[0], g("w8_1"), 6)
            for m2 in range(2):
                bp = 3 if m2 == 0 else 5
                for tg in range(2):
                    ems = []
                    for dc in range(16):
                        ems.append((lambda d, t, m, oap=bk(bp + tg): (lambda e: e.matmul(
                            oap,
                            slk[0][:, d * 256 + m * 128:d * 256 + (m + 1) * 128],
                            xbfP[:, d * TP + t * 512:d * TP + (t + 1) * 512],
                            start=(d == 0), stop=False)))(dc, tg, m2))
                    ems.append((lambda t, m, hh, oap=bk(bp + tg): (lambda e: e.matmul(
                        oap,
                        aug_all[0:2, AUG["wk"] + hh * 256 + m * 128:
                                AUG["wk"] + hh * 256 + (m + 1) * 128],
                        mrows[0:2, t * 512:(t + 1) * 512],
                        start=False, stop=True)))(tg, m2, h))
                    B.step("tensor", ems,
                           reads=[g("xbfP"), g("w8_1"), g("mrows"), g("augwk")],
                           writes=[pb[bp + tg]], mark="te")
                    co = m2 * 2 * TP + tg * 512
                    shift_epi(bp + tg, cskP[:, co:co + 512],
                              cskP[:, co + TP:co + TP + 512], g("cskP"),
                              ktP[:, m2 * TP + tg * 512:m2 * TP + (tg + 1) * 512],
                              g("ktP"))
            if PH2_SUB < 5:
                continue
            for m2 in range(2):
                for ssb in range(8):
                    B.step("tensor", [(lambda m, s2: (lambda e: e.transpose(
                        psb[:, 0:128],
                        ktP[:, m * TP + s2 * 128:m * TP + (s2 + 1) * 128],
                        ident[:])))(m2, ssb)],
                        reads=[g("ktP"), g("ident")], writes=[g("psb")], mark="te")
                    B.step("scalar", [(lambda m, s2, hh: (lambda e: e.activation(
                        ktmP[:, s2 * 256 + m * 128:s2 * 256 + (m + 1) * 128],
                        psb[:, 0:128], AF.Identity, bias=0.0,
                        scale=kdecP[:, 8 * hh + s2:8 * hh + s2 + 1])))(m2, ssb, h)],
                        reads=[g("psb"), g("kdec")], writes=[g("ktmP")], mark="se")
            for dk2 in range(2):
                b = dk2
                ems = []
                for ssb in range(8):
                    ems.append((lambda s2, d2, oap=bk(b, width=256): (lambda e: e.matmul(
                        oap,
                        ktmP[:, s2 * 256 + d2 * 128:s2 * 256 + (d2 + 1) * 128],
                        vtmP[:, s2 * 256:(s2 + 1) * 256],
                        start=(s2 == 0), stop=(s2 == 7))))(ssb, dk2))
                B.step("tensor", ems, reads=[g("ktmP"), g("vtmP")],
                       writes=[pb[b]], mark="te")
                B.step("vector", [(lambda d2, bb, hh: (lambda e: e.scalar_tensor_tensor(
                    S_fp[:, hh * 512 + d2 * 256:hh * 512 + (d2 + 1) * 256],
                    S_fp[:, hh * 512 + d2 * 256:hh * 512 + (d2 + 1) * 256],
                    GTP[hh], bk(bb, width=256), op0=ALU.mult, op1=ALU.add)))(dk2, b, h)],
                    reads=[pb[b]], writes=[g("S_fp")], mark="ve")

    tap("ktmP", ktmP[:, 0:2048], [g("ktmP")])
    tap("vtmP", vtmP[:, 0:2048], [g("vtmP")])
    tap("ktP", ktP[:, 0:2048], [g("ktP")])
    barrier()
    for hh in (xbfP_h, cskP_h, rkbP_h, ktP_h, ktmP_h, vtmP_h):
        arb.freed(hh)
    tap("S_prior", S_fp[:], [g("S_fp")])

    xbf, xbf_h = abf(16 * T)
    csq, csq_h = abf(8 * T)
    rqb, rqb_h = abf(T)
    rkb, rkb_h = abf(T)
    r1bc, r1bc_h = af32(T)
    qt, qt_h = abf(2 * T)
    ktl, ktl_h = abf(2 * T)
    ktm, ktm_h = abf(4 * 256)
    vtm, vtm_h = abf(4 * 256)
    qhb, qhb_h = abf(2 * T)
    simb, simb_h = abf(4 * T)
    mskb, mskb_h = abf(4 * T)
    retb, retb_h = abf(16 * T)
    sgb, sgb_h = abf(16 * T)
    aob = sgb
    bct0, bct0_h = abf(T)
    bct1, bct1_h = abf(T)
    x32 = [af32(T) for _ in range(2)]
    yt32 = [af32(T) for _ in range(2)]

    def CSQ(m2, kind):
        return csq[:, (m2 * 2 + kind) * T:(m2 * 2 + kind + 1) * T]

    def CSK(m2, kind):
        return csq[:, 4 * T + (m2 * 2 + kind) * T:4 * T + (m2 * 2 + kind + 1) * T]

    n_ch = 0 if stop_after < 2 else (1 if stop_after == 2 else NCH)
    for c in range(n_ch):
        cols = slice(c * T, (c + 1) * T)
        for q4 in range(4):
            dma("gpsimd", "dg",
                xbf.rearrange("p (dc w) -> p dc w", w=T)[:, q4 * 4:(q4 + 1) * 4],
                d_x[:, cols].rearrange("(dc p) w -> p dc w", p=128)[:, q4 * 4:(q4 + 1) * 4],
                reads=[g("xbf")], writes=[g("xbf")])
        stats_mms(xbf, g("xbf"), T, 0, 1)
        strip_chain(T, 0, 1, want_rcols=True)
        bcast(bcm2, g("bcm2"), T, 4, rqb, g("rqb"))
        bcast(bcm, g("bcm"), T, 5, rkb, g("rkb"))
        B.step("tensor", [lambda e: e.matmul(bk(6), ones_bc[:], bcm[0:1, 0:T],
                                             start=True, stop=True)],
               reads=[g("bcm"), g("const1")], writes=[pb[6]], mark="te")
        B.step("scalar", [lambda e: e.copy(r1bc, bk(6))],
               reads=[pb[6]], writes=[g("r1bc")], mark="se")
        cs_tiles(csq, g("csq"), T, c * T, True, [(rqb, g("rqb")), (rkb, g("rkb"))])
        if c == 0:
            tap("csq", csq[:, 0:8 * T], [g("csq")])
            tap("thcol", thcol[:], [g("thcol")])
            tap("altsign", altsign[:], [g("altsign")])
            tap("csS", csS[:, 0:512], [g("csS")])
            tap("csC", csC[:, 0:512], [g("csC")])
            tap("rqb", rqb[:, 0:T], [g("rqb")])
            tap("rkb", rkb[:, 0:T], [g("rkb")])

        for h in range(NH):
            dma("sync", "ds",
                mskb[:, 0:4 * T].rearrange("p (s t) -> p s t", s=4),
                d_mask[h].rearrange("s p t -> p s t"),
                reads=[g("mskslot")], writes=[g("mskslot")])
            B.step("vector", [(lambda hh: (lambda e: e.tensor_copy(
                S_bf[:], S_fp[:, hh * 512:(hh + 1) * 512])))(h)],
                reads=[g("S_fp")], writes=[g("S_bf")], mark="ve")
            stream_w8("wv", h, w8s[0][0], g("w8_0"))
            for ts in range(4):
                b = ts
                ems = []
                for dc in range(16):
                    ems.append((lambda d, t, oap=bk(b, width=256): (lambda e: e.matmul(
                        oap,
                        xbf[:, d * T + t * 128:d * T + (t + 1) * 128],
                        w8s[0][0][:, d * 256:(d + 1) * 256],
                        start=(d == 0), stop=False)))(dc, ts))
                ems.append((lambda t, hh, oap=bk(b, width=256): (lambda e: e.matmul(
                    oap, mrows[0:2, t * 128:(t + 1) * 128],
                    aug_all[0:2, AUG["wv"] + hh * 256:AUG["wv"] + (hh + 1) * 256],
                    start=False, stop=True)))(ts, h))
                B.step("tensor", ems,
                       reads=[g("xbf"), g("w8_0"), g("mrows"), g("augwv")],
                       writes=[pb[b]], mark="te")
                B.step("scalar", [(lambda t, bb: (lambda e: e.activation(
                    vtm[:, t * 256:(t + 1) * 256], bk(bb, width=256),
                    AF.Identity, bias=0.0, scale=rcols[:, t:t + 1])))(ts, b)],
                    reads=[pb[b], g("rcols")], writes=[g("vtm")], mark="se")
            stream_w8("wq", h, w8s[1][0], g("w8_1"))
            if c == 0:
                wsum("wq", h, w8s[1][0], g("w8_1"), 4)
            for m2 in range(2):
                b = 3 + m2
                ems = []
                for dc in range(16):
                    ems.append((lambda d, m, oap=bk(b): (lambda e: e.matmul(
                        oap, w8s[1][0][:, d * 256 + m * 128:d * 256 + (m + 1) * 128],
                        xbf[:, d * T:(d + 1) * T],
                        start=(d == 0), stop=False)))(dc, m2))
                ems.append((lambda m, hh, oap=bk(b): (lambda e: e.matmul(
                    oap, aug_all[0:2, AUG["wq"] + hh * 256 + m * 128:
                                 AUG["wq"] + hh * 256 + (m + 1) * 128],
                    mrows[0:2, 0:T], start=False, stop=True)))(m2, h))
                B.step("tensor", ems,
                       reads=[g("xbf"), g("w8_1"), g("mrows"), g("augwq")],
                       writes=[pb[b]], mark="te")
                shift_epi(b, CSQ(m2, 0), CSQ(m2, 1), g("csq"),
                          qt[:, m2 * T:(m2 + 1) * T], g("qt"))
            stream_w8("wk", h, w8s[0][0], g("w8_0"))
            for m2 in range(2):
                b = 0 + m2
                ems = []
                for dc in range(16):
                    ems.append((lambda d, m, oap=bk(b): (lambda e: e.matmul(
                        oap, w8s[0][0][:, d * 256 + m * 128:d * 256 + (m + 1) * 128],
                        xbf[:, d * T:(d + 1) * T],
                        start=(d == 0), stop=False)))(dc, m2))
                ems.append((lambda m, hh, oap=bk(b): (lambda e: e.matmul(
                    oap, aug_all[0:2, AUG["wk"] + hh * 256 + m * 128:
                                 AUG["wk"] + hh * 256 + (m + 1) * 128],
                    mrows[0:2, 0:T], start=False, stop=True)))(m2, h))
                B.step("tensor", ems,
                       reads=[g("xbf"), g("w8_0"), g("mrows"), g("augwk")],
                       writes=[pb[b]], mark="te")
                shift_epi(b, CSK(m2, 0), CSK(m2, 1), g("csq"),
                          ktl[:, m2 * T:(m2 + 1) * T], g("ktl"))
            for m2 in range(2):
                for ssb in range(4):
                    B.step("tensor", [(lambda m, s2: (lambda e: e.transpose(
                        psb[:, 0:128],
                        ktl[:, m * T + s2 * 128:m * T + (s2 + 1) * 128],
                        ident[:])))(m2, ssb)],
                        reads=[g("ktl"), g("ident")], writes=[g("psb")], mark="te")
                    B.step("scalar", [(lambda m, s2, hh: (lambda e: e.activation(
                        ktm[:, s2 * 256 + m * 128:s2 * 256 + (m + 1) * 128],
                        psb[:, 0:128], AF.Identity, bias=0.0,
                        scale=kdec[:, 4 * hh + s2:4 * hh + s2 + 1])))(m2, ssb, h)],
                        reads=[g("psb"), g("kdec")], writes=[g("ktm")], mark="se")
            B.step("scalar", [(lambda hh: (lambda e: e.activation(
                gqs, gqb[:, 0:512], AF.Exp,
                scale=cvals[:, CV_LNG + hh:CV_LNG + hh + 1])))(h)],
                reads=[g("gqb"), g("cvals")], writes=[g("gqs")], mark="se")
            B.step("vector", [
                (lambda m: (lambda e: e.tensor_tensor(
                    qhb[:, m * T:(m + 1) * T], qt[:, m * T:(m + 1) * T],
                    gqs, op=ALU.mult)))(m2)
                for m2 in range(2)],
                reads=[g("qt"), g("gqs")], writes=[g("qhb")], mark="ve")
            for ssb in range(4):
                b = 3 + (ssb % 2)
                ems = []
                for dh2 in range(2):
                    ems.append((lambda s2, d2, oap=bk(b): (lambda e: e.matmul(
                        oap, ktl[:, d2 * T + s2 * 128:d2 * T + (s2 + 1) * 128],
                        qt[:, d2 * T:(d2 + 1) * T],
                        start=(d2 == 0), stop=(d2 == 1))))(ssb, dh2))
                B.step("tensor", ems, reads=[g("ktl"), g("qt")], writes=[pb[b]],
                       mark="te")
                B.step("vector", [(lambda s2, bb: (lambda e: e.tensor_tensor(
                    simb[:, s2 * T:(s2 + 1) * T], bk(bb),
                    mskb[:, s2 * T:(s2 + 1) * T],
                    op=ALU.mult)))(ssb, b)],
                    reads=[pb[b], g("mskslot")], writes=[g("simb")], mark="ve")
            for dvh in range(2):
                b = 0 if dvh == 0 else 2
                ems = []
                for ssb in range(4):
                    ems.append((lambda s2, dv, oap=bk(b): (lambda e: e.matmul(
                        oap, vtm[:, s2 * 256 + dv * 128:s2 * 256 + (dv + 1) * 128],
                        simb[:, s2 * T:(s2 + 1) * T],
                        start=(s2 == 0), stop=False)))(ssb, dvh))
                for dh2 in range(2):
                    ems.append((lambda d2, dv, oap=bk(b): (lambda e: e.matmul(
                        oap, S_bf[:, d2 * 256 + dv * 128:d2 * 256 + (dv + 1) * 128],
                        qhb[:, d2 * T:(d2 + 1) * T],
                        start=False, stop=(d2 == 1))))(dh2, dvh))
                B.step("tensor", ems, reads=[g("vtm"), g("simb"), g("S_bf"), g("qhb")],
                       writes=[pb[b]], mark="te")
            for dvh in range(2):
                b = 0 if dvh == 0 else 2
                B.step("scalar", [
                    (lambda bb, hh, dv: (lambda e: e.copy(
                        retb[:, (2 * hh + dv) * T:(2 * hh + dv + 1) * T], bk(bb))))(b, h, dvh),
                    (lambda bb: (lambda e: e.activation(
                        sqt[:, 0:T], bk(bb), AF.Square)))(b),
                ], reads=[pb[b]], writes=[g("retb"), g("sqt")], mark="se")
                B.step("tensor", [
                    (lambda hh, dv: (lambda e: e.matmul(
                        bk(5, parts=8), oneh8[:, hh * 8:hh * 8 + 8],
                        retb[:, (2 * hh + dv) * T:(2 * hh + dv + 1) * T],
                        start=(hh == 0 and dv == 0), stop=(hh == 7 and dv == 1))))(h, dvh),
                    (lambda hh, dv: (lambda e: e.matmul(
                        bk(6, parts=8), oneh8[:, hh * 8:hh * 8 + 8], sqt[:, 0:T],
                        start=(hh == 0 and dv == 0), stop=(hh == 7 and dv == 1))))(h, dvh),
                ], reads=[g("retb"), g("sqt"), g("oneh8")],
                    writes=[pb[5], pb[6]], mark="te")
            for dk2 in range(2):
                b = 1 if dk2 == 0 else 3
                ems = []
                for ssb in range(4):
                    ems.append((lambda s2, d2, oap=bk(b, width=256): (lambda e: e.matmul(
                        oap,
                        ktm[:, s2 * 256 + d2 * 128:s2 * 256 + (d2 + 1) * 128],
                        vtm[:, s2 * 256:(s2 + 1) * 256],
                        start=(s2 == 0), stop=(s2 == 3))))(ssb, dk2))
                B.step("tensor", ems, reads=[g("ktm"), g("vtm")], writes=[pb[b]],
                       mark="te")
                B.step("vector", [(lambda d2, bb, hh: (lambda e: e.scalar_tensor_tensor(
                    S_fp[:, hh * 512 + d2 * 256:hh * 512 + (d2 + 1) * 256],
                    S_fp[:, hh * 512 + d2 * 256:hh * 512 + (d2 + 1) * 256],
                    GT[hh], bk(bb, width=256), op0=ALU.mult, op1=ALU.add)))(dk2, b, h)],
                    reads=[pb[b]], writes=[g("S_fp")], mark="ve")
        if c == 0:
            tap("qt0", qt[:, 0:2 * T], [g("qt")])
            tap("vtm0", vtm[:, 0:1024], [g("vtm")])
            tap("retb", retb[:, 0:16 * T], [g("retb")])

        for eh in range(NH):
            stream_w8("wg", eh, w8s[0][0], g("w8_0"))
            if c == 0:
                wsum("wg", eh, w8s[0][0], g("w8_0"), 4)
            for m2 in range(2):
                m = 2 * eh + m2
                b = m2
                ems = []
                for dc in range(16):
                    ems.append((lambda d, mm2, oap=bk(b): (lambda e: e.matmul(
                        oap, w8s[0][0][:, d * 256 + mm2 * 128:d * 256 + (mm2 + 1) * 128],
                        xbf[:, d * T:(d + 1) * T],
                        start=(d == 0), stop=False)))(dc, m2))
                ems.append((lambda mm, oap=bk(b): (lambda e: e.matmul(
                    oap, aug_all[0:1, AUG["wg"] + mm * 128:AUG["wg"] + (mm + 1) * 128],
                    mrows[0:1, 0:T], start=False, stop=True)))(m))
                B.step("tensor", ems,
                       reads=[g("xbf"), g("w8_0"), g("mrows"), g("augwg")],
                       writes=[pb[b]], mark="te")
                B.step("vector", [(lambda bb: (lambda e: e.tensor_tensor(
                    csr[:, 0:T], bk(bb), r1bc, op=ALU.mult)))(b)],
                    reads=[pb[b], g("r1bc")], writes=[g("csr")], mark="ve")
                B.step("scalar", [(lambda mm: (lambda e: e.activation(
                    sgb[:, mm * T:(mm + 1) * T], csr[:, 0:T], AF.Silu,
                    bias=bgc[:, mm:mm + 1], scale=1.0)))(m)],
                    reads=[g("csr"), g("bgc")], writes=[g("sgb")], mark="se")

        GA = grows[0:8, 0:T]
        GB = grows[0:8, T:2 * T]
        GC = grows[0:8, 2 * T:3 * T]
        B.step("scalar", [
            lambda e: e.activation(GA, bk(5, parts=8), AF.Identity,
                                   bias=0.0, scale=cvals[0:8, CV_256:CV_256 + 1]),
            lambda e: e.activation(GB, bk(6, parts=8), AF.Identity,
                                   bias=0.0, scale=cvals[0:8, CV_256:CV_256 + 1]),
        ], reads=[pb[5], pb[6], g("cvals")], writes=[g("grows")], mark="se")
        B.step("scalar", [lambda e: e.activation(GC, GA, AF.Square)],
               reads=[g("grows")], writes=[g("grows")], mark="se")
        B.step("vector", [lambda e: e.tensor_tensor(GB, GB, GC, op=ALU.subtract)],
               reads=[g("grows")], writes=[g("grows")], mark="ve")
        B.step("scalar", [lambda e: e.activation(GC, GB, AF.Sqrt,
                                                 bias=cvals[0:8, CV_EPS:CV_EPS + 1])],
               reads=[g("grows"), g("cvals")], writes=[g("grows")], mark="se")
        B.step("vector", [
            lambda e: e.reciprocal(gtmp[0:8, 0:T], GC),
        ], reads=[g("grows"), g("gtmp")], writes=[g("gtmp")], mark="ve")
        B.step("scalar", [lambda e: e.activation(GB, gtmp[0:8, 0:T],
                                                 AF.Identity, bias=0.0)],
               reads=[g("gtmp")], writes=[g("grows")], mark="se")
        B.step("vector", [
            lambda e: e.scalar_tensor_tensor(GC, GA, -1.0, GB, op0=ALU.mult,
                                             op1=ALU.mult),
            lambda e: e.tensor_copy(gbf[0:8, 0:T], GB),
            lambda e: e.tensor_copy(gbf2[0:8, 0:T], GC),
        ], reads=[g("grows")], writes=[g("grows"), g("gbf")], mark="ve")
        for h in range(NH):
            B.step("tensor", [
                (lambda hh: (lambda e: e.matmul(
                    bk(4), oneh8T[0:8, hh * 128:hh * 128 + 128], gbf[0:8, 0:T],
                    start=True, stop=True)))(h),
                (lambda hh: (lambda e: e.matmul(
                    bk(5), oneh8T[0:8, hh * 128:hh * 128 + 128], gbf2[0:8, 0:T],
                    start=True, stop=True)))(h),
            ], reads=[g("gbf"), g("oneh8")], writes=[pb[4], pb[5]], mark="te")
            B.step("scalar", [lambda e: e.copy(bct0, bk(4)),
                              lambda e: e.copy(bct1, bk(5))],
                   reads=[pb[4], pb[5]], writes=[g("bct")], mark="se")
            for dvh in range(2):
                m = 2 * h + dvh
                B.step("vector", [
                    (lambda mm: (lambda e: e.tensor_tensor(
                        e2[:, 0:T], retb[:, mm * T:(mm + 1) * T], bct0,
                        op=ALU.mult)))(m),
                    (lambda mm: (lambda e: e.tensor_tensor(
                        e3[:, 0:T], e2[:, 0:T], bct1, op=ALU.add)))(m),
                    (lambda mm: (lambda e: e.tensor_scalar(
                        e2[:, 0:T], e3[:, 0:T], gnwc[:, mm:mm + 1],
                        gnbc[:, mm:mm + 1], op0=ALU.mult, op1=ALU.add)))(m),
                    (lambda mm: (lambda e: e.tensor_tensor(
                        retb[:, mm * T:(mm + 1) * T], e2[:, 0:T],
                        sgb[:, mm * T:(mm + 1) * T], op=ALU.mult)))(m),
                ], reads=[g("retb"), g("bct"), g("gnw"), g("gnb"), g("sgb")],
                    writes=[g("retb"), g("e2"), g("e3")], mark="ve")
        if c == 0:
            tap("gate", retb[:, 0:16 * T], [g("retb")])

        for eh in range(NH):
            stream_w8("wo", eh, w8s[1][0], g("w8_1"))
            for m2 in range(2):
                m = 2 * eh + m2
                b = m % 4
                ems = []
                for dc in range(16):
                    ems.append((lambda d, mm2, oap=bk(b): (lambda e: e.matmul(
                        oap, w8s[1][0][:, d * 256 + mm2 * 128:d * 256 + (mm2 + 1) * 128],
                        retb[:, d * T:(d + 1) * T],
                        start=(d == 0), stop=False)))(dc, m2))
                ems.append((lambda mm, oap=bk(b): (lambda e: e.matmul(
                    oap, aug_all[0:1, AUG["wo"] + mm * 128:AUG["wo"] + (mm + 1) * 128],
                    ones_bf[0:1, 0:T], start=False, stop=True)))(m))
                B.step("tensor", ems,
                       reads=[g("retb"), g("w8_1"), g("const1"), g("augwo")],
                       writes=[pb[b]], mark="te")
                B.step("scalar", [
                    (lambda bb, mm: (lambda e: e.copy(
                        aob[:, mm * T:(mm + 1) * T], bk(bb))))(b, m),
                    (lambda bb: (lambda e: e.activation(
                        sqt[:, 0:T], bk(bb), AF.Square)))(b),
                ], reads=[pb[b]], writes=[g("sgb"), g("sqt")], mark="se")
                B.step("tensor", [
                    (lambda mm: (lambda e: e.matmul(
                        bk(5, parts=1), ones_colbf[:], aob[:, mm * T:(mm + 1) * T],
                        start=(mm == 0), stop=(mm == 15))))(m),
                    (lambda mm: (lambda e: e.matmul(
                        bk(6, parts=1), ones_colbf[:], sqt[:, 0:T],
                        start=(mm == 0), stop=(mm == 15))))(m),
                ], reads=[g("sgb"), g("sqt"), g("const1")], writes=[pb[5], pb[6]],
                    mark="te")
        strip_chain(T, 5, 6, want_aug=False)
        B.step("vector", [
            lambda e: e.scalar_tensor_tensor(s_tmp[0:1, :], s_mean[0:1, :], -1.0,
                                             s_r[0:1, :], op0=ALU.mult, op1=ALU.mult),
            lambda e: e.tensor_copy(bcm2[0:1, 0:T], s_tmp[0:1, :]),
        ], reads=[g("strip")], writes=[g("strip"), g("bcm2")], mark="ve")
        bcast(bcm, g("bcm"), T, 4, bct0, g("bct"))
        bcast(bcm2, g("bcm2"), T, 5, bct1, g("bct"))
        for dc in range(16):
            sl = dc % 2
            dma("sync", "ds", x32[sl][0],
                d_x[dc * 128:(dc + 1) * 128, cols],
                reads=[g(f"x32_{sl}")], writes=[g(f"x32_{sl}")])
            B.step("vector", [
                (lambda d: (lambda e: e.tensor_tensor(
                    e2[:, 0:T], aob[:, d * T:(d + 1) * T], bct0, op=ALU.mult)))(dc),
                lambda e: e.tensor_tensor(e3[:, 0:T], e2[:, 0:T], bct1, op=ALU.add),
                (lambda s: (lambda e: e.tensor_tensor(
                    yt32[s][0], e3[:, 0:T], x32[s][0], op=ALU.add)))(sl),
            ], reads=[g("sgb"), g("bct"), g(f"x32_{sl}"), g(f"yt32_{sl}")],
                writes=[g("e2"), g("e3"), g(f"yt32_{sl}")], mark="ve")
            dma("sync", "ds", d_y[dc, :, cols], yt32[sl][0],
                reads=[g(f"yt32_{sl}")])
        if c == 0:
            tap("y0", yt32[0][0], [g("yt32_0")])

    barrier()
    for hh in (xbf_h, csq_h, rqb_h, rkb_h, qt_h, ktl_h, ktm_h, vtm_h, qhb_h,
               simb_h, mskb_h, retb_h, sgb_h, bct0_h, bct1_h, gqs_h, mtmp_h):
        arb.freed(hh)
    arf.freed(r1bc_h)
    for a in x32 + yt32:
        arf.freed(a[1])

    ybf, ybf_h = abf(16 * TP)
    unb, unb_h = abf(16 * TP)
    ffns = [abf(TP) for _ in range(2)]
    ffnr = ffns
    rwbt, rwbt_h = abf(TP)
    pbt, pbt_h = abf(TP)
    bbt, bbt_h = abf(TP)
    fA, fA_h = abf(TP)
    fB, fB_h = abf(TP)
    wgt8, wgt8_h = abf(128)
    y32 = [af32(512) for _ in range(2)]
    o32 = [af32(512) for _ in range(2)]

    for hf4 in range(2 if stop_after >= 4 else 0):
        cols = slice(hf4 * TP, (hf4 + 1) * TP)
        for q4 in range(4):
            dma("gpsimd", "dg",
                ybf.rearrange("p (dc w) -> p dc w", w=TP)[:, q4 * 4:(q4 + 1) * 4],
                d_y[:, :, cols].rearrange("dc p w -> p dc w")[:, q4 * 4:(q4 + 1) * 4],
                reads=[g("ybf")], writes=[g("ybf")])
        stats_mms(ybf, g("ybf"), TP, 0, 2)
        strip_chain(TP, 0, 2, r2_save=True, want_rcols=True)
        dma("sync", "ds", wgt8.rearrange("p (dc m) -> p dc m", m=8),
            d_wgTbf.rearrange("dc p m -> p dc m"),
            reads=[g("scr_wgT")], writes=[g("wgt8")])
        if hf4 == 0:
            ems = []
            for dc in range(16):
                ems.append((lambda d: (lambda e: e.matmul(
                    bk(2, parts=1, width=8), ones_colbf[:], wgt8[:, d * 8:(d + 1) * 8],
                    start=(d == 0), stop=(d == 15))))(dc))
            B.step("tensor", ems, reads=[g("wgt8"), g("const1")], writes=[pb[2]],
                   mark="te")
            B.step("scalar", [lambda e: e.copy(augw[0:1, 0:8],
                                               bk(2, parts=1, width=8))],
                   reads=[pb[2]], writes=[g("augw")], mark="se")
            B.step("vector", [lambda e: e.tensor_copy(aug8[0:1, 0:64],
                                                      augw[0:1, 0:64])],
                   reads=[g("augw")], writes=[g("aug8")], mark="ve")
        for tg in range(2):
            b = 2 + tg
            ems = []
            for dc in range(16):
                ems.append((lambda d, t, oap=bk(b, parts=8): (lambda e: e.matmul(
                    oap, wgt8[:, d * 8:(d + 1) * 8],
                    ybf[:, d * TP + t * 512:d * TP + (t + 1) * 512],
                    start=(d == 0), stop=False)))(dc, tg))
            ems.append((lambda t, oap=bk(b, parts=8): (lambda e: e.matmul(
                oap, aug8[0:1, 0:8], mrows[0:1, t * 512:(t + 1) * 512],
                start=False, stop=True)))(tg))
            B.step("tensor", ems, reads=[g("ybf"), g("wgt8"), g("aug8"), g("mrows")],
                   writes=[pb[b]], mark="te")
            B.step("scalar", [(lambda t, bb: (lambda e: e.copy(
                lgsb[0:8, t * 512:(t + 1) * 512], bk(bb, parts=8))))(tg, b)],
                reads=[pb[b]], writes=[g("lgsb")], mark="se")
        for tb in range(8):
            B.step("tensor", [(lambda t: (lambda e: e.transpose(
                psb[:, 0:8], lgsb[0:8, t * 128:(t + 1) * 128],
                ident[0:8, 0:8])))(tb)],
                reads=[g("lgsb"), g("ident")], writes=[g("psb")], mark="te")
            B.step("scalar", [(lambda t: (lambda e: e.copy(
                lgt[:, t * 8:(t + 1) * 8], psb[:, 0:8])))(tb)],
                reads=[g("psb")], writes=[g("lgt")], mark="se")
        if tb == 7:
            pass
        B.step("vector", [lambda e: e.tensor_copy(lgtf[:], lgt[:])],
               reads=[g("lgt")], writes=[g("lgtf")], mark="ve")
        for tb in range(8):
            B.step("vector", [
                (lambda t: (lambda e: e.max(out=m8t[:], in_=lgtf[:, t * 8:(t + 1) * 8])))(tb),
                lambda e: e.tensor_tensor(m8t[:, 2:3], m8t[:, 1:2], m8t[:, 0:1],
                                          op=ALU.subtract),
            ], reads=[g("lgtf")], writes=[g("m8t")], mark="ve")
            B.step("scalar", [
                (lambda t: (lambda e: e.activation(
                    m8t[:, 3:4], m8t[:, 2:3], AF.Exp, scale=rcols[:, t:t + 1])))(tb),
                lambda e: e.activation(m8t[:, 4:5], m8t[:, 3:4], AF.Identity,
                                       bias=cvals[:, CV_ONE:CV_ONE + 1]),
            ], reads=[g("m8t"), g("rcols"), g("cvals")], writes=[g("m8t")],
                mark="se")
            B.step("vector", [
                lambda e: e.reciprocal(m8t[:, 5:6], m8t[:, 4:5]),
            ], reads=[g("m8t")], writes=[g("m8t")], mark="ve")
            B.step("scalar", [(lambda t: (lambda e: e.activation(
                rwc[:, t:t + 1], m8t[:, 4:5], AF.Identity, bias=0.0,
                scale=m8t[:, 5:6])))(tb)],
                reads=[g("m8t")], writes=[g("rwc")], mark="se")
        B.step("tensor", [lambda e: e.transpose(bk(2, parts=8, width=128),
                                                rwc[:, 0:8], identF[:])],
               reads=[g("rwc"), g("ident")], writes=[pb[2]], mark="te")
        B.step("scalar", [lambda e: e.copy(rw8[0:8, 0:128],
                                           bk(2, parts=8, width=128))],
               reads=[pb[2]], writes=[g("rw8")], mark="se")
        B.step("vector", [lambda e: e.tensor_copy(lgsb[0:8, 0:128], rw8[0:8, 0:128])],
               reads=[g("rw8")], writes=[g("lgsb")], mark="ve")
        for tb in range(8):
            B.step("tensor", [(lambda t: (lambda e: e.matmul(
                bk(2, width=128), oneh8T[0:8, t * 128:t * 128 + 128],
                lgsb[0:8, 0:128], start=True, stop=True)))(tb)],
                reads=[g("lgsb"), g("oneh8")], writes=[pb[2]], mark="te")
            B.step("scalar", [(lambda t: (lambda e: e.copy(
                rwbt[:, t * 128:(t + 1) * 128], bk(2, width=128))))(tb)],
                reads=[pb[2]], writes=[g("rwbt")], mark="se")
        for eh in range(NH):
            stream_w8("wupT", eh, w8s[0][0], g("w8_0"))
            if hf4 == 0:
                wsum("wupT", eh, w8s[0][0], g("w8_0"), 2)
            for m2 in range(2):
                m = 2 * eh + m2
                bb0 = 3
                ems = []
                for dc in range(16):
                    for tg in range(2):
                        ems.append((lambda d, t, mm2: (lambda e: e.matmul(
                            bk(bb0 + t),
                            w8s[0][0][:, d * 256 + mm2 * 128:d * 256 + (mm2 + 1) * 128],
                            ybf[:, d * TP + t * 512:d * TP + (t + 1) * 512],
                            start=(d == 0), stop=False)))(dc, tg, m2))
                for tg in range(2):
                    ems.append((lambda t, mm: (lambda e: e.matmul(
                        bk(bb0 + t),
                        aug_all[0:1, AUG["wupT"] + mm * 128:AUG["wupT"] + (mm + 1) * 128],
                        mrows[0:1, t * 512:(t + 1) * 512],
                        start=False, stop=True)))(tg, m))
                B.step("tensor", ems,
                       reads=[g("ybf"), g("w8_0"), g("mrows"), g("augwupT")],
                       writes=[pb[bb0], pb[bb0 + 1]], mark="te")
                B.step("scalar", [(lambda t, mm, bb: (lambda e: e.activation(
                    unb[:, mm * TP + t * 512:mm * TP + (t + 1) * 512],
                    bk(bb + t), AF.Relu)))(tg, m, bb0) for tg in range(2)],
                    reads=[pb[bb0], pb[bb0 + 1]], writes=[g("unb")], mark="se")
        for dc in range(16):
            B.step("vector", [(lambda d: (lambda e: e.tensor_tensor(
                ybf[:, d * TP:(d + 1) * TP], ybf[:, d * TP:(d + 1) * TP],
                rwbt, op=ALU.mult)))(dc)],
                reads=[g("ybf"), g("rwbt")], writes=[g("ybf")], mark="ve")
        B.step("vector", [lambda e: e.tensor_tensor(mrows[0:1, 0:TP], mrows[0:1, 0:TP],
                                                    rwbt[0:1, 0:TP], op=ALU.mult)],
               reads=[g("mrows"), g("rwbt")], writes=[g("mrows")], mark="ve")
        for eh in range(NH):
            stream_w8("wdnT", eh, w8s[0][0], g("w8_0"))
            stream_w8("we", eh, w8s[1][0], g("w8_1"))
            if hf4 == 0:
                wsum("we", eh, w8s[1][0], g("w8_1"), 2)
            for m2 in range(2):
                m = 2 * eh + m2
                bb0 = 3
                ems = []
                for oc in range(16):
                    for tg in range(2):
                        ems.append((lambda d, t, mm2: (lambda e: e.matmul(
                            bk(bb0 + t),
                            w8s[0][0][:, d * 256 + mm2 * 128:d * 256 + (mm2 + 1) * 128],
                            unb[:, d * TP + t * 512:d * TP + (t + 1) * 512],
                            start=(d == 0), stop=False)))(oc, tg, m2))
                for dc in range(16):
                    for tg in range(2):
                        ems.append((lambda d, t, mm2: (lambda e: e.matmul(
                            bk(bb0 + t),
                            w8s[1][0][:, d * 256 + mm2 * 128:d * 256 + (mm2 + 1) * 128],
                            ybf[:, d * TP + t * 512:d * TP + (t + 1) * 512],
                            start=False, stop=False)))(dc, tg, m2))
                for tg in range(2):
                    ems.append((lambda t, mm: (lambda e: e.matmul(
                        bk(bb0 + t),
                        aug_all[0:1, AUG["we"] + mm * 128:AUG["we"] + (mm + 1) * 128],
                        mrows[0:1, t * 512:(t + 1) * 512],
                        start=False, stop=True)))(tg, m))
                B.step("tensor", ems,
                       reads=[g("ybf"), g("unb"), g("w8_0"), g("w8_1"), g("mrows"),
                              g("augwe")],
                       writes=[pb[bb0], pb[bb0 + 1]], mark="te")
                fsl = m % 2
                B.step("scalar", [
                    (lambda t, bb, s: (lambda e: e.copy(
                        ffns[s][0][:, t * 512:(t + 1) * 512],
                        bk(bb + t))))(0, bb0, fsl),
                    (lambda bb: (lambda e: e.activation(
                        sqt[:, 0:T], bk(bb), AF.Square)))(bb0),
                    (lambda t, bb, s: (lambda e: e.copy(
                        ffns[s][0][:, t * 512:(t + 1) * 512],
                        bk(bb + t))))(1, bb0, fsl),
                    (lambda bb: (lambda e: e.activation(
                        e0[:, 0:T], bk(bb + 1), AF.Square)))(bb0),
                ], reads=[pb[bb0], pb[bb0 + 1]]
                    + [g(f"ffns{fsl}")],
                    writes=[g(f"ffns{fsl}"), g("sqt"), g("e0")], mark="se")
                dma("sync", "ds", d_ffn[m, :, cols], ffns[fsl][0],
                    reads=[g(f"ffns{fsl}")])
                B.step("tensor", [
                    (lambda mm, s: (lambda e: e.matmul(
                        bk(0, parts=1), ones_colbf[:], ffns[s][0][:, 0:512],
                        start=(mm == 0), stop=(mm == 15))))(m, fsl),
                    (lambda mm: (lambda e: e.matmul(
                        bk(5, parts=1), ones_colbf[:], sqt[:, 0:T],
                        start=(mm == 0), stop=(mm == 15))))(m),
                    (lambda mm, s: (lambda e: e.matmul(
                        bk(1, parts=1), ones_colbf[:], ffns[s][0][:, 512:1024],
                        start=(mm == 0), stop=(mm == 15))))(m, fsl),
                    (lambda mm: (lambda e: e.matmul(
                        bk(6, parts=1), ones_colbf[:], e0[:, 0:T],
                        start=(mm == 0), stop=(mm == 15))))(m),
                ], reads=[g(f"ffns{fsl}"), g("sqt"), g("e0"), g("const1")],
                    writes=[pb[0], pb[1], pb[5], pb[6]], mark="te")
        strip_chain(TP, 0, 5, want_aug=False, r2_mul=True)
        bcast(bcm, g("bcm"), TP, 2, pbt, g("pbt"))
        bcast(bcm2, g("bcm2"), TP, 2, bbt, g("bbt"))
        for dc in range(16):
            for tg in range(2):
                sl = (2 * dc + tg) % 2
                c2 = slice(hf4 * TP + tg * 512, hf4 * TP + (tg + 1) * 512)
                dma("sync", "ds", y32[sl][0], d_y[dc, :, c2],
                    reads=[g(f"y32_{sl}")], writes=[g(f"y32_{sl}")])
                dma("sync", "ds", ffnr[sl][0][:, 0:512], d_ffn[dc, :, c2],
                    reads=[g(f"ffns{sl}")], writes=[g(f"ffns{sl}")])
                B.step("vector", [
                    (lambda s, t: (lambda e: e.tensor_tensor(
                        fA[:, 0:512], ffnr[s][0][:, 0:512],
                        pbt[:, t * 512:(t + 1) * 512], op=ALU.mult)))(sl, tg),
                    (lambda t: (lambda e: e.tensor_tensor(
                        fB[:, 0:512], fA[:, 0:512],
                        bbt[:, t * 512:(t + 1) * 512], op=ALU.add)))(tg),
                    (lambda s: (lambda e: e.tensor_tensor(
                        o32[s][0], fB[:, 0:512], y32[s][0], op=ALU.add)))(sl),
                ], reads=[g(f"ffns{sl}"), g("pbt"), g("bbt"), g(f"y32_{sl}"),
                          g(f"o32_{sl}")],
                    writes=[g("fA"), g("fB"), g(f"o32_{sl}")], mark="ve")
                dma("sync", "ds", d_out[dc, :, c2], o32[sl][0],
                    reads=[g(f"o32_{sl}")])

    if stop_after >= 4:
        tap("ybf4", ybf[:, 0:2 * TP], [g("ybf")])
        tap("lgtf", lgtf[:], [g("lgtf")])
        tap("rwc", rwc[:, 0:8], [g("rwc")])
        tap("rwbt", rwbt[:], [g("rwbt")])
        tap("pbt", pbt[:], [g("pbt")])
        tap("bbt", bbt[:], [g("bbt")])
        tap("r2row", r2row[0:1, :], [g("r2row")])
        tap("unb4", unb[:, 0:2 * TP], [g("unb")])
        tap("mrows4", mrows[0:2, 0:TP], [g("mrows")])
        tap("o32l", o32[1][0][:], [g("o32_1")])

    B.step("sync", [lambda e: e.nop()], extra=[("ds", B.cnt.get("ds", 0), None)])

    globals()["LAST_B"] = B
    B.check()
    with nc.Block() as block:
        B.emit_all(block)
    ctx.close()
    return nc


_CACHE = {}


STOP_AFTER = 4
PH2SUB = 5


def _get_nc():
    key = ("nc", STOP_AFTER, PH2SUB)
    if key not in _CACHE:
        _CACHE[key] = build_program(STOP_AFTER, PH2SUB)
    return _CACHE[key]


def kernel(**inputs):
    x = np.ascontiguousarray(np.asarray(inputs["L0_attn_x"], dtype=np.float32))
    w = {k: np.ascontiguousarray(np.asarray(inputs[k], dtype=np.float32))
         for k in inputs}
    nc = _get_nc()

    wq = w["L0_attn_w_q"]; wk = w["L0_attn_w_k"]; wv = w["L0_attn_w_v"]
    wg = w["L0_attn_w_g"]; wo = w["L0_attn_w_o"]
    wupT = np.ascontiguousarray(w["L0_ffn_W_up"].T)
    wdnT = np.ascontiguousarray(w["L0_ffn_W_down"].T)
    wgT = np.ascontiguousarray(w["L0_ffn_W_gate"].T)
    we = w["L0_ffn_W_expert"]
    colify = lambda v: np.ascontiguousarray(v.reshape(16, 128).T)
    common = {
        "wq": wq, "wk": wk, "wv": wv, "wg": wg, "wo": wo,
        "wupT": wupT, "wdnT": wdnT, "we": we, "wgT": wgT,
        "bq": w["L0_attn_b_q"].reshape(1, HID),
        "bk": w["L0_attn_b_k"].reshape(1, HID),
        "bv": w["L0_attn_b_v"].reshape(1, HID),
        "bg": w["L0_attn_b_g"].reshape(1, HID),
        "bo": w["L0_attn_b_o"].reshape(1, HID),
        "gnw": colify(w["L0_attn_gn_weight"]),
        "gnb": colify(w["L0_attn_gn_bias"]),
        "bgcol": colify(w["L0_attn_b_g"]),
    }
    zeros_fm = np.zeros((HID, SPAN), dtype=np.float32)
    in_maps = []
    for core in range(8):
        b, hf = core // 2, core % 2
        xs = np.ascontiguousarray(x[b, hf * SPAN:(hf + 1) * SPAN].T)
        xp = (np.ascontiguousarray(x[b, 0:SPAN].T) if hf == 1 else zeros_fm)
        im = dict(common)
        im["x_fm"] = xs
        im["xp_fm"] = xp
        im["pos_base"] = np.full((128, 1), float(hf * SPAN), dtype=np.float32)
        im["pflag"] = np.full((1, TP), 1.0 if hf == 1 else 0.0, dtype=np.float32)
        in_maps.append(im)

    res = run_bass_kernel_spmd(nc, in_maps, core_ids=list(range(8)))
    out = np.empty((4, 4096, HID), dtype=np.float32)
    for core in range(8):
        b, hf = core // 2, core % 2
        o = np.asarray(res.results[core]["out_fm"])
        out[b, hf * SPAN:(hf + 1) * SPAN] = o.reshape(HID, SPAN).T
    return out


# version 08
# speedup vs baseline: 1.0697x; 1.0585x over previous
import math
from contextlib import ExitStack

import numpy as np
import ml_dtypes

import concourse.bass as bass
import concourse.mybir as mybir
from concourse.bass_utils import run_bass_kernel_spmd

F32 = mybir.dt.float32
BF16 = mybir.dt.bfloat16
I32 = mybir.dt.int32
AF = mybir.ActivationFunctionType
ALU = mybir.AluOpType

HID, NH = 2048, 8
T, NCH = 512, 4
TP, NPC = 1024, 2
SPAN = 2048
EPS = 1e-5
_2PI = 2.0 * math.pi
C1 = float(np.float32(_2PI))
C2 = float(np.float32(_2PI - np.float64(np.float32(_2PI))))
C3 = float(np.float32(_2PI - np.float64(np.float32(_2PI))
                      - np.float64(np.float32(_2PI - np.float64(np.float32(_2PI))))))
HALF_PI = float(np.float32(math.pi / 2))
PI = float(np.float32(math.pi))
N2PI = -float(np.float32(_2PI))
INV_2PI = float(np.float32(1.0 / _2PI))
LN1E4 = float(math.log(10000.0))
GAMMAS = [float(v) for v in (1.0 - np.exp(np.linspace(np.log(1 / 32), np.log(1 / 512), 8)))]
LNG = [float(np.float32(math.log(v))) for v in GAMMAS]
LNG_LO = [float(np.float64(math.log(v)) - np.float64(np.float32(math.log(v))))
          for v in GAMMAS]
THS_HI = float(np.float32(-LN1E4 / 127.0))
THS_LO = float(np.float64(-LN1E4 / 127.0) - np.float64(THS_HI))
THB_HI = float(np.float32(-LN1E4 * 64.0 / 127.0))
THB_LO = float(np.float64(-LN1E4 * 64.0 / 127.0) - np.float64(THB_HI))
GT = [float(np.float32(v) ** T) for v in GAMMAS]
GTP = [float(np.float32(v) ** TP) for v in GAMMAS]

ENGINES = ("sync", "gpsimd", "tensor", "vector", "scalar")
DEBUG_TAPS = []
SHUF_MASK = []
for _i in range(0, 32, 2):
    SHUF_MASK += [_i + 1, _i]


class Buf:
    __slots__ = ("name", "w", "rs")

    def __init__(self, name):
        self.name = name
        self.w = None
        self.rs = []


class Builder:
    def __init__(self):
        self.sched = []
        self.cnt = {}
        self.floor = {e: {} for e in ENGINES}
        self.sem_handles = {}
        self.bufs = {}
        self.last_dma = {}

    def buf(self, name):
        if name not in self.bufs:
            self.bufs[name] = Buf(name)
        return self.bufs[name]

    def step(self, eng, emits, reads=(), writes=(), mark=None, dma=False, extra=()):
        evs = [b.w for b in reads]
        for b in writes:
            evs.append(b.w)
            evs.extend(b.rs)
        evs.extend(extra)
        if dma and mark is not None and self.cnt.get(mark, 0) > 0:
            wkey = tuple(sorted(id(b) for b in writes))
            prev = self.last_dma.get(mark)
            if not (wkey and prev == (wkey, len(self.sched))):
                evs.append((mark, self.cnt[mark], None))
        if dma and mark is not None:
            self.last_dma[mark] = (tuple(sorted(id(b) for b in writes)),
                                   len(self.sched) + 1)
        ws = []
        fl = self.floor[eng]
        for ev in evs:
            if ev is None:
                continue
            s, v, peng = ev
            if v <= 0 or s is None:
                continue
            if fl.get(s, 0) >= v:
                continue
            fl[s] = v
            ws.append((s, v))
        incs = []
        ev = None
        if mark is not None:
            if dma:
                amt = 16 * len(emits)
                incs = [(mark, 16)] * len(emits)
            elif eng in ("vector", "scalar", "gpsimd"):
                amt = len(emits)
                incs = [(mark, 1)] * len(emits)
            else:
                amt = 1
                incs = [(mark, 1)]
            base = self.cnt.get(mark, 0)
            self.cnt[mark] = base + amt
            ev = (mark, self.cnt[mark], None if dma else eng)
        else:
            base = 0
        self.sched.append((eng, ws, emits, incs, dma, base))
        if ev is not None:
            for b in writes:
                b.w = ev
                b.rs = []
            for b in reads:
                b.rs.append(ev)
        else:
            for b in writes:
                if b.w is None:
                    b.w = (None, 0, eng)
        return ev

    def check(self):
        qs = {e: [] for e in ENGINES}
        for it in self.sched:
            qs[it[0]].append(it)
        idx = {e: 0 for e in ENGINES}
        sim = {}
        prog = True
        while prog:
            prog = False
            for e in ENGINES:
                while idx[e] < len(qs[e]):
                    _, ws, emits, incs, dma, _base = qs[e][idx[e]]
                    if all(sim.get(s, 0) >= v for s, v in ws):
                        for s, a in incs:
                            sim[s] = sim.get(s, 0) + a
                        idx[e] += 1
                        prog = True
                    else:
                        break
        stuck = {e: i for e, i in idx.items() if i < len(qs[e])}
        if stuck:
            lines = []
            for e, i in stuck.items():
                _, ws, _, _, _ = qs[e][i]
                lines.append(f"{e}[{i}/{len(qs[e])}] waits " +
                             ",".join(f"{s}>={v}(at {sim.get(s, 0)})" for s, v in ws))
            raise RuntimeError("schedule deadlock:\n" + "\n".join(lines))

    def emit_all(self, block):
        sems = self.sem_handles

        def make(name):
            def f(eng_obj):
                for eng, ws, emits, incs, dma, base in self.sched:
                    if eng != name:
                        continue
                    for s, v in ws:
                        eng_obj.wait_ge(sems[s], v)
                    strict = (not dma) and eng in ("vector", "scalar", "gpsimd") \
                        and incs and len(incs) == len(emits)
                    insts = []
                    for i, em in enumerate(emits):
                        if strict and i > 0:
                            eng_obj.wait_ge(sems[incs[0][0]], base + i)
                        inst = em(eng_obj)
                        insts.append(inst)
                        if incs and len(incs) == len(emits):
                            inst.then_inc(sems[incs[i][0]], incs[i][1])
                    if incs and len(incs) != len(emits):
                        insts[-1].then_inc(sems[incs[0][0]], 1)
            return f
        block.sync(make("sync"))
        block.gpsimd(make("gpsimd"))
        block.tensor(make("tensor"))
        block.vector(make("vector"))
        block.scalar(make("scalar"))


class Arena:
    def __init__(self, nbytes, esz, name):
        self.esz = esz
        self.free = [(0, nbytes)]
        self.hwm = 0
        self.name = name

    def alloc(self, nelem):
        nb = ((nelem * self.esz + 63) // 64) * 64
        for i, (a, b) in enumerate(self.free):
            if b - a >= nb:
                if a + nb == b:
                    self.free.pop(i)
                else:
                    self.free[i] = (a + nb, b)
                self.hwm = max(self.hwm, a + nb)
                return a // self.esz, nb
        raise RuntimeError(f"{self.name} OOM want {nb} free {self.free}")

    def freed(self, h):
        off, nb = h
        a = off * self.esz
        self.free.append((a, a + nb))
        self.free.sort()
        m = []
        for s in self.free:
            if m and m[-1][1] >= s[0]:
                m[-1] = (m[-1][0], max(m[-1][1], s[1]))
            else:
                m.append(list(s))
        self.free = [tuple(x) for x in m]


def build_program(stop_after=4, PH2_SUB=5):
    nc = bass.Bass("TRN2", target_bir_lowering=False)
    B = Builder()
    ctx = ExitStack()
    g = B.buf

    d_x = nc.dram_tensor("x_fm", [HID, SPAN], F32, kind="ExternalInput").ap()
    d_xp = nc.dram_tensor("xp_fm", [HID, SPAN], F32, kind="ExternalInput").ap()
    d_posb = nc.dram_tensor("pos_base", [128, 1], F32, kind="ExternalInput").ap()
    d_pflag = nc.dram_tensor("pflag", [1, TP], F32, kind="ExternalInput").ap()
    WN = ["wq", "wk", "wv", "wg", "wo", "wupT", "wdnT", "we"]
    d_w = {n: nc.dram_tensor(n, [HID, HID], F32, kind="ExternalInput").ap() for n in WN}
    d_wgT = nc.dram_tensor("wgT", [HID, 8], F32, kind="ExternalInput").ap()
    d_b = {n: nc.dram_tensor("b" + n, [1, HID], F32, kind="ExternalInput").ap()
           for n in ["q", "k", "v", "g", "o"]}
    d_gnw = nc.dram_tensor("gnw", [128, 16], F32, kind="ExternalInput").ap()
    d_gnb = nc.dram_tensor("gnb", [128, 16], F32, kind="ExternalInput").ap()
    d_bgc = nc.dram_tensor("bgcol", [128, 16], F32, kind="ExternalInput").ap()
    d_out = nc.dram_tensor("out_fm", [16, 128, SPAN], F32, kind="ExternalOutput").ap()
    d_wbf = {n: nc.dram_tensor("scr_" + n, [8, 16, 128, 256], BF16, kind="Internal").ap()
             for n in WN}
    d_wgTbf = nc.dram_tensor("scr_wgT", [16, 128, 8], BF16, kind="Internal").ap()
    d_mask = nc.dram_tensor("scr_mask", [NH, 4, 128, T], BF16, kind="Internal").ap()
    d_y = nc.dram_tensor("scr_y", [16, 128, SPAN], F32, kind="Internal").ap()
    d_ffn = nc.dram_tensor("scr_ffn", [16, 128, SPAN], BF16, kind="Internal").ap()

    def sb(name, shape, dt):
        return ctx.enter_context(nc.sbuf_tensor(name, shape, dt))

    ident = sb("ident", [128, 128], BF16)
    identF = sb("identF", [128, 128], F32)
    ones_bf = sb("ones_bf", [1, T], BF16)
    ones_colbf = sb("ones_colbf", [128, 1], BF16)
    ones_bc = sb("ones_bc", [1, 128], BF16)
    oneh8 = sb("oneh8", [128, 64], BF16)
    oneh8T = sb("oneh8T", [8, 1024], BF16)
    rw8 = sb("rw8", [8, 128], F32)
    thcol = sb("thcol", [128, 2], F32)
    altsign = sb("altsign", [128, 1], F32)
    posb = sb("posb", [128, 1], F32)
    gnwc = sb("gnwc", [128, 16], F32)
    gnbc = sb("gnbc", [128, 16], F32)
    bgc = sb("bgc", [128, 16], F32)
    kdec = sb("kdec", [128, 32], F32)
    kdecP = sb("kdecP", [128, 64], F32)
    pfl = sb("pfl", [1, TP], BF16)
    rcols = sb("rcols", [128, 16], F32)
    S_fp = sb("S_fp", [128, 4096], F32)
    S_bf = sb("S_bf", [128, 512], BF16)
    aug_all = sb("aug_all", [2, 5 * HID], BF16)
    AUG = {"wq": 0, "wk": HID, "wv": 2 * HID, "wg": 3 * HID, "wo": 4 * HID,
           "wupT": 0, "we": HID}
    augw = sb("augw", [1, 256], F32)
    aug8 = sb("aug8", [1, 64], BF16)
    mrows = sb("mrows", [2, TP], BF16)
    s_sum = sb("s_sum", [1, 512], F32)
    s_sq = sb("s_sq", [1, 512], F32)
    s_mean = sb("s_mean", [1, 512], F32)
    s_tmp = sb("s_tmp", [1, 512], F32)
    s_r = sb("s_r", [1, 512], F32)
    s_t3 = sb("s_t3", [1, 512], F32)
    r2row = sb("r2row", [1, TP], F32)
    bcm = sb("bcm", [1, TP], BF16)
    bcm2 = sb("bcm2", [1, TP], BF16)
    grows = sb("grows", [8, 3 * T], F32)
    gtmp = sb("gtmp", [8, T], F32)
    gbf = sb("gbf", [8, T], BF16)
    gbf2 = sb("gbf2", [8, T], BF16)
    lgsb = sb("lgsb", [8, TP], BF16)
    lgt = sb("lgt", [128, 64], BF16)
    lgtf = sb("lgtf", [128, 64], F32)
    m8t = sb("m8t", [128, 8], F32)
    rwc = sb("rwc", [128, 8], F32)
    csi_t = sb("csi", [128, 512], I32)
    cvals = sb("cvals", [128, 23], F32)
    cvi1 = sb("cvi1", [128, 1], I32)
    CV_THS, CV_THB, CV_EPS, CV_256 = 0, 1, 2, 3
    CV_LNG = 4
    CV_ONE = 12
    CV_THS2, CV_THB2 = 13, 14
    CV_LNG2 = 15

    ABF = 101 * 1024
    AFW = 23 * 1024
    arb = Arena(ABF, 2, "bf16")
    arf = Arena(AFW, 4, "f32")
    t_bf = sb("arena_bf", [128, ABF // 2], BF16)
    t_f32 = sb("arena_f32", [128, AFW // 4], F32)

    def abf(n):
        o, nb = arb.alloc(n)
        return t_bf[:, o:o + n], (o, nb)

    def af32(n):
        o, nb = arf.alloc(n)
        return t_f32[:, o:o + n], (o, nb)

    ps = ctx.enter_context(nc.psum_tensor("ps", [128, 3584], F32))
    psb = ctx.enter_context(nc.psum_tensor("psb", [128, 1024], BF16))

    def bk(i, parts=128, width=512, off=0):
        return ps[0:parts, 512 * i + off:512 * i + off + width]

    pb = [g(f"ps{i}") for i in range(7)]

    for s in (["te", "ve", "se", "gp", "sy"]
              + [q + str(i) for q in ("dg", "ds") for i in range(4)]):
        B.sem_handles[s] = ctx.enter_context(nc.semaphore(s))

    DMA_K = 4
    dma_rot = {}
    dma_last = {}

    def dma(eng, sem, out_ap, in_ap, reads=(), writes=()):
        wkey = tuple(sorted(id(b) for b in writes))
        last = dma_last.get(sem)
        if not (last and wkey and last[0] == wkey and last[1] == len(B.sched)):
            dma_rot[sem] = dma_rot.get(sem, 0) + 1
        mark = sem + str(dma_rot.get(sem, 0) % DMA_K)
        dma_last[sem] = (wkey, len(B.sched) + 1)
        return B.step(eng, [lambda e, o=out_ap, i=in_ap: e.dma_start(out=o, in_=i)],
                      reads=reads, writes=writes, mark=mark, dma=True)

    ENG_MARK = {"sync": "sy", "gpsimd": "gp", "tensor": "te", "vector": "ve",
                "scalar": "se"}

    def barrier():
        snap = [(s, v) for s, v in B.cnt.items() if v > 0]
        for e in ENGINES:
            B.step(e, [lambda eng: eng.nop()],
                   extra=[(s, v, None) for s, v in snap], mark=ENG_MARK[e])

    taps_made = {}

    def tap(name, ap, readbufs):
        if name in DEBUG_TAPS and name not in taps_made:
            dd = nc.dram_tensor("tap_" + name, list(ap.shape), ap.dtype,
                                kind="ExternalOutput").ap()
            dma("sync", "ds", dd, ap, reads=list(readbufs))
            taps_made[name] = True

    csc, _ = af32(512)
    csn, _ = af32(512)
    csr, _ = af32(512)
    csS, _ = af32(512)
    csC, _ = af32(512)

    B.step("gpsimd", [lambda e: e.memset(ident[:], 1.0),
                      lambda e: e.memset(identF[:], 1.0)],
           writes=[g("ident")], mark="gp")
    B.step("gpsimd", [lambda e: e.affine_select(ident[:], ident[:], pattern=[[1, 128]],
                                                compare_op=ALU.is_equal, fill=0.0,
                                                base=0, channel_multiplier=-1),
                      lambda e: e.affine_select(identF[:], identF[:], pattern=[[1, 128]],
                                                compare_op=ALU.is_equal, fill=0.0,
                                                base=0, channel_multiplier=-1)],
           writes=[g("ident")], mark="gp")
    B.step("vector", [lambda e: e.memset(cvi1[:], 1),
                      lambda e: e.memset(augw[:], 0.0),
                      lambda e: e.memset(cvals[:, CV_THS:CV_THS + 1], -LN1E4 / 127.0),
                      lambda e: e.memset(cvals[:, CV_THB:CV_THB + 1], -LN1E4 * 64.0 / 127.0),
                      lambda e: e.memset(cvals[:, CV_EPS:CV_EPS + 1], EPS),
                      lambda e: e.memset(cvals[:, CV_256:CV_256 + 1], 1.0 / 256.0),
                      lambda e: e.memset(cvals[:, CV_ONE:CV_ONE + 1], 1.0),
                      lambda e: e.memset(cvals[:, CV_THS2:CV_THS2 + 1], THS_LO),
                      lambda e: e.memset(cvals[:, CV_THB2:CV_THB2 + 1], THB_LO)] +
           [(lambda h: (lambda e: e.memset(cvals[:, CV_LNG2 + h:CV_LNG2 + h + 1],
                                           LNG_LO[h])))(h) for h in range(NH)] +
           [(lambda h: (lambda e: e.memset(cvals[:, CV_LNG + h:CV_LNG + h + 1],
                                           LNG[h])))(h) for h in range(NH)] +
           [lambda e: e.memset(ones_bf[:], 1.0),
                      lambda e: e.memset(ones_colbf[:], 1.0),
                      lambda e: e.memset(ones_bc[:], 1.0),
                      lambda e: e.memset(S_fp[:], 0.0)],
           writes=[g("const1"), g("S_fp"), g("cvals")], mark="ve")
    B.step("gpsimd", [lambda e: e.memset(oneh8[:], 1.0),
                      lambda e: e.memset(oneh8T[:], 1.0)],
           writes=[g("oneh8")], mark="gp")
    B.step("gpsimd", [lambda e: e.affine_select(oneh8[:], oneh8[:],
                                                pattern=[[-1, 8], [1, 8]],
                                                compare_op=ALU.is_equal, fill=0.0,
                                                base=0, channel_multiplier=0),
                      lambda e: e.affine_select(oneh8T[:], oneh8T[:],
                                                pattern=[[1, 8], [0, 128]],
                                                compare_op=ALU.is_equal, fill=0.0,
                                                base=0, channel_multiplier=-1)],
           writes=[g("oneh8")], mark="gp")
    B.step("gpsimd", [lambda e: e.iota(csc[:, 0:64], pattern=[[0, 64]], base=0,
                                       channel_multiplier=1,
                                       allow_small_or_imprecise_dtypes=True)],
           writes=[g("csc")], mark="gp")
    B.step("vector", [
        lambda e: e.tensor_copy(csi_t[:, 0:64], csc[:, 0:64]),
        lambda e: e.tensor_single_scalar(csi_t[:, 64:128], csi_t[:, 0:64],
                                         cvi1[:, 0:1], op=ALU.bitwise_and),
        lambda e: e.tensor_copy(csr[:, 0:64], csi_t[:, 64:128]),
        lambda e: e.scalar_tensor_tensor(csn[:, 0:64], csr[:, 0:64], -1.0,
                                         csc[:, 0:64], op0=ALU.mult, op1=ALU.add),
        lambda e: e.tensor_scalar(csn[:, 0:64], csn[:, 0:64], 0.5, None,
                                  op0=ALU.mult),
    ], reads=[g("csc")], writes=[g("csn"), g("csi"), g("csr")], mark="ve")
    B.step("scalar", [
        lambda e: e.activation(thcol[:, 0:1], csn[:, 0:1], AF.Exp,
                               scale=cvals[:, CV_THS:CV_THS + 1]),
        lambda e: e.activation(thcol[:, 1:2], csn[:, 0:1], AF.Exp,
                               bias=cvals[:, CV_THB:CV_THB + 1],
                               scale=cvals[:, CV_THS:CV_THS + 1]),
        lambda e: e.activation(csS[:, 0:1], csn[:, 0:1], AF.Exp,
                               scale=cvals[:, CV_THS2:CV_THS2 + 1]),
        lambda e: e.activation(csS[:, 1:2], csn[:, 0:1], AF.Exp,
                               bias=cvals[:, CV_THB2:CV_THB2 + 1],
                               scale=cvals[:, CV_THS2:CV_THS2 + 1]),
    ], reads=[g("csn"), g("cvals")], writes=[g("thcol"), g("csS")], mark="se")
    B.step("vector", [
        lambda e: e.tensor_tensor(thcol[:, 0:1], thcol[:, 0:1], csS[:, 0:1],
                                  op=ALU.mult),
        lambda e: e.tensor_tensor(thcol[:, 1:2], thcol[:, 1:2], csS[:, 1:2],
                                  op=ALU.mult),
    ], reads=[g("csS")], writes=[g("thcol")], mark="ve")
    B.step("vector", [
        lambda e: e.tensor_scalar(altsign[:], csr[:, 0:1], 2.0, -1.0,
                                  op0=ALU.mult, op1=ALU.add),
    ], reads=[g("csr")], writes=[g("altsign")], mark="ve")
    B.step("gpsimd", [lambda e: e.iota(kdec[:], pattern=[[0, 8], [-128, 4]], base=T - 1,
                                       channel_multiplier=-1,
                                       allow_small_or_imprecise_dtypes=True),
                      lambda e: e.iota(kdecP[:], pattern=[[0, 8], [-128, 8]], base=TP - 1,
                                       channel_multiplier=-1,
                                       allow_small_or_imprecise_dtypes=True),
                      lambda e: e.iota(csc[:, 0:512], pattern=[[1, 512]], base=1,
                                       channel_multiplier=0,
                                       allow_small_or_imprecise_dtypes=True)],
           reads=[g("csc")], writes=[g("kdecr"), g("csc")], mark="gp")
    B.step("scalar",
           [(lambda h: (lambda e: e.activation(csS[:, 4 * h:4 * h + 4],
                                               kdec[:, 4 * h:4 * h + 4], AF.Exp,
                                               scale=cvals[:, CV_LNG2 + h:CV_LNG2 + h + 1])))(h)
            for h in range(NH)] +
           [(lambda h: (lambda e: e.activation(csC[:, 8 * h:8 * h + 8],
                                               kdecP[:, 8 * h:8 * h + 8], AF.Exp,
                                               scale=cvals[:, CV_LNG2 + h:CV_LNG2 + h + 1])))(h)
            for h in range(NH)] +
           [(lambda h: (lambda e: e.activation(kdec[:, 4 * h:4 * h + 4],
                                               kdec[:, 4 * h:4 * h + 4], AF.Exp,
                                               scale=cvals[:, CV_LNG + h:CV_LNG + h + 1])))(h)
            for h in range(NH)] +
           [(lambda h: (lambda e: e.activation(kdecP[:, 8 * h:8 * h + 8],
                                               kdecP[:, 8 * h:8 * h + 8], AF.Exp,
                                               scale=cvals[:, CV_LNG + h:CV_LNG + h + 1])))(h)
            for h in range(NH)],
           reads=[g("kdecr"), g("csc"), g("cvals")],
           writes=[g("kdec"), g("csS"), g("csC")], mark="se")
    B.step("vector", [
        lambda e: e.tensor_tensor(kdec[:], kdec[:], csS[:, 0:32], op=ALU.mult),
        lambda e: e.tensor_tensor(kdecP[:], kdecP[:], csC[:, 0:64], op=ALU.mult),
    ], reads=[g("csS"), g("csC"), g("kdec")], writes=[g("kdec")], mark="ve")
    gqb, _ = af32(512)
    B.step("vector", [lambda e: e.tensor_copy(gqb[:, 0:512], csc[:, 0:512])],
           reads=[g("csc")], writes=[g("gqb")], mark="ve")
    dma("sync", "ds", gnwc[:], d_gnw, writes=[g("gnw")])
    dma("sync", "ds", gnbc[:], d_gnb, writes=[g("gnb")])
    dma("sync", "ds", posb[:], d_posb, writes=[g("posb")])
    dma("sync", "ds", bgc[:], d_bgc, writes=[g("bgc")])
    dma("gpsimd", "dg", pfl[:], d_pflag, writes=[g("pfl")])
    for n, blk in [("q", AUG["wq"]), ("k", AUG["wk"]), ("v", AUG["wv"])]:
        dma("gpsimd", "dg", aug_all[1:2, blk:blk + HID], d_b[n], writes=[g("augw" + n)])
    dma("gpsimd", "dg", aug_all[0:1, AUG["wo"]:AUG["wo"] + HID], d_b["o"],
        writes=[g("augwo")])
    for n in WN:
        for eh in range(8):
            dma("gpsimd", "dg", d_wbf[n][eh].rearrange("dc p m -> (dc p) m"),
                d_w[n][:, eh * 256:(eh + 1) * 256],
                writes=[g("scr_" + n)])
    dma("gpsimd", "dg", d_wgTbf.rearrange("dc p m -> (dc p) m"), d_wgT,
        writes=[g("scr_wgT")])
    sqt, _ = abf(T)
    e0, _ = abf(T)
    e1, _ = abf(T)
    e2, _ = abf(T)
    e3, _ = abf(T)
    gqs, gqs_h = abf(T)
    mtmp, mtmp_h = abf(T)
    for h in range(NH):
        for ss in range(4):
            B.step("gpsimd", [(lambda sv: (lambda e: e.iota(
                csc[:, 0:T], pattern=[[1, T]], base=-128 * sv,
                channel_multiplier=-1, allow_small_or_imprecise_dtypes=True)))(ss)],
                reads=[g("csc")], writes=[g("csc")], mark="gp")
            B.step("scalar", [
                (lambda hv: (lambda e: e.activation(
                    csn[:, 0:T], csc[:, 0:T], AF.Exp,
                    scale=cvals[:, CV_LNG + hv:CV_LNG + hv + 1])))(h),
                (lambda hv: (lambda e: e.activation(
                    csS[:, 0:T], csc[:, 0:T], AF.Exp,
                    scale=cvals[:, CV_LNG2 + hv:CV_LNG2 + hv + 1])))(h),
            ], reads=[g("csc"), g("cvals")], writes=[g("csn"), g("csS")],
                mark="se")
            B.step("vector", [lambda e: e.tensor_tensor(
                csn[:, 0:T], csn[:, 0:T], csS[:, 0:T], op=ALU.mult)],
                reads=[g("csS")], writes=[g("csn")], mark="ve")
            B.step("gpsimd", [(lambda sv: (lambda e: e.affine_select(
                mtmp, csn[:, 0:T], pattern=[[1, T]],
                compare_op=ALU.is_ge, fill=0.0, base=-128 * sv,
                channel_multiplier=-1)))(ss)],
                reads=[g("csn")], writes=[g("mtmp")], mark="gp")
            dma("sync", "ds", d_mask[h, ss], mtmp, reads=[g("mtmp")],
                writes=[g("d_mask")])


    def stats_mms(xb, xbbuf, W, bs, bq):
        ntg = W // 512
        for dc in range(16):
            sq_aps = [sqt[:, 0:512], e0[:, 0:512]]
            B.step("scalar", [(lambda d, t: (lambda e: e.activation(
                sq_aps[t], xb[:, d * W + t * 512:d * W + (t + 1) * 512],
                AF.Square)))(dc, tg) for tg in range(ntg)],
                reads=[xbbuf], writes=[g("sqt"), g("e0")], mark="se")
            ems = []
            for tg in range(ntg):
                ems.append((lambda d, t: (lambda e: e.matmul(
                    bk(bs + t, parts=1), ones_colbf[:],
                    xb[:, d * W + t * 512:d * W + (t + 1) * 512],
                    start=(d == 0), stop=(d == 15))))(dc, tg))
                ems.append((lambda d, t, sa=sq_aps[tg]: (lambda e: e.matmul(
                    bk(bq + t, parts=1), ones_colbf[:], sa,
                    start=(d == 0), stop=(d == 15))))(dc, tg))
            B.step("tensor", ems, reads=[xbbuf, g("sqt"), g("e0"), g("const1")],
                   writes=[pb[bs + t] for t in range(ntg)] +
                          [pb[bq + t] for t in range(ntg)], mark="te")

    def strip_chain(W, bs, bq, flagged=False, want_aug=True, div=HID,
                    r2_mul=False, r2_save=False, want_rcols=False):
        for tg in range(W // 512):
            c0 = tg * 512
            B.step("scalar", [
                (lambda t: (lambda e: e.copy(s_sum[0:1, :], bk(bs + t, parts=1))))(tg),
                (lambda t: (lambda e: e.copy(s_sq[0:1, :], bk(bq + t, parts=1))))(tg),
            ], reads=[pb[bs + tg], pb[bq + tg]], writes=[g("strip")], mark="se")
            B.step("vector", [lambda e: e.tensor_scalar(
                s_mean[0:1, :], s_sum[0:1, :], 1.0 / div, None, op0=ALU.mult)],
                reads=[g("strip")], writes=[g("strip")], mark="ve")
            B.step("scalar", [lambda e: e.activation(s_tmp[0:1, :], s_mean[0:1, :],
                                                     AF.Square)],
                   reads=[g("strip")], writes=[g("strip")], mark="se")
            ems = [lambda e: e.scalar_tensor_tensor(
                s_sq[0:1, :], s_sq[0:1, :], 1.0 / div, s_tmp[0:1, :],
                op0=ALU.mult, op1=ALU.subtract)]
            if r2_mul:
                ems.append((lambda c: (lambda e: e.tensor_tensor(
                    s_t3[0:1, :], r2row[0:1, c:c + 512], r2row[0:1, c:c + 512],
                    op=ALU.mult)))(c0))
                ems.append(lambda e: e.tensor_tensor(
                    s_sq[0:1, :], s_sq[0:1, :], s_t3[0:1, :], op=ALU.mult))
            B.step("vector", ems, reads=[g("strip"), g("r2row")],
                   writes=[g("strip")], mark="ve")
            B.step("scalar", [lambda e: e.activation(
                s_tmp[0:1, :], s_sq[0:1, :], AF.Sqrt,
                bias=cvals[0:1, CV_EPS:CV_EPS + 1])],
                reads=[g("strip"), g("cvals")], writes=[g("strip")], mark="se")
            B.step("vector", [lambda e: e.reciprocal(s_sum[0:1, :],
                                                     s_tmp[0:1, :])],
                   reads=[g("strip")], writes=[g("strip")], mark="ve")
            B.step("scalar", [lambda e: e.activation(s_r[0:1, :], s_sum[0:1, :],
                                                     AF.Identity, bias=0.0)],
                   reads=[g("strip")], writes=[g("strip")], mark="se")
            ems = []
            if want_aug:
                ems.append((lambda c: (lambda e: e.tensor_scalar(
                    mrows[0:1, c:c + 512], s_mean[0:1, :], -1.0, None,
                    op0=ALU.mult)))(c0))
                if flagged:
                    ems.append((lambda c: (lambda e: e.tensor_tensor(
                        s_t3[0:1, :], s_tmp[0:1, :], pfl[0:1, c:c + 512],
                        op=ALU.mult)))(c0))
                    ems.append((lambda c: (lambda e: e.tensor_tensor(
                        mrows[0:1, c:c + 512], mrows[0:1, c:c + 512],
                        pfl[0:1, c:c + 512], op=ALU.mult)))(c0))
                else:
                    ems.append(lambda e: e.tensor_copy(s_t3[0:1, :], s_tmp[0:1, :]))
            ems.append((lambda c: (lambda e: e.tensor_copy(
                bcm[0:1, c:c + 512], s_r[0:1, :])))(c0))
            ems.append((lambda c: (lambda e: e.tensor_scalar(
                bcm2[0:1, c:c + 512], s_r[0:1, :], 1.0 / 16.0, None,
                op0=ALU.mult)))(c0))
            if r2_mul:
                ems.append((lambda c: (lambda e: e.tensor_tensor(
                    s_r[0:1, :], s_r[0:1, :], r2row[0:1, c:c + 512],
                    op=ALU.mult)))(c0))
                ems.append((lambda c: (lambda e: e.scalar_tensor_tensor(
                    s_tmp[0:1, :], s_mean[0:1, :], -1.0, s_r[0:1, :],
                    op0=ALU.mult, op1=ALU.mult)))(c0))
                ems.append((lambda c: (lambda e: e.tensor_copy(
                    bcm[0:1, c:c + 512], s_r[0:1, :])))(c0))
                ems.append((lambda c: (lambda e: e.tensor_copy(
                    bcm2[0:1, c:c + 512], s_tmp[0:1, :])))(c0))
            B.step("vector", ems,
                   reads=[g("strip"), g("pfl"), g("r2sq"), g("r2row")],
                   writes=[g("strip"), g("mrows"), g("bcm"), g("bcm2")], mark="ve")
            if want_aug:
                dma("gpsimd", "dg", mrows[1:2, c0:c0 + 512], s_t3[0:1, :],
                    reads=[g("strip")], writes=[g("mrows")])
            if r2_save:
                B.step("vector", [(lambda c: (lambda e: e.tensor_copy(
                    r2row[0:1, c:c + 512], s_r[0:1, :])))(c0)],
                    reads=[g("strip")], writes=[g("r2row")], mark="ve")
            if want_rcols:
                for ts in range(4):
                    b = 4 + (ts % 3)
                    B.step("tensor", [(lambda t, bb: (lambda e: e.transpose(
                        bk(bb, width=1), s_r[0:1, t * 128:(t + 1) * 128],
                        identF[0:1, 0:1])))(ts, b)],
                        reads=[g("strip"), g("ident")], writes=[pb[b]], mark="te")
                    B.step("scalar", [(lambda t, bb, d0=tg * 4: (lambda e: e.copy(
                        rcols[:, d0 + t:d0 + t + 1], bk(bb, width=1))))(ts, b)],
                        reads=[pb[b]], writes=[g("rcols")], mark="se")

    def bcast(row_ap, row_buf, W, bankidx, out_ap, out_buf, parts_k=1, st_ap=None,
              st_buf=None):
        stat = st_ap if st_ap is not None else ones_bc[:]
        ems = []
        for tg in range(W // 512):
            ems.append((lambda t, sa=stat: (lambda e: e.matmul(
                bk(bankidx + t), sa, row_ap[0:parts_k, t * 512:(t + 1) * 512],
                start=True, stop=True)))(tg))
        B.step("tensor", ems,
               reads=[row_buf, g("const1")] + ([st_buf] if st_buf else []),
               writes=[pb[bankidx + t] for t in range(W // 512)], mark="te")
        B.step("scalar", [(lambda t: (lambda e: e.copy(
            out_ap[:, t * 512:(t + 1) * 512], bk(bankidx + t))))(tg)
            for tg in range(W // 512)],
            reads=[pb[bankidx + t] for t in range(W // 512)],
            writes=[out_buf], mark="se")

    def cs_tiles(dst, dstbuf, W, iota_base, use_posb, folds):
        for hf in range(2):
            for sub in range(W // 512):
                o = sub * 512
                B.step("gpsimd", [(lambda ib: (lambda e: e.iota(
                    csc[:, 0:512], pattern=[[1, 512]], base=ib, channel_multiplier=0,
                    allow_small_or_imprecise_dtypes=True)))(iota_base + o)],
                    reads=[g("csc")], writes=[g("csc")], mark="gp")
                ems = []
                if use_posb:
                    ems.append(lambda e: e.tensor_scalar(csc[:, 0:512], csc[:, 0:512],
                                                         posb[:, 0:1], None,
                                                         op0=ALU.add))
                ems += [
                    (lambda h2: (lambda e: e.tensor_scalar(
                        csc[:, 0:512], csc[:, 0:512], thcol[:, h2:h2 + 1], None,
                        op0=ALU.mult)))(hf),
                    lambda e: e.tensor_scalar(csn[:, 0:512], csc[:, 0:512], INV_2PI,
                                              None, op0=ALU.mult),
                    lambda e: e.tensor_copy(csi_t[:, 0:512], csn[:, 0:512]),
                    lambda e: e.tensor_copy(csn[:, 0:512], csi_t[:, 0:512]),
                    lambda e: e.scalar_tensor_tensor(csr[:, 0:512], csn[:, 0:512], -C1,
                                                     csc[:, 0:512], op0=ALU.mult,
                                                     op1=ALU.add),
                    lambda e: e.scalar_tensor_tensor(csr[:, 0:512], csn[:, 0:512], -C2,
                                                     csr[:, 0:512], op0=ALU.mult,
                                                     op1=ALU.add),
                    lambda e: e.scalar_tensor_tensor(csr[:, 0:512], csn[:, 0:512], -C3,
                                                     csr[:, 0:512], op0=ALU.mult,
                                                     op1=ALU.add),
                    lambda e: e.tensor_single_scalar(csc[:, 0:512], csr[:, 0:512], PI,
                                                     op=ALU.is_gt),
                    lambda e: e.scalar_tensor_tensor(csr[:, 0:512], csc[:, 0:512], N2PI,
                                                     csr[:, 0:512], op0=ALU.mult,
                                                     op1=ALU.add),
                    lambda e: e.tensor_single_scalar(csc[:, 0:512], csr[:, 0:512], -PI,
                                                     op=ALU.is_lt),
                    lambda e: e.scalar_tensor_tensor(csr[:, 0:512], csc[:, 0:512], -N2PI,
                                                     csr[:, 0:512], op0=ALU.mult,
                                                     op1=ALU.add),
                ]
                B.step("vector", ems, reads=[g("csc"), g("posb"), g("thcol")],
                       writes=[g("csc"), g("csn"), g("csr"), g("csi")], mark="ve")
                B.step("scalar", [lambda e: e.activation(csS[:, 0:512], csr[:, 0:512],
                                                         AF.Sin)],
                       reads=[g("csr")], writes=[g("csS")], mark="se")
                B.step("vector", [
                    lambda e: e.tensor_scalar(csn[:, 0:512], csr[:, 0:512], HALF_PI,
                                              None, op0=ALU.add),
                    lambda e: e.tensor_single_scalar(csc[:, 0:512], csn[:, 0:512], PI,
                                                     op=ALU.is_gt),
                    lambda e: e.scalar_tensor_tensor(csn[:, 0:512], csc[:, 0:512], N2PI,
                                                     csn[:, 0:512], op0=ALU.mult,
                                                     op1=ALU.add),
                    lambda e: e.tensor_scalar(csS[:, 0:512], csS[:, 0:512],
                                              altsign[:, 0:1], None, op0=ALU.mult),
                ], reads=[g("csr"), g("altsign")],
                    writes=[g("csn"), g("csc"), g("csS")], mark="ve")
                B.step("scalar", [lambda e: e.activation(csC[:, 0:512], csn[:, 0:512],
                                                         AF.Sin)],
                       reads=[g("csn")], writes=[g("csC")], mark="se")
                ems = []
                for fi, (tile, tilebuf) in enumerate(folds):
                    bb = fi * 4 * W + hf * 2 * W
                    ems.append((lambda b2, ta, o2=o: (lambda e: e.tensor_tensor(
                        dst[:, b2 + o2:b2 + o2 + 512], csC[:, 0:512],
                        ta[:, o2:o2 + 512], op=ALU.mult)))(bb, tile))
                    ems.append((lambda b2, ta, o2=o: (lambda e: e.tensor_tensor(
                        dst[:, b2 + W + o2:b2 + W + o2 + 512], csS[:, 0:512],
                        ta[:, o2:o2 + 512], op=ALU.mult)))(bb, tile))
                B.step("vector", ems,
                       reads=[g("csC"), g("csS")] + [fb for _, fb in folds],
                       writes=[dstbuf], mark="ve")

    def stream_w8(n, eh, slot_ap, slot_buf):
        return dma("sync", "ds", slot_ap.rearrange("p (dc m) -> p dc m", m=256),
                   d_wbf[n][eh].rearrange("dc p m -> p dc m"),
                   reads=[g("scr_" + n)], writes=[slot_buf])

    def wsum(n, eh, slot_ap, slot_buf, bankidx):
        ems = []
        for dc in range(16):
            ems.append((lambda d: (lambda e: e.matmul(
                bk(bankidx, parts=1, width=256), ones_colbf[:],
                slot_ap[:, d * 256:(d + 1) * 256],
                start=(d == 0), stop=(d == 15))))(dc))
        B.step("tensor", ems, reads=[slot_buf, g("const1")], writes=[pb[bankidx]],
               mark="te")
        B.step("scalar", [lambda e: e.copy(augw[0:1, 0:256],
                                           bk(bankidx, parts=1, width=256))],
               reads=[pb[bankidx]], writes=[g("augw")], mark="se")
        B.step("vector", [(lambda e2, nn: (lambda e: e.tensor_copy(
            aug_all[0:1, AUG[nn] + e2 * 256:AUG[nn] + (e2 + 1) * 256],
            augw[0:1, 0:256])))(eh, n)],
            reads=[g("augw")], writes=[g("augw" + n[1:])], mark="ve")

    def shift_epi(src_bank, Cap, Sap, csbuf, dst_ap, dstbuf):
        B.step("scalar", [lambda e: e.copy(e0[:, 0:T], bk(src_bank))],
               reads=[pb[src_bank]], writes=[g("e0")], mark="se")
        B.step("vector", [
            lambda e: e.stream_shuffle(e1[:, 0:T], e0[:, 0:T], mask=SHUF_MASK),
            (lambda ca: lambda e: e.tensor_tensor(e2[:, 0:T], e0[:, 0:T], ca,
                                                  op=ALU.mult))(Cap),
        ], reads=[g("e0"), csbuf], writes=[g("e1"), g("e2")], mark="ve")
        B.step("vector", [
            (lambda sa: lambda e: e.tensor_tensor(e3[:, 0:T], e1[:, 0:T], sa,
                                                  op=ALU.mult))(Sap),
            (lambda da: lambda e: e.tensor_tensor(da, e2[:, 0:T], e3[:, 0:T],
                                                  op=ALU.add))(dst_ap),
        ], reads=[g("e1"), g("e2"), csbuf], writes=[g("e3"), dstbuf], mark="ve")

    w8s = [abf(4096) for _ in range(2)]
    xbfP, xbfP_h = abf(16 * TP)
    cskP, cskP_h = abf(4 * TP)
    rkbP, rkbP_h = abf(TP)
    ktP, ktP_h = abf(2 * TP)
    ktmP, ktmP_h = abf(8 * 256)
    vtmP, vtmP_h = abf(8 * 256)

    for pc in range(NPC if stop_after >= 1 else 0):
        cols = slice(pc * TP, (pc + 1) * TP)
        for q4 in range(4):
            dma("gpsimd", "dg",
                xbfP.rearrange("p (dc w) -> p dc w", w=TP)[:, q4 * 4:(q4 + 1) * 4],
                d_xp[:, cols].rearrange("(dc p) w -> p dc w", p=128)[:, q4 * 4:(q4 + 1) * 4],
                reads=[g("xbfP")], writes=[g("xbfP")])
        if PH2_SUB < 1:
            continue
        stats_mms(xbfP, g("xbfP"), TP, 0, 2)
        strip_chain(TP, 0, 2, flagged=True, want_rcols=True)
        if PH2_SUB < 2:
            continue
        bcast(bcm, g("bcm"), TP, 4, rkbP, g("rkbP"))
        cs_tiles(cskP, g("cskP"), TP, pc * TP, False, [(rkbP, g("rkbP"))])
        if PH2_SUB < 3:
            continue
        for h in range(NH if PH2_SUB >= 3 else 0):
            slv = w8s[0]
            slk = w8s[1]
            stream_w8("wv", h, slv[0], g("w8_0"))
            if pc == 0:
                wsum("wv", h, slv[0], g("w8_0"), 6)
            for ts in range(8):
                b = ts % 4
                ems = []
                for dc in range(16):
                    ems.append((lambda d, t, oap=bk(b, width=256): (lambda e: e.matmul(
                        oap,
                        xbfP[:, d * TP + t * 128:d * TP + (t + 1) * 128],
                        slv[0][:, d * 256:(d + 1) * 256],
                        start=(d == 0), stop=False)))(dc, ts))
                ems.append((lambda t, hh, oap=bk(b, width=256): (lambda e: e.matmul(
                    oap, mrows[0:2, t * 128:(t + 1) * 128],
                    aug_all[0:2, AUG["wv"] + hh * 256:AUG["wv"] + (hh + 1) * 256],
                    start=False, stop=True)))(ts, h))
                B.step("tensor", ems,
                       reads=[g("xbfP"), g("w8_0"), g("mrows"), g("augwv")],
                       writes=[pb[b]], mark="te")
                B.step("scalar", [(lambda t, bb: (lambda e: e.activation(
                    vtmP[:, t * 256:(t + 1) * 256], bk(bb, width=256),
                    AF.Identity, bias=0.0, scale=rcols[:, t:t + 1])))(ts, b)],
                    reads=[pb[b], g("rcols")], writes=[g("vtmP")], mark="se")
            if PH2_SUB < 4:
                continue
            stream_w8("wk", h, slk[0], g("w8_1"))
            if pc == 0:
                wsum("wk", h, slk[0], g("w8_1"), 6)
            for m2 in range(2):
                bp = 3 if m2 == 0 else 5
                for tg in range(2):
                    ems = []
                    for dc in range(16):
                        ems.append((lambda d, t, m, oap=bk(bp + tg): (lambda e: e.matmul(
                            oap,
                            slk[0][:, d * 256 + m * 128:d * 256 + (m + 1) * 128],
                            xbfP[:, d * TP + t * 512:d * TP + (t + 1) * 512],
                            start=(d == 0), stop=False)))(dc, tg, m2))
                    ems.append((lambda t, m, hh, oap=bk(bp + tg): (lambda e: e.matmul(
                        oap,
                        aug_all[0:2, AUG["wk"] + hh * 256 + m * 128:
                                AUG["wk"] + hh * 256 + (m + 1) * 128],
                        mrows[0:2, t * 512:(t + 1) * 512],
                        start=False, stop=True)))(tg, m2, h))
                    B.step("tensor", ems,
                           reads=[g("xbfP"), g("w8_1"), g("mrows"), g("augwk")],
                           writes=[pb[bp + tg]], mark="te")
                    co = m2 * 2 * TP + tg * 512
                    shift_epi(bp + tg, cskP[:, co:co + 512],
                              cskP[:, co + TP:co + TP + 512], g("cskP"),
                              ktP[:, m2 * TP + tg * 512:m2 * TP + (tg + 1) * 512],
                              g("ktP"))
            if PH2_SUB < 5:
                continue
            for m2 in range(2):
                for ssb in range(8):
                    B.step("tensor", [(lambda m, s2: (lambda e: e.transpose(
                        psb[:, 0:128],
                        ktP[:, m * TP + s2 * 128:m * TP + (s2 + 1) * 128],
                        ident[:])))(m2, ssb)],
                        reads=[g("ktP"), g("ident")], writes=[g("psb")], mark="te")
                    B.step("scalar", [(lambda m, s2, hh: (lambda e: e.activation(
                        ktmP[:, s2 * 256 + m * 128:s2 * 256 + (m + 1) * 128],
                        psb[:, 0:128], AF.Identity, bias=0.0,
                        scale=kdecP[:, 8 * hh + s2:8 * hh + s2 + 1])))(m2, ssb, h)],
                        reads=[g("psb"), g("kdec")], writes=[g("ktmP")], mark="se")
            for dk2 in range(2):
                b = dk2
                ems = []
                for ssb in range(8):
                    ems.append((lambda s2, d2, oap=bk(b, width=256): (lambda e: e.matmul(
                        oap,
                        ktmP[:, s2 * 256 + d2 * 128:s2 * 256 + (d2 + 1) * 128],
                        vtmP[:, s2 * 256:(s2 + 1) * 256],
                        start=(s2 == 0), stop=(s2 == 7))))(ssb, dk2))
                B.step("tensor", ems, reads=[g("ktmP"), g("vtmP")],
                       writes=[pb[b]], mark="te")
                B.step("vector", [(lambda d2, bb, hh: (lambda e: e.scalar_tensor_tensor(
                    S_fp[:, hh * 512 + d2 * 256:hh * 512 + (d2 + 1) * 256],
                    S_fp[:, hh * 512 + d2 * 256:hh * 512 + (d2 + 1) * 256],
                    GTP[hh], bk(bb, width=256), op0=ALU.mult, op1=ALU.add)))(dk2, b, h)],
                    reads=[pb[b]], writes=[g("S_fp")], mark="ve")

    tap("ktmP", ktmP[:, 0:2048], [g("ktmP")])
    tap("vtmP", vtmP[:, 0:2048], [g("vtmP")])
    tap("ktP", ktP[:, 0:2048], [g("ktP")])
    barrier()
    for hh in (xbfP_h, cskP_h, rkbP_h, ktP_h, ktmP_h, vtmP_h):
        arb.freed(hh)
    tap("S_prior", S_fp[:], [g("S_fp")])

    xbf, xbf_h = abf(16 * T)
    csq, csq_h = abf(8 * T)
    rqb, rqb_h = abf(T)
    rkb, rkb_h = abf(T)
    r1bc, r1bc_h = af32(T)
    qt, qt_h = abf(2 * T)
    ktl, ktl_h = abf(2 * T)
    ktm, ktm_h = abf(4 * 256)
    vtm, vtm_h = abf(4 * 256)
    qhb, qhb_h = abf(2 * T)
    simb, simb_h = abf(4 * T)
    mskb, mskb_h = abf(4 * T)
    retb, retb_h = abf(16 * T)
    sgb, sgb_h = abf(16 * T)
    aob = sgb
    bct0, bct0_h = abf(T)
    bct1, bct1_h = abf(T)
    x32 = [af32(T) for _ in range(2)]
    yt32 = [af32(T) for _ in range(2)]

    def CSQ(m2, kind):
        return csq[:, (m2 * 2 + kind) * T:(m2 * 2 + kind + 1) * T]

    def CSK(m2, kind):
        return csq[:, 4 * T + (m2 * 2 + kind) * T:4 * T + (m2 * 2 + kind + 1) * T]

    n_ch = 0 if stop_after < 2 else (1 if stop_after == 2 else NCH)
    for c in range(n_ch):
        cols = slice(c * T, (c + 1) * T)
        for q4 in range(4):
            dma("gpsimd", "dg",
                xbf.rearrange("p (dc w) -> p dc w", w=T)[:, q4 * 4:(q4 + 1) * 4],
                d_x[:, cols].rearrange("(dc p) w -> p dc w", p=128)[:, q4 * 4:(q4 + 1) * 4],
                reads=[g("xbf")], writes=[g("xbf")])
        stats_mms(xbf, g("xbf"), T, 0, 1)
        strip_chain(T, 0, 1, want_rcols=True)
        bcast(bcm2, g("bcm2"), T, 4, rqb, g("rqb"))
        bcast(bcm, g("bcm"), T, 5, rkb, g("rkb"))
        B.step("tensor", [lambda e: e.matmul(bk(6), ones_bc[:], bcm[0:1, 0:T],
                                             start=True, stop=True)],
               reads=[g("bcm"), g("const1")], writes=[pb[6]], mark="te")
        B.step("scalar", [lambda e: e.copy(r1bc, bk(6))],
               reads=[pb[6]], writes=[g("r1bc")], mark="se")
        cs_tiles(csq, g("csq"), T, c * T, True, [(rqb, g("rqb")), (rkb, g("rkb"))])
        if c == 0:
            tap("csq", csq[:, 0:8 * T], [g("csq")])
            tap("thcol", thcol[:], [g("thcol")])
            tap("altsign", altsign[:], [g("altsign")])
            tap("csS", csS[:, 0:512], [g("csS")])
            tap("csC", csC[:, 0:512], [g("csC")])
            tap("rqb", rqb[:, 0:T], [g("rqb")])
            tap("rkb", rkb[:, 0:T], [g("rkb")])

        for h in range(NH):
            dma("sync", "ds",
                mskb[:, 0:4 * T].rearrange("p (s t) -> p s t", s=4),
                d_mask[h].rearrange("s p t -> p s t"),
                reads=[g("mskslot"), g("d_mask")], writes=[g("mskslot")])
            B.step("vector", [(lambda hh: (lambda e: e.tensor_copy(
                S_bf[:], S_fp[:, hh * 512:(hh + 1) * 512])))(h)],
                reads=[g("S_fp")], writes=[g("S_bf")], mark="ve")
            stream_w8("wv", h, w8s[0][0], g("w8_0"))
            for ts in range(4):
                b = ts
                ems = []
                for dc in range(16):
                    ems.append((lambda d, t, oap=bk(b, width=256): (lambda e: e.matmul(
                        oap,
                        xbf[:, d * T + t * 128:d * T + (t + 1) * 128],
                        w8s[0][0][:, d * 256:(d + 1) * 256],
                        start=(d == 0), stop=False)))(dc, ts))
                ems.append((lambda t, hh, oap=bk(b, width=256): (lambda e: e.matmul(
                    oap, mrows[0:2, t * 128:(t + 1) * 128],
                    aug_all[0:2, AUG["wv"] + hh * 256:AUG["wv"] + (hh + 1) * 256],
                    start=False, stop=True)))(ts, h))
                B.step("tensor", ems,
                       reads=[g("xbf"), g("w8_0"), g("mrows"), g("augwv")],
                       writes=[pb[b]], mark="te")
                B.step("scalar", [(lambda t, bb: (lambda e: e.activation(
                    vtm[:, t * 256:(t + 1) * 256], bk(bb, width=256),
                    AF.Identity, bias=0.0, scale=rcols[:, t:t + 1])))(ts, b)],
                    reads=[pb[b], g("rcols")], writes=[g("vtm")], mark="se")
            stream_w8("wq", h, w8s[1][0], g("w8_1"))
            if c == 0:
                wsum("wq", h, w8s[1][0], g("w8_1"), 4)
            for m2 in range(2):
                b = 3 + m2
                ems = []
                for dc in range(16):
                    ems.append((lambda d, m, oap=bk(b): (lambda e: e.matmul(
                        oap, w8s[1][0][:, d * 256 + m * 128:d * 256 + (m + 1) * 128],
                        xbf[:, d * T:(d + 1) * T],
                        start=(d == 0), stop=False)))(dc, m2))
                ems.append((lambda m, hh, oap=bk(b): (lambda e: e.matmul(
                    oap, aug_all[0:2, AUG["wq"] + hh * 256 + m * 128:
                                 AUG["wq"] + hh * 256 + (m + 1) * 128],
                    mrows[0:2, 0:T], start=False, stop=True)))(m2, h))
                B.step("tensor", ems,
                       reads=[g("xbf"), g("w8_1"), g("mrows"), g("augwq")],
                       writes=[pb[b]], mark="te")
                shift_epi(b, CSQ(m2, 0), CSQ(m2, 1), g("csq"),
                          qt[:, m2 * T:(m2 + 1) * T], g("qt"))
            stream_w8("wk", h, w8s[0][0], g("w8_0"))
            for m2 in range(2):
                b = 0 + m2
                ems = []
                for dc in range(16):
                    ems.append((lambda d, m, oap=bk(b): (lambda e: e.matmul(
                        oap, w8s[0][0][:, d * 256 + m * 128:d * 256 + (m + 1) * 128],
                        xbf[:, d * T:(d + 1) * T],
                        start=(d == 0), stop=False)))(dc, m2))
                ems.append((lambda m, hh, oap=bk(b): (lambda e: e.matmul(
                    oap, aug_all[0:2, AUG["wk"] + hh * 256 + m * 128:
                                 AUG["wk"] + hh * 256 + (m + 1) * 128],
                    mrows[0:2, 0:T], start=False, stop=True)))(m2, h))
                B.step("tensor", ems,
                       reads=[g("xbf"), g("w8_0"), g("mrows"), g("augwk")],
                       writes=[pb[b]], mark="te")
                shift_epi(b, CSK(m2, 0), CSK(m2, 1), g("csq"),
                          ktl[:, m2 * T:(m2 + 1) * T], g("ktl"))
            for m2 in range(2):
                for ssb in range(4):
                    B.step("tensor", [(lambda m, s2: (lambda e: e.transpose(
                        psb[:, 0:128],
                        ktl[:, m * T + s2 * 128:m * T + (s2 + 1) * 128],
                        ident[:])))(m2, ssb)],
                        reads=[g("ktl"), g("ident")], writes=[g("psb")], mark="te")
                    B.step("scalar", [(lambda m, s2, hh: (lambda e: e.activation(
                        ktm[:, s2 * 256 + m * 128:s2 * 256 + (m + 1) * 128],
                        psb[:, 0:128], AF.Identity, bias=0.0,
                        scale=kdec[:, 4 * hh + s2:4 * hh + s2 + 1])))(m2, ssb, h)],
                        reads=[g("psb"), g("kdec")], writes=[g("ktm")], mark="se")
            B.step("scalar", [(lambda hh: (lambda e: e.activation(
                gqs, gqb[:, 0:512], AF.Exp,
                scale=cvals[:, CV_LNG + hh:CV_LNG + hh + 1])))(h)],
                reads=[g("gqb"), g("cvals")], writes=[g("gqs")], mark="se")
            B.step("vector", [
                (lambda m: (lambda e: e.tensor_tensor(
                    qhb[:, m * T:(m + 1) * T], qt[:, m * T:(m + 1) * T],
                    gqs, op=ALU.mult)))(m2)
                for m2 in range(2)],
                reads=[g("qt"), g("gqs")], writes=[g("qhb")], mark="ve")
            for ssb in range(4):
                b = 3 + (ssb % 2)
                ems = []
                for dh2 in range(2):
                    ems.append((lambda s2, d2, oap=bk(b): (lambda e: e.matmul(
                        oap, ktl[:, d2 * T + s2 * 128:d2 * T + (s2 + 1) * 128],
                        qt[:, d2 * T:(d2 + 1) * T],
                        start=(d2 == 0), stop=(d2 == 1))))(ssb, dh2))
                B.step("tensor", ems, reads=[g("ktl"), g("qt")], writes=[pb[b]],
                       mark="te")
                B.step("vector", [(lambda s2, bb: (lambda e: e.tensor_tensor(
                    simb[:, s2 * T:(s2 + 1) * T], bk(bb),
                    mskb[:, s2 * T:(s2 + 1) * T],
                    op=ALU.mult)))(ssb, b)],
                    reads=[pb[b], g("mskslot")], writes=[g("simb")], mark="ve")
            for dvh in range(2):
                b = 0 if dvh == 0 else 2
                ems = []
                for ssb in range(4):
                    ems.append((lambda s2, dv, oap=bk(b): (lambda e: e.matmul(
                        oap, vtm[:, s2 * 256 + dv * 128:s2 * 256 + (dv + 1) * 128],
                        simb[:, s2 * T:(s2 + 1) * T],
                        start=(s2 == 0), stop=False)))(ssb, dvh))
                for dh2 in range(2):
                    ems.append((lambda d2, dv, oap=bk(b): (lambda e: e.matmul(
                        oap, S_bf[:, d2 * 256 + dv * 128:d2 * 256 + (dv + 1) * 128],
                        qhb[:, d2 * T:(d2 + 1) * T],
                        start=False, stop=(d2 == 1))))(dh2, dvh))
                B.step("tensor", ems, reads=[g("vtm"), g("simb"), g("S_bf"), g("qhb")],
                       writes=[pb[b]], mark="te")
            for dvh in range(2):
                b = 0 if dvh == 0 else 2
                B.step("scalar", [
                    (lambda bb, hh, dv: (lambda e: e.copy(
                        retb[:, (2 * hh + dv) * T:(2 * hh + dv + 1) * T], bk(bb))))(b, h, dvh),
                    (lambda bb: (lambda e: e.activation(
                        sqt[:, 0:T], bk(bb), AF.Square)))(b),
                ], reads=[pb[b]], writes=[g("retb"), g("sqt")], mark="se")
                B.step("tensor", [
                    (lambda hh, dv: (lambda e: e.matmul(
                        bk(5, parts=8), oneh8[:, hh * 8:hh * 8 + 8],
                        retb[:, (2 * hh + dv) * T:(2 * hh + dv + 1) * T],
                        start=(hh == 0 and dv == 0), stop=(hh == 7 and dv == 1))))(h, dvh),
                    (lambda hh, dv: (lambda e: e.matmul(
                        bk(6, parts=8), oneh8[:, hh * 8:hh * 8 + 8], sqt[:, 0:T],
                        start=(hh == 0 and dv == 0), stop=(hh == 7 and dv == 1))))(h, dvh),
                ], reads=[g("retb"), g("sqt"), g("oneh8")],
                    writes=[pb[5], pb[6]], mark="te")
            for dk2 in range(2):
                b = 1 if dk2 == 0 else 3
                ems = []
                for ssb in range(4):
                    ems.append((lambda s2, d2, oap=bk(b, width=256): (lambda e: e.matmul(
                        oap,
                        ktm[:, s2 * 256 + d2 * 128:s2 * 256 + (d2 + 1) * 128],
                        vtm[:, s2 * 256:(s2 + 1) * 256],
                        start=(s2 == 0), stop=(s2 == 3))))(ssb, dk2))
                B.step("tensor", ems, reads=[g("ktm"), g("vtm")], writes=[pb[b]],
                       mark="te")
                B.step("vector", [(lambda d2, bb, hh: (lambda e: e.scalar_tensor_tensor(
                    S_fp[:, hh * 512 + d2 * 256:hh * 512 + (d2 + 1) * 256],
                    S_fp[:, hh * 512 + d2 * 256:hh * 512 + (d2 + 1) * 256],
                    GT[hh], bk(bb, width=256), op0=ALU.mult, op1=ALU.add)))(dk2, b, h)],
                    reads=[pb[b]], writes=[g("S_fp")], mark="ve")
        if c == 0:
            tap("qt0", qt[:, 0:2 * T], [g("qt")])
            tap("vtm0", vtm[:, 0:1024], [g("vtm")])
            tap("retb", retb[:, 0:16 * T], [g("retb")])

        for eh in range(NH):
            stream_w8("wg", eh, w8s[0][0], g("w8_0"))
            if c == 0:
                wsum("wg", eh, w8s[0][0], g("w8_0"), 4)
            for m2 in range(2):
                m = 2 * eh + m2
                b = m2
                ems = []
                for dc in range(16):
                    ems.append((lambda d, mm2, oap=bk(b): (lambda e: e.matmul(
                        oap, w8s[0][0][:, d * 256 + mm2 * 128:d * 256 + (mm2 + 1) * 128],
                        xbf[:, d * T:(d + 1) * T],
                        start=(d == 0), stop=False)))(dc, m2))
                ems.append((lambda mm, oap=bk(b): (lambda e: e.matmul(
                    oap, aug_all[0:1, AUG["wg"] + mm * 128:AUG["wg"] + (mm + 1) * 128],
                    mrows[0:1, 0:T], start=False, stop=True)))(m))
                B.step("tensor", ems,
                       reads=[g("xbf"), g("w8_0"), g("mrows"), g("augwg")],
                       writes=[pb[b]], mark="te")
                B.step("vector", [(lambda bb: (lambda e: e.tensor_tensor(
                    csr[:, 0:T], bk(bb), r1bc, op=ALU.mult)))(b)],
                    reads=[pb[b], g("r1bc")], writes=[g("csr")], mark="ve")
                B.step("scalar", [(lambda mm: (lambda e: e.activation(
                    sgb[:, mm * T:(mm + 1) * T], csr[:, 0:T], AF.Silu,
                    bias=bgc[:, mm:mm + 1], scale=1.0)))(m)],
                    reads=[g("csr"), g("bgc")], writes=[g("sgb")], mark="se")

        GA = grows[0:8, 0:T]
        GB = grows[0:8, T:2 * T]
        GC = grows[0:8, 2 * T:3 * T]
        B.step("scalar", [
            lambda e: e.activation(GA, bk(5, parts=8), AF.Identity,
                                   bias=0.0, scale=cvals[0:8, CV_256:CV_256 + 1]),
            lambda e: e.activation(GB, bk(6, parts=8), AF.Identity,
                                   bias=0.0, scale=cvals[0:8, CV_256:CV_256 + 1]),
        ], reads=[pb[5], pb[6], g("cvals")], writes=[g("grows")], mark="se")
        B.step("scalar", [lambda e: e.activation(GC, GA, AF.Square)],
               reads=[g("grows")], writes=[g("grows")], mark="se")
        B.step("vector", [lambda e: e.tensor_tensor(GB, GB, GC, op=ALU.subtract)],
               reads=[g("grows")], writes=[g("grows")], mark="ve")
        B.step("scalar", [lambda e: e.activation(GC, GB, AF.Sqrt,
                                                 bias=cvals[0:8, CV_EPS:CV_EPS + 1])],
               reads=[g("grows"), g("cvals")], writes=[g("grows")], mark="se")
        B.step("vector", [
            lambda e: e.reciprocal(gtmp[0:8, 0:T], GC),
        ], reads=[g("grows"), g("gtmp")], writes=[g("gtmp")], mark="ve")
        B.step("scalar", [lambda e: e.activation(GB, gtmp[0:8, 0:T],
                                                 AF.Identity, bias=0.0)],
               reads=[g("gtmp")], writes=[g("grows")], mark="se")
        B.step("vector", [
            lambda e: e.scalar_tensor_tensor(GC, GA, -1.0, GB, op0=ALU.mult,
                                             op1=ALU.mult),
            lambda e: e.tensor_copy(gbf[0:8, 0:T], GB),
            lambda e: e.tensor_copy(gbf2[0:8, 0:T], GC),
        ], reads=[g("grows")], writes=[g("grows"), g("gbf")], mark="ve")
        for h in range(NH):
            B.step("tensor", [
                (lambda hh: (lambda e: e.matmul(
                    bk(4), oneh8T[0:8, hh * 128:hh * 128 + 128], gbf[0:8, 0:T],
                    start=True, stop=True)))(h),
                (lambda hh: (lambda e: e.matmul(
                    bk(5), oneh8T[0:8, hh * 128:hh * 128 + 128], gbf2[0:8, 0:T],
                    start=True, stop=True)))(h),
            ], reads=[g("gbf"), g("oneh8")], writes=[pb[4], pb[5]], mark="te")
            B.step("scalar", [lambda e: e.copy(bct0, bk(4)),
                              lambda e: e.copy(bct1, bk(5))],
                   reads=[pb[4], pb[5]], writes=[g("bct")], mark="se")
            for dvh in range(2):
                m = 2 * h + dvh
                B.step("vector", [
                    (lambda mm: (lambda e: e.tensor_tensor(
                        e2[:, 0:T], retb[:, mm * T:(mm + 1) * T], bct0,
                        op=ALU.mult)))(m),
                    (lambda mm: (lambda e: e.tensor_tensor(
                        e3[:, 0:T], e2[:, 0:T], bct1, op=ALU.add)))(m),
                    (lambda mm: (lambda e: e.tensor_scalar(
                        e2[:, 0:T], e3[:, 0:T], gnwc[:, mm:mm + 1],
                        gnbc[:, mm:mm + 1], op0=ALU.mult, op1=ALU.add)))(m),
                    (lambda mm: (lambda e: e.tensor_tensor(
                        retb[:, mm * T:(mm + 1) * T], e2[:, 0:T],
                        sgb[:, mm * T:(mm + 1) * T], op=ALU.mult)))(m),
                ], reads=[g("retb"), g("bct"), g("gnw"), g("gnb"), g("sgb")],
                    writes=[g("retb"), g("e2"), g("e3")], mark="ve")
        if c == 0:
            tap("gate", retb[:, 0:16 * T], [g("retb")])

        for eh in range(NH):
            stream_w8("wo", eh, w8s[1][0], g("w8_1"))
            for m2 in range(2):
                m = 2 * eh + m2
                b = m % 4
                ems = []
                for dc in range(16):
                    ems.append((lambda d, mm2, oap=bk(b): (lambda e: e.matmul(
                        oap, w8s[1][0][:, d * 256 + mm2 * 128:d * 256 + (mm2 + 1) * 128],
                        retb[:, d * T:(d + 1) * T],
                        start=(d == 0), stop=False)))(dc, m2))
                ems.append((lambda mm, oap=bk(b): (lambda e: e.matmul(
                    oap, aug_all[0:1, AUG["wo"] + mm * 128:AUG["wo"] + (mm + 1) * 128],
                    ones_bf[0:1, 0:T], start=False, stop=True)))(m))
                B.step("tensor", ems,
                       reads=[g("retb"), g("w8_1"), g("const1"), g("augwo")],
                       writes=[pb[b]], mark="te")
                B.step("scalar", [
                    (lambda bb, mm: (lambda e: e.copy(
                        aob[:, mm * T:(mm + 1) * T], bk(bb))))(b, m),
                    (lambda bb: (lambda e: e.activation(
                        sqt[:, 0:T], bk(bb), AF.Square)))(b),
                ], reads=[pb[b]], writes=[g("sgb"), g("sqt")], mark="se")
                B.step("tensor", [
                    (lambda mm: (lambda e: e.matmul(
                        bk(5, parts=1), ones_colbf[:], aob[:, mm * T:(mm + 1) * T],
                        start=(mm == 0), stop=(mm == 15))))(m),
                    (lambda mm: (lambda e: e.matmul(
                        bk(6, parts=1), ones_colbf[:], sqt[:, 0:T],
                        start=(mm == 0), stop=(mm == 15))))(m),
                ], reads=[g("sgb"), g("sqt"), g("const1")], writes=[pb[5], pb[6]],
                    mark="te")
        strip_chain(T, 5, 6, want_aug=False)
        B.step("vector", [
            lambda e: e.scalar_tensor_tensor(s_tmp[0:1, :], s_mean[0:1, :], -1.0,
                                             s_r[0:1, :], op0=ALU.mult, op1=ALU.mult),
            lambda e: e.tensor_copy(bcm2[0:1, 0:T], s_tmp[0:1, :]),
        ], reads=[g("strip")], writes=[g("strip"), g("bcm2")], mark="ve")
        bcast(bcm, g("bcm"), T, 4, bct0, g("bct"))
        bcast(bcm2, g("bcm2"), T, 5, bct1, g("bct"))
        for dc in range(16):
            sl = dc % 2
            dma("sync", "ds", x32[sl][0],
                d_x[dc * 128:(dc + 1) * 128, cols],
                reads=[g(f"x32_{sl}")], writes=[g(f"x32_{sl}")])
            B.step("vector", [
                (lambda d: (lambda e: e.tensor_tensor(
                    e2[:, 0:T], aob[:, d * T:(d + 1) * T], bct0, op=ALU.mult)))(dc),
                lambda e: e.tensor_tensor(e3[:, 0:T], e2[:, 0:T], bct1, op=ALU.add),
                (lambda s: (lambda e: e.tensor_tensor(
                    yt32[s][0], e3[:, 0:T], x32[s][0], op=ALU.add)))(sl),
            ], reads=[g("sgb"), g("bct"), g(f"x32_{sl}"), g(f"yt32_{sl}")],
                writes=[g("e2"), g("e3"), g(f"yt32_{sl}")], mark="ve")
            dma("sync", "ds", d_y[dc, :, cols], yt32[sl][0],
                reads=[g(f"yt32_{sl}")], writes=[g("d_y")])
        if c == 0:
            tap("y0", yt32[0][0], [g("yt32_0")])

    barrier()
    for hh in (xbf_h, csq_h, rqb_h, rkb_h, qt_h, ktl_h, ktm_h, vtm_h, qhb_h,
               simb_h, mskb_h, retb_h, sgb_h, bct0_h, bct1_h, gqs_h, mtmp_h):
        arb.freed(hh)
    arf.freed(r1bc_h)
    for a in x32 + yt32:
        arf.freed(a[1])

    ybf, ybf_h = abf(16 * TP)
    unb, unb_h = abf(16 * TP)
    ffns = [abf(TP) for _ in range(2)]
    ffnr = ffns
    rwbt, rwbt_h = abf(TP)
    pbt, pbt_h = abf(TP)
    bbt, bbt_h = abf(TP)
    fA, fA_h = abf(TP)
    fB, fB_h = abf(TP)
    wgt8, wgt8_h = abf(128)
    y32 = [af32(512) for _ in range(2)]
    o32 = [af32(512) for _ in range(2)]

    for hf4 in range(2 if stop_after >= 4 else 0):
        cols = slice(hf4 * TP, (hf4 + 1) * TP)
        for q4 in range(4):
            dma("gpsimd", "dg",
                ybf.rearrange("p (dc w) -> p dc w", w=TP)[:, q4 * 4:(q4 + 1) * 4],
                d_y[:, :, cols].rearrange("dc p w -> p dc w")[:, q4 * 4:(q4 + 1) * 4],
                reads=[g("ybf"), g("d_y")], writes=[g("ybf")])
        stats_mms(ybf, g("ybf"), TP, 0, 2)
        strip_chain(TP, 0, 2, r2_save=True, want_rcols=True)
        dma("sync", "ds", wgt8.rearrange("p (dc m) -> p dc m", m=8),
            d_wgTbf.rearrange("dc p m -> p dc m"),
            reads=[g("scr_wgT")], writes=[g("wgt8")])
        if hf4 == 0:
            ems = []
            for dc in range(16):
                ems.append((lambda d: (lambda e: e.matmul(
                    bk(2, parts=1, width=8), ones_colbf[:], wgt8[:, d * 8:(d + 1) * 8],
                    start=(d == 0), stop=(d == 15))))(dc))
            B.step("tensor", ems, reads=[g("wgt8"), g("const1")], writes=[pb[2]],
                   mark="te")
            B.step("scalar", [lambda e: e.copy(augw[0:1, 0:8],
                                               bk(2, parts=1, width=8))],
                   reads=[pb[2]], writes=[g("augw")], mark="se")
            B.step("vector", [lambda e: e.tensor_copy(aug8[0:1, 0:64],
                                                      augw[0:1, 0:64])],
                   reads=[g("augw")], writes=[g("aug8")], mark="ve")
        for tg in range(2):
            b = 2 + tg
            ems = []
            for dc in range(16):
                ems.append((lambda d, t, oap=bk(b, parts=8): (lambda e: e.matmul(
                    oap, wgt8[:, d * 8:(d + 1) * 8],
                    ybf[:, d * TP + t * 512:d * TP + (t + 1) * 512],
                    start=(d == 0), stop=False)))(dc, tg))
            ems.append((lambda t, oap=bk(b, parts=8): (lambda e: e.matmul(
                oap, aug8[0:1, 0:8], mrows[0:1, t * 512:(t + 1) * 512],
                start=False, stop=True)))(tg))
            B.step("tensor", ems, reads=[g("ybf"), g("wgt8"), g("aug8"), g("mrows")],
                   writes=[pb[b]], mark="te")
            B.step("scalar", [(lambda t, bb: (lambda e: e.copy(
                lgsb[0:8, t * 512:(t + 1) * 512], bk(bb, parts=8))))(tg, b)],
                reads=[pb[b]], writes=[g("lgsb")], mark="se")
        for tb in range(8):
            B.step("tensor", [(lambda t: (lambda e: e.transpose(
                psb[:, 0:8], lgsb[0:8, t * 128:(t + 1) * 128],
                ident[0:8, 0:8])))(tb)],
                reads=[g("lgsb"), g("ident")], writes=[g("psb")], mark="te")
            B.step("scalar", [(lambda t: (lambda e: e.copy(
                lgt[:, t * 8:(t + 1) * 8], psb[:, 0:8])))(tb)],
                reads=[g("psb")], writes=[g("lgt")], mark="se")
        if tb == 7:
            pass
        B.step("vector", [lambda e: e.tensor_copy(lgtf[:], lgt[:])],
               reads=[g("lgt")], writes=[g("lgtf")], mark="ve")
        for tb in range(8):
            B.step("vector", [
                (lambda t: (lambda e: e.max(out=m8t[:], in_=lgtf[:, t * 8:(t + 1) * 8])))(tb),
                lambda e: e.tensor_tensor(m8t[:, 2:3], m8t[:, 1:2], m8t[:, 0:1],
                                          op=ALU.subtract),
            ], reads=[g("lgtf")], writes=[g("m8t")], mark="ve")
            B.step("scalar", [
                (lambda t: (lambda e: e.activation(
                    m8t[:, 3:4], m8t[:, 2:3], AF.Exp, scale=rcols[:, t:t + 1])))(tb),
                lambda e: e.activation(m8t[:, 4:5], m8t[:, 3:4], AF.Identity,
                                       bias=cvals[:, CV_ONE:CV_ONE + 1]),
            ], reads=[g("m8t"), g("rcols"), g("cvals")], writes=[g("m8t")],
                mark="se")
            B.step("vector", [
                lambda e: e.reciprocal(m8t[:, 5:6], m8t[:, 4:5]),
            ], reads=[g("m8t")], writes=[g("m8t")], mark="ve")
            B.step("scalar", [(lambda t: (lambda e: e.activation(
                rwc[:, t:t + 1], m8t[:, 4:5], AF.Identity, bias=0.0,
                scale=m8t[:, 5:6])))(tb)],
                reads=[g("m8t")], writes=[g("rwc")], mark="se")
        B.step("tensor", [lambda e: e.transpose(bk(2, parts=8, width=128),
                                                rwc[:, 0:8], identF[:])],
               reads=[g("rwc"), g("ident")], writes=[pb[2]], mark="te")
        B.step("scalar", [lambda e: e.copy(rw8[0:8, 0:128],
                                           bk(2, parts=8, width=128))],
               reads=[pb[2]], writes=[g("rw8")], mark="se")
        B.step("vector", [lambda e: e.tensor_copy(lgsb[0:8, 0:128], rw8[0:8, 0:128])],
               reads=[g("rw8")], writes=[g("lgsb")], mark="ve")
        for tb in range(8):
            B.step("tensor", [(lambda t: (lambda e: e.matmul(
                bk(2, width=128), oneh8T[0:8, t * 128:t * 128 + 128],
                lgsb[0:8, 0:128], start=True, stop=True)))(tb)],
                reads=[g("lgsb"), g("oneh8")], writes=[pb[2]], mark="te")
            B.step("scalar", [(lambda t: (lambda e: e.copy(
                rwbt[:, t * 128:(t + 1) * 128], bk(2, width=128))))(tb)],
                reads=[pb[2]], writes=[g("rwbt")], mark="se")
        for eh in range(NH):
            stream_w8("wupT", eh, w8s[0][0], g("w8_0"))
            if hf4 == 0:
                wsum("wupT", eh, w8s[0][0], g("w8_0"), 2)
            for m2 in range(2):
                m = 2 * eh + m2
                bb0 = 3
                ems = []
                for dc in range(16):
                    for tg in range(2):
                        ems.append((lambda d, t, mm2: (lambda e: e.matmul(
                            bk(bb0 + t),
                            w8s[0][0][:, d * 256 + mm2 * 128:d * 256 + (mm2 + 1) * 128],
                            ybf[:, d * TP + t * 512:d * TP + (t + 1) * 512],
                            start=(d == 0), stop=False)))(dc, tg, m2))
                for tg in range(2):
                    ems.append((lambda t, mm: (lambda e: e.matmul(
                        bk(bb0 + t),
                        aug_all[0:1, AUG["wupT"] + mm * 128:AUG["wupT"] + (mm + 1) * 128],
                        mrows[0:1, t * 512:(t + 1) * 512],
                        start=False, stop=True)))(tg, m))
                B.step("tensor", ems,
                       reads=[g("ybf"), g("w8_0"), g("mrows"), g("augwupT")],
                       writes=[pb[bb0], pb[bb0 + 1]], mark="te")
                B.step("scalar", [(lambda t, mm, bb: (lambda e: e.activation(
                    unb[:, mm * TP + t * 512:mm * TP + (t + 1) * 512],
                    bk(bb + t), AF.Relu)))(tg, m, bb0) for tg in range(2)],
                    reads=[pb[bb0], pb[bb0 + 1]], writes=[g("unb")], mark="se")
        for dc in range(16):
            B.step("vector", [(lambda d: (lambda e: e.tensor_tensor(
                ybf[:, d * TP:(d + 1) * TP], ybf[:, d * TP:(d + 1) * TP],
                rwbt, op=ALU.mult)))(dc)],
                reads=[g("ybf"), g("rwbt")], writes=[g("ybf")], mark="ve")
        B.step("vector", [lambda e: e.tensor_tensor(mrows[0:1, 0:TP], mrows[0:1, 0:TP],
                                                    rwbt[0:1, 0:TP], op=ALU.mult)],
               reads=[g("mrows"), g("rwbt")], writes=[g("mrows")], mark="ve")
        for eh in range(NH):
            stream_w8("wdnT", eh, w8s[0][0], g("w8_0"))
            stream_w8("we", eh, w8s[1][0], g("w8_1"))
            if hf4 == 0:
                wsum("we", eh, w8s[1][0], g("w8_1"), 2)
            for m2 in range(2):
                m = 2 * eh + m2
                bb0 = 3
                ems = []
                for oc in range(16):
                    for tg in range(2):
                        ems.append((lambda d, t, mm2: (lambda e: e.matmul(
                            bk(bb0 + t),
                            w8s[0][0][:, d * 256 + mm2 * 128:d * 256 + (mm2 + 1) * 128],
                            unb[:, d * TP + t * 512:d * TP + (t + 1) * 512],
                            start=(d == 0), stop=False)))(oc, tg, m2))
                for dc in range(16):
                    for tg in range(2):
                        ems.append((lambda d, t, mm2: (lambda e: e.matmul(
                            bk(bb0 + t),
                            w8s[1][0][:, d * 256 + mm2 * 128:d * 256 + (mm2 + 1) * 128],
                            ybf[:, d * TP + t * 512:d * TP + (t + 1) * 512],
                            start=False, stop=False)))(dc, tg, m2))
                for tg in range(2):
                    ems.append((lambda t, mm: (lambda e: e.matmul(
                        bk(bb0 + t),
                        aug_all[0:1, AUG["we"] + mm * 128:AUG["we"] + (mm + 1) * 128],
                        mrows[0:1, t * 512:(t + 1) * 512],
                        start=False, stop=True)))(tg, m))
                B.step("tensor", ems,
                       reads=[g("ybf"), g("unb"), g("w8_0"), g("w8_1"), g("mrows"),
                              g("augwe")],
                       writes=[pb[bb0], pb[bb0 + 1]], mark="te")
                fsl = m % 2
                B.step("scalar", [
                    (lambda t, bb, s: (lambda e: e.copy(
                        ffns[s][0][:, t * 512:(t + 1) * 512],
                        bk(bb + t))))(0, bb0, fsl),
                    (lambda bb: (lambda e: e.activation(
                        sqt[:, 0:T], bk(bb), AF.Square)))(bb0),
                    (lambda t, bb, s: (lambda e: e.copy(
                        ffns[s][0][:, t * 512:(t + 1) * 512],
                        bk(bb + t))))(1, bb0, fsl),
                    (lambda bb: (lambda e: e.activation(
                        e0[:, 0:T], bk(bb + 1), AF.Square)))(bb0),
                ], reads=[pb[bb0], pb[bb0 + 1]]
                    + [g(f"ffns{fsl}")],
                    writes=[g(f"ffns{fsl}"), g("sqt"), g("e0")], mark="se")
                dma("sync", "ds", d_ffn[m, :, cols], ffns[fsl][0],
                    reads=[g(f"ffns{fsl}")], writes=[g("d_ffn")])
                B.step("tensor", [
                    (lambda mm, s: (lambda e: e.matmul(
                        bk(0, parts=1), ones_colbf[:], ffns[s][0][:, 0:512],
                        start=(mm == 0), stop=(mm == 15))))(m, fsl),
                    (lambda mm: (lambda e: e.matmul(
                        bk(5, parts=1), ones_colbf[:], sqt[:, 0:T],
                        start=(mm == 0), stop=(mm == 15))))(m),
                    (lambda mm, s: (lambda e: e.matmul(
                        bk(1, parts=1), ones_colbf[:], ffns[s][0][:, 512:1024],
                        start=(mm == 0), stop=(mm == 15))))(m, fsl),
                    (lambda mm: (lambda e: e.matmul(
                        bk(6, parts=1), ones_colbf[:], e0[:, 0:T],
                        start=(mm == 0), stop=(mm == 15))))(m),
                ], reads=[g(f"ffns{fsl}"), g("sqt"), g("e0"), g("const1")],
                    writes=[pb[0], pb[1], pb[5], pb[6]], mark="te")
        strip_chain(TP, 0, 5, want_aug=False, r2_mul=True)
        bcast(bcm, g("bcm"), TP, 2, pbt, g("pbt"))
        bcast(bcm2, g("bcm2"), TP, 2, bbt, g("bbt"))
        for dc in range(16):
            for tg in range(2):
                sl = (2 * dc + tg) % 2
                c2 = slice(hf4 * TP + tg * 512, hf4 * TP + (tg + 1) * 512)
                dma("sync", "ds", y32[sl][0], d_y[dc, :, c2],
                    reads=[g(f"y32_{sl}"), g("d_y")], writes=[g(f"y32_{sl}")])
                dma("sync", "ds", ffnr[sl][0][:, 0:512], d_ffn[dc, :, c2],
                    reads=[g(f"ffns{sl}"), g("d_ffn")], writes=[g(f"ffns{sl}")])
                B.step("vector", [
                    (lambda s, t: (lambda e: e.tensor_tensor(
                        fA[:, 0:512], ffnr[s][0][:, 0:512],
                        pbt[:, t * 512:(t + 1) * 512], op=ALU.mult)))(sl, tg),
                    (lambda t: (lambda e: e.tensor_tensor(
                        fB[:, 0:512], fA[:, 0:512],
                        bbt[:, t * 512:(t + 1) * 512], op=ALU.add)))(tg),
                    (lambda s: (lambda e: e.tensor_tensor(
                        o32[s][0], fB[:, 0:512], y32[s][0], op=ALU.add)))(sl),
                ], reads=[g(f"ffns{sl}"), g("pbt"), g("bbt"), g(f"y32_{sl}"),
                          g(f"o32_{sl}")],
                    writes=[g("fA"), g("fB"), g(f"o32_{sl}")], mark="ve")
                dma("sync", "ds", d_out[dc, :, c2], o32[sl][0],
                    reads=[g(f"o32_{sl}")])

    if stop_after >= 4:
        tap("ybf4", ybf[:, 0:2 * TP], [g("ybf")])
        tap("lgtf", lgtf[:], [g("lgtf")])
        tap("rwc", rwc[:, 0:8], [g("rwc")])
        tap("rwbt", rwbt[:], [g("rwbt")])
        tap("pbt", pbt[:], [g("pbt")])
        tap("bbt", bbt[:], [g("bbt")])
        tap("r2row", r2row[0:1, :], [g("r2row")])
        tap("unb4", unb[:, 0:2 * TP], [g("unb")])
        tap("mrows4", mrows[0:2, 0:TP], [g("mrows")])
        tap("o32l", o32[1][0][:], [g("o32_1")])

    B.step("sync", [lambda e: e.nop()],
           extra=[(s, v, None) for s, v in B.cnt.items()
                  if s[:2] in ("ds", "dg") and v > 0])

    globals()["LAST_B"] = B
    B.check()
    with nc.Block() as block:
        B.emit_all(block)
    ctx.close()
    return nc


_CACHE = {}


STOP_AFTER = 4
PH2SUB = 5


def _get_nc():
    key = ("nc", STOP_AFTER, PH2SUB)
    if key not in _CACHE:
        _CACHE[key] = build_program(STOP_AFTER, PH2SUB)
    return _CACHE[key]


def kernel(**inputs):
    x = np.ascontiguousarray(np.asarray(inputs["L0_attn_x"], dtype=np.float32))
    w = {k: np.ascontiguousarray(np.asarray(inputs[k], dtype=np.float32))
         for k in inputs}
    nc = _get_nc()

    wq = w["L0_attn_w_q"]; wk = w["L0_attn_w_k"]; wv = w["L0_attn_w_v"]
    wg = w["L0_attn_w_g"]; wo = w["L0_attn_w_o"]
    wupT = np.ascontiguousarray(w["L0_ffn_W_up"].T)
    wdnT = np.ascontiguousarray(w["L0_ffn_W_down"].T)
    wgT = np.ascontiguousarray(w["L0_ffn_W_gate"].T)
    we = w["L0_ffn_W_expert"]
    colify = lambda v: np.ascontiguousarray(v.reshape(16, 128).T)
    common = {
        "wq": wq, "wk": wk, "wv": wv, "wg": wg, "wo": wo,
        "wupT": wupT, "wdnT": wdnT, "we": we, "wgT": wgT,
        "bq": w["L0_attn_b_q"].reshape(1, HID),
        "bk": w["L0_attn_b_k"].reshape(1, HID),
        "bv": w["L0_attn_b_v"].reshape(1, HID),
        "bg": w["L0_attn_b_g"].reshape(1, HID),
        "bo": w["L0_attn_b_o"].reshape(1, HID),
        "gnw": colify(w["L0_attn_gn_weight"]),
        "gnb": colify(w["L0_attn_gn_bias"]),
        "bgcol": colify(w["L0_attn_b_g"]),
    }
    zeros_fm = np.zeros((HID, SPAN), dtype=np.float32)
    in_maps = []
    for core in range(8):
        b, hf = core // 2, core % 2
        xs = np.ascontiguousarray(x[b, hf * SPAN:(hf + 1) * SPAN].T)
        xp = (np.ascontiguousarray(x[b, 0:SPAN].T) if hf == 1 else zeros_fm)
        im = dict(common)
        im["x_fm"] = xs
        im["xp_fm"] = xp
        im["pos_base"] = np.full((128, 1), float(hf * SPAN), dtype=np.float32)
        im["pflag"] = np.full((1, TP), 1.0 if hf == 1 else 0.0, dtype=np.float32)
        in_maps.append(im)

    res = run_bass_kernel_spmd(nc, in_maps, core_ids=list(range(8)))
    out = np.empty((4, 4096, HID), dtype=np.float32)
    for core in range(8):
        b, hf = core // 2, core % 2
        o = np.asarray(res.results[core]["out_fm"])
        out[b, hf * SPAN:(hf + 1) * SPAN] = o.reshape(HID, SPAN).T
    return out


# version 09 (final)
# speedup vs baseline: 1.0784x; 1.0082x over previous
import math
from contextlib import ExitStack

import numpy as np
import ml_dtypes

import concourse.bass as bass
import concourse.mybir as mybir
from concourse.bass_utils import run_bass_kernel_spmd

F32 = mybir.dt.float32
BF16 = mybir.dt.bfloat16
I32 = mybir.dt.int32
AF = mybir.ActivationFunctionType
ALU = mybir.AluOpType

HID, NH = 2048, 8
T, NCH = 512, 4
TP, NPC = 1024, 2
SPAN = 2048
EPS = 1e-5
_2PI = 2.0 * math.pi
C1 = float(np.float32(_2PI))
C2 = float(np.float32(_2PI - np.float64(np.float32(_2PI))))
C3 = float(np.float32(_2PI - np.float64(np.float32(_2PI))
                      - np.float64(np.float32(_2PI - np.float64(np.float32(_2PI))))))
HALF_PI = float(np.float32(math.pi / 2))
PI = float(np.float32(math.pi))
N2PI = -float(np.float32(_2PI))
INV_2PI = float(np.float32(1.0 / _2PI))
LN1E4 = float(math.log(10000.0))
GAMMAS = [float(v) for v in (1.0 - np.exp(np.linspace(np.log(1 / 32), np.log(1 / 512), 8)))]
LNG = [float(np.float32(math.log(v))) for v in GAMMAS]
LNG_LO = [float(np.float64(math.log(v)) - np.float64(np.float32(math.log(v))))
          for v in GAMMAS]
THS_HI = float(np.float32(-LN1E4 / 127.0))
THS_LO = float(np.float64(-LN1E4 / 127.0) - np.float64(THS_HI))
THB_HI = float(np.float32(-LN1E4 * 64.0 / 127.0))
THB_LO = float(np.float64(-LN1E4 * 64.0 / 127.0) - np.float64(THB_HI))
GT = [float(np.float32(v) ** T) for v in GAMMAS]
GTP = [float(np.float32(v) ** TP) for v in GAMMAS]

ENGINES = ("sync", "gpsimd", "tensor", "vector", "scalar")
DEBUG_TAPS = []
SHUF_MASK = []
for _i in range(0, 32, 2):
    SHUF_MASK += [_i + 1, _i]


class Buf:
    __slots__ = ("name", "w", "rs")

    def __init__(self, name):
        self.name = name
        self.w = None
        self.rs = []


class Builder:
    def __init__(self):
        self.sched = []
        self.cnt = {}
        self.floor = {e: {} for e in ENGINES}
        self.sem_handles = {}
        self.bufs = {}
        self.last_dma = {}

    def buf(self, name):
        if name not in self.bufs:
            self.bufs[name] = Buf(name)
        return self.bufs[name]

    def step(self, eng, emits, reads=(), writes=(), mark=None, dma=False, extra=()):
        evs = [b.w for b in reads]
        for b in writes:
            evs.append(b.w)
            evs.extend(b.rs)
        evs.extend(extra)
        if dma and mark is not None and self.cnt.get(mark, 0) > 0:
            wkey = tuple(sorted(id(b) for b in writes))
            prev = self.last_dma.get(mark)
            if not (wkey and prev == (wkey, len(self.sched))):
                evs.append((mark, self.cnt[mark], None))
        if dma and mark is not None:
            self.last_dma[mark] = (tuple(sorted(id(b) for b in writes)),
                                   len(self.sched) + 1)
        ws = []
        fl = self.floor[eng]
        for ev in evs:
            if ev is None:
                continue
            s, v, peng = ev
            if v <= 0 or s is None:
                continue
            if fl.get(s, 0) >= v:
                continue
            fl[s] = v
            ws.append((s, v))
        incs = []
        ev = None
        if mark is not None:
            if dma:
                amt = 16 * len(emits)
                incs = [(mark, 16)] * len(emits)
            elif eng in ("vector", "scalar", "gpsimd"):
                amt = len(emits)
                incs = [(mark, 1)] * len(emits)
            else:
                amt = 1
                incs = [(mark, 1)]
            base = self.cnt.get(mark, 0)
            self.cnt[mark] = base + amt
            ev = (mark, self.cnt[mark], None if dma else eng)
        else:
            base = 0
        self.sched.append((eng, ws, emits, incs, dma, base))
        if ev is not None:
            for b in writes:
                b.w = ev
                b.rs = []
            for b in reads:
                b.rs.append(ev)
        else:
            for b in writes:
                if b.w is None:
                    b.w = (None, 0, eng)
        return ev

    def check(self):
        qs = {e: [] for e in ENGINES}
        for it in self.sched:
            qs[it[0]].append(it)
        idx = {e: 0 for e in ENGINES}
        sim = {}
        prog = True
        while prog:
            prog = False
            for e in ENGINES:
                while idx[e] < len(qs[e]):
                    _, ws, emits, incs, dma, _base = qs[e][idx[e]]
                    if all(sim.get(s, 0) >= v for s, v in ws):
                        for s, a in incs:
                            sim[s] = sim.get(s, 0) + a
                        idx[e] += 1
                        prog = True
                    else:
                        break
        stuck = {e: i for e, i in idx.items() if i < len(qs[e])}
        if stuck:
            lines = []
            for e, i in stuck.items():
                _, ws, _, _, _ = qs[e][i]
                lines.append(f"{e}[{i}/{len(qs[e])}] waits " +
                             ",".join(f"{s}>={v}(at {sim.get(s, 0)})" for s, v in ws))
            raise RuntimeError("schedule deadlock:\n" + "\n".join(lines))

    def emit_all(self, block):
        sems = self.sem_handles

        def make(name):
            def f(eng_obj):
                for eng, ws, emits, incs, dma, base in self.sched:
                    if eng != name:
                        continue
                    for s, v in ws:
                        eng_obj.wait_ge(sems[s], v)
                    strict = (not dma) and eng in ("vector", "scalar", "gpsimd") \
                        and incs and len(incs) == len(emits)
                    insts = []
                    for i, em in enumerate(emits):
                        if strict and i > 0:
                            eng_obj.wait_ge(sems[incs[0][0]], base + i)
                        inst = em(eng_obj)
                        insts.append(inst)
                        if incs and len(incs) == len(emits):
                            inst.then_inc(sems[incs[i][0]], incs[i][1])
                    if incs and len(incs) != len(emits):
                        insts[-1].then_inc(sems[incs[0][0]], 1)
            return f
        block.sync(make("sync"))
        block.gpsimd(make("gpsimd"))
        block.tensor(make("tensor"))
        block.vector(make("vector"))
        block.scalar(make("scalar"))


class Arena:
    def __init__(self, nbytes, esz, name):
        self.esz = esz
        self.free = [(0, nbytes)]
        self.hwm = 0
        self.name = name

    def alloc(self, nelem):
        nb = ((nelem * self.esz + 63) // 64) * 64
        for i, (a, b) in enumerate(self.free):
            if b - a >= nb:
                if a + nb == b:
                    self.free.pop(i)
                else:
                    self.free[i] = (a + nb, b)
                self.hwm = max(self.hwm, a + nb)
                return a // self.esz, nb
        raise RuntimeError(f"{self.name} OOM want {nb} free {self.free}")

    def freed(self, h):
        off, nb = h
        a = off * self.esz
        self.free.append((a, a + nb))
        self.free.sort()
        m = []
        for s in self.free:
            if m and m[-1][1] >= s[0]:
                m[-1] = (m[-1][0], max(m[-1][1], s[1]))
            else:
                m.append(list(s))
        self.free = [tuple(x) for x in m]


def build_program(stop_after=4, PH2_SUB=5):
    nc = bass.Bass("TRN2", target_bir_lowering=False)
    B = Builder()
    ctx = ExitStack()
    g = B.buf

    d_x = nc.dram_tensor("x_fm", [HID, SPAN], F32, kind="ExternalInput").ap()
    d_xp = nc.dram_tensor("xp_fm", [HID, SPAN], F32, kind="ExternalInput").ap()
    d_posb = nc.dram_tensor("pos_base", [128, 1], F32, kind="ExternalInput").ap()
    d_pflag = nc.dram_tensor("pflag", [1, TP], F32, kind="ExternalInput").ap()
    WN = ["wv", "wk", "wq", "wg", "wo", "wupT", "wdnT", "we"]
    d_w = {n: nc.dram_tensor(n, [HID, HID], F32, kind="ExternalInput").ap() for n in WN}
    d_wgT = nc.dram_tensor("wgT", [HID, 8], F32, kind="ExternalInput").ap()
    d_b = {n: nc.dram_tensor("b" + n, [1, HID], F32, kind="ExternalInput").ap()
           for n in ["q", "k", "v", "g", "o"]}
    d_gnw = nc.dram_tensor("gnw", [128, 16], F32, kind="ExternalInput").ap()
    d_gnb = nc.dram_tensor("gnb", [128, 16], F32, kind="ExternalInput").ap()
    d_bgc = nc.dram_tensor("bgcol", [128, 16], F32, kind="ExternalInput").ap()
    d_out = nc.dram_tensor("out_fm", [16, 128, SPAN], F32, kind="ExternalOutput").ap()
    d_wbf = {n: nc.dram_tensor("scr_" + n, [8, 16, 128, 256], BF16, kind="Internal").ap()
             for n in WN}
    d_wgTbf = nc.dram_tensor("scr_wgT", [16, 128, 8], BF16, kind="Internal").ap()
    d_mask = nc.dram_tensor("scr_mask", [NH, 4, 128, T], BF16, kind="Internal").ap()
    d_y = nc.dram_tensor("scr_y", [16, 128, SPAN], F32, kind="Internal").ap()
    d_ffn = nc.dram_tensor("scr_ffn", [16, 128, SPAN], BF16, kind="Internal").ap()

    def sb(name, shape, dt):
        return ctx.enter_context(nc.sbuf_tensor(name, shape, dt))

    ident = sb("ident", [128, 128], BF16)
    identF = sb("identF", [128, 128], F32)
    ones_bf = sb("ones_bf", [1, T], BF16)
    ones_colbf = sb("ones_colbf", [128, 1], BF16)
    ones_bc = sb("ones_bc", [1, 128], BF16)
    oneh8 = sb("oneh8", [128, 64], BF16)
    oneh8T = sb("oneh8T", [8, 1024], BF16)
    rw8 = sb("rw8", [8, 128], F32)
    thcol = sb("thcol", [128, 2], F32)
    altsign = sb("altsign", [128, 1], F32)
    posb = sb("posb", [128, 1], F32)
    gnwc = sb("gnwc", [128, 16], F32)
    gnbc = sb("gnbc", [128, 16], F32)
    bgc = sb("bgc", [128, 16], F32)
    kdec = sb("kdec", [128, 32], F32)
    kdecP = sb("kdecP", [128, 64], F32)
    pfl = sb("pfl", [1, TP], BF16)
    rcols = sb("rcols", [128, 16], F32)
    S_fp = sb("S_fp", [128, 4096], F32)
    S_bf = sb("S_bf", [128, 512], BF16)
    aug_all = sb("aug_all", [2, 5 * HID], BF16)
    AUG = {"wq": 0, "wk": HID, "wv": 2 * HID, "wg": 3 * HID, "wo": 4 * HID,
           "wupT": 0, "we": HID}
    augw = sb("augw", [1, 256], F32)
    aug8 = sb("aug8", [1, 64], BF16)
    mrows = sb("mrows", [2, TP], BF16)
    s_sum = sb("s_sum", [1, 512], F32)
    s_sq = sb("s_sq", [1, 512], F32)
    s_mean = sb("s_mean", [1, 512], F32)
    s_tmp = sb("s_tmp", [1, 512], F32)
    s_r = sb("s_r", [1, 512], F32)
    s_t3 = sb("s_t3", [1, 512], F32)
    r2row = sb("r2row", [1, TP], F32)
    bcm = sb("bcm", [1, TP], BF16)
    bcm2 = sb("bcm2", [1, TP], BF16)
    grows = sb("grows", [8, 3 * T], F32)
    gtmp = sb("gtmp", [8, T], F32)
    gbf = sb("gbf", [8, T], BF16)
    gbf2 = sb("gbf2", [8, T], BF16)
    lgsb = sb("lgsb", [8, TP], BF16)
    lgt = sb("lgt", [128, 64], BF16)
    lgtf = sb("lgtf", [128, 64], F32)
    m8t = sb("m8t", [128, 8], F32)
    rwc = sb("rwc", [128, 8], F32)
    csi_t = sb("csi", [128, 512], I32)
    cvals = sb("cvals", [128, 23], F32)
    cvi1 = sb("cvi1", [128, 1], I32)
    CV_THS, CV_THB, CV_EPS, CV_256 = 0, 1, 2, 3
    CV_LNG = 4
    CV_ONE = 12
    CV_THS2, CV_THB2 = 13, 14
    CV_LNG2 = 15

    ABF = 101 * 1024
    AFW = 23 * 1024
    arb = Arena(ABF, 2, "bf16")
    arf = Arena(AFW, 4, "f32")
    t_bf = sb("arena_bf", [128, ABF // 2], BF16)
    t_f32 = sb("arena_f32", [128, AFW // 4], F32)

    def abf(n):
        o, nb = arb.alloc(n)
        return t_bf[:, o:o + n], (o, nb)

    def af32(n):
        o, nb = arf.alloc(n)
        return t_f32[:, o:o + n], (o, nb)

    ps = ctx.enter_context(nc.psum_tensor("ps", [128, 3584], F32))
    psb = ctx.enter_context(nc.psum_tensor("psb", [128, 1024], BF16))

    def bk(i, parts=128, width=512, off=0):
        return ps[0:parts, 512 * i + off:512 * i + off + width]

    pb = [g(f"ps{i}") for i in range(7)]

    for s in (["te", "ve", "se", "gp", "sy"]
              + [q + str(i) for q in ("dg", "ds", "dc") for i in range(4)]):
        B.sem_handles[s] = ctx.enter_context(nc.semaphore(s))

    DMA_K = 4
    dma_rot = {}
    dma_last = {}

    def dma(eng, sem, out_ap, in_ap, reads=(), writes=()):
        wkey = tuple(sorted(id(b) for b in writes))
        last = dma_last.get(sem)
        if not (last and wkey and last[0] == wkey and last[1] == len(B.sched)):
            dma_rot[sem] = dma_rot.get(sem, 0) + 1
        mark = sem + str(dma_rot.get(sem, 0) % DMA_K)
        dma_last[sem] = (wkey, len(B.sched) + 1)
        return B.step(eng, [lambda e, o=out_ap, i=in_ap: e.dma_start(out=o, in_=i)],
                      reads=reads, writes=writes, mark=mark, dma=True)

    ENG_MARK = {"sync": "sy", "gpsimd": "gp", "tensor": "te", "vector": "ve",
                "scalar": "se"}

    def barrier():
        snap = [(s, v) for s, v in B.cnt.items() if v > 0]
        for e in ENGINES:
            B.step(e, [lambda eng: eng.nop()],
                   extra=[(s, v, None) for s, v in snap], mark=ENG_MARK[e])

    taps_made = {}

    def tap(name, ap, readbufs):
        if name in DEBUG_TAPS and name not in taps_made:
            dd = nc.dram_tensor("tap_" + name, list(ap.shape), ap.dtype,
                                kind="ExternalOutput").ap()
            dma("sync", "ds", dd, ap, reads=list(readbufs))
            taps_made[name] = True

    csc, _ = af32(512)
    csn, _ = af32(512)
    csr, _ = af32(512)
    csS, _ = af32(512)
    csC, _ = af32(512)

    B.step("gpsimd", [lambda e: e.memset(ident[:], 1.0),
                      lambda e: e.memset(identF[:], 1.0)],
           writes=[g("ident")], mark="gp")
    B.step("gpsimd", [lambda e: e.affine_select(ident[:], ident[:], pattern=[[1, 128]],
                                                compare_op=ALU.is_equal, fill=0.0,
                                                base=0, channel_multiplier=-1),
                      lambda e: e.affine_select(identF[:], identF[:], pattern=[[1, 128]],
                                                compare_op=ALU.is_equal, fill=0.0,
                                                base=0, channel_multiplier=-1)],
           writes=[g("ident")], mark="gp")
    B.step("vector", [lambda e: e.memset(cvi1[:], 1),
                      lambda e: e.memset(augw[:], 0.0),
                      lambda e: e.memset(cvals[:, CV_THS:CV_THS + 1], -LN1E4 / 127.0),
                      lambda e: e.memset(cvals[:, CV_THB:CV_THB + 1], -LN1E4 * 64.0 / 127.0),
                      lambda e: e.memset(cvals[:, CV_EPS:CV_EPS + 1], EPS),
                      lambda e: e.memset(cvals[:, CV_256:CV_256 + 1], 1.0 / 256.0),
                      lambda e: e.memset(cvals[:, CV_ONE:CV_ONE + 1], 1.0),
                      lambda e: e.memset(cvals[:, CV_THS2:CV_THS2 + 1], THS_LO),
                      lambda e: e.memset(cvals[:, CV_THB2:CV_THB2 + 1], THB_LO)] +
           [(lambda h: (lambda e: e.memset(cvals[:, CV_LNG2 + h:CV_LNG2 + h + 1],
                                           LNG_LO[h])))(h) for h in range(NH)] +
           [(lambda h: (lambda e: e.memset(cvals[:, CV_LNG + h:CV_LNG + h + 1],
                                           LNG[h])))(h) for h in range(NH)] +
           [lambda e: e.memset(ones_bf[:], 1.0),
                      lambda e: e.memset(ones_colbf[:], 1.0),
                      lambda e: e.memset(ones_bc[:], 1.0),
                      lambda e: e.memset(S_fp[:], 0.0)],
           writes=[g("const1"), g("S_fp"), g("cvals")], mark="ve")
    B.step("gpsimd", [lambda e: e.memset(oneh8[:], 1.0),
                      lambda e: e.memset(oneh8T[:], 1.0)],
           writes=[g("oneh8")], mark="gp")
    B.step("gpsimd", [lambda e: e.affine_select(oneh8[:], oneh8[:],
                                                pattern=[[-1, 8], [1, 8]],
                                                compare_op=ALU.is_equal, fill=0.0,
                                                base=0, channel_multiplier=0),
                      lambda e: e.affine_select(oneh8T[:], oneh8T[:],
                                                pattern=[[1, 8], [0, 128]],
                                                compare_op=ALU.is_equal, fill=0.0,
                                                base=0, channel_multiplier=-1)],
           writes=[g("oneh8")], mark="gp")
    B.step("gpsimd", [lambda e: e.iota(csc[:, 0:64], pattern=[[0, 64]], base=0,
                                       channel_multiplier=1,
                                       allow_small_or_imprecise_dtypes=True)],
           writes=[g("csc")], mark="gp")
    B.step("vector", [
        lambda e: e.tensor_copy(csi_t[:, 0:64], csc[:, 0:64]),
        lambda e: e.tensor_single_scalar(csi_t[:, 64:128], csi_t[:, 0:64],
                                         cvi1[:, 0:1], op=ALU.bitwise_and),
        lambda e: e.tensor_copy(csr[:, 0:64], csi_t[:, 64:128]),
        lambda e: e.scalar_tensor_tensor(csn[:, 0:64], csr[:, 0:64], -1.0,
                                         csc[:, 0:64], op0=ALU.mult, op1=ALU.add),
        lambda e: e.tensor_scalar(csn[:, 0:64], csn[:, 0:64], 0.5, None,
                                  op0=ALU.mult),
    ], reads=[g("csc")], writes=[g("csn"), g("csi"), g("csr")], mark="ve")
    B.step("scalar", [
        lambda e: e.activation(thcol[:, 0:1], csn[:, 0:1], AF.Exp,
                               scale=cvals[:, CV_THS:CV_THS + 1]),
        lambda e: e.activation(thcol[:, 1:2], csn[:, 0:1], AF.Exp,
                               bias=cvals[:, CV_THB:CV_THB + 1],
                               scale=cvals[:, CV_THS:CV_THS + 1]),
        lambda e: e.activation(csS[:, 0:1], csn[:, 0:1], AF.Exp,
                               scale=cvals[:, CV_THS2:CV_THS2 + 1]),
        lambda e: e.activation(csS[:, 1:2], csn[:, 0:1], AF.Exp,
                               bias=cvals[:, CV_THB2:CV_THB2 + 1],
                               scale=cvals[:, CV_THS2:CV_THS2 + 1]),
    ], reads=[g("csn"), g("cvals")], writes=[g("thcol"), g("csS")], mark="se")
    B.step("vector", [
        lambda e: e.tensor_tensor(thcol[:, 0:1], thcol[:, 0:1], csS[:, 0:1],
                                  op=ALU.mult),
        lambda e: e.tensor_tensor(thcol[:, 1:2], thcol[:, 1:2], csS[:, 1:2],
                                  op=ALU.mult),
    ], reads=[g("csS")], writes=[g("thcol")], mark="ve")
    B.step("vector", [
        lambda e: e.tensor_scalar(altsign[:], csr[:, 0:1], 2.0, -1.0,
                                  op0=ALU.mult, op1=ALU.add),
    ], reads=[g("csr")], writes=[g("altsign")], mark="ve")
    B.step("gpsimd", [lambda e: e.iota(kdec[:], pattern=[[0, 8], [-128, 4]], base=T - 1,
                                       channel_multiplier=-1,
                                       allow_small_or_imprecise_dtypes=True),
                      lambda e: e.iota(kdecP[:], pattern=[[0, 8], [-128, 8]], base=TP - 1,
                                       channel_multiplier=-1,
                                       allow_small_or_imprecise_dtypes=True),
                      lambda e: e.iota(csc[:, 0:512], pattern=[[1, 512]], base=1,
                                       channel_multiplier=0,
                                       allow_small_or_imprecise_dtypes=True)],
           reads=[g("csc")], writes=[g("kdecr"), g("csc")], mark="gp")
    B.step("scalar",
           [(lambda h: (lambda e: e.activation(csS[:, 4 * h:4 * h + 4],
                                               kdec[:, 4 * h:4 * h + 4], AF.Exp,
                                               scale=cvals[:, CV_LNG2 + h:CV_LNG2 + h + 1])))(h)
            for h in range(NH)] +
           [(lambda h: (lambda e: e.activation(csC[:, 8 * h:8 * h + 8],
                                               kdecP[:, 8 * h:8 * h + 8], AF.Exp,
                                               scale=cvals[:, CV_LNG2 + h:CV_LNG2 + h + 1])))(h)
            for h in range(NH)] +
           [(lambda h: (lambda e: e.activation(kdec[:, 4 * h:4 * h + 4],
                                               kdec[:, 4 * h:4 * h + 4], AF.Exp,
                                               scale=cvals[:, CV_LNG + h:CV_LNG + h + 1])))(h)
            for h in range(NH)] +
           [(lambda h: (lambda e: e.activation(kdecP[:, 8 * h:8 * h + 8],
                                               kdecP[:, 8 * h:8 * h + 8], AF.Exp,
                                               scale=cvals[:, CV_LNG + h:CV_LNG + h + 1])))(h)
            for h in range(NH)],
           reads=[g("kdecr"), g("csc"), g("cvals")],
           writes=[g("kdec"), g("csS"), g("csC")], mark="se")
    B.step("vector", [
        lambda e: e.tensor_tensor(kdec[:], kdec[:], csS[:, 0:32], op=ALU.mult),
        lambda e: e.tensor_tensor(kdecP[:], kdecP[:], csC[:, 0:64], op=ALU.mult),
    ], reads=[g("csS"), g("csC"), g("kdec")], writes=[g("kdec")], mark="ve")
    gqb, _ = af32(512)
    B.step("vector", [lambda e: e.tensor_copy(gqb[:, 0:512], csc[:, 0:512])],
           reads=[g("csc")], writes=[g("gqb")], mark="ve")
    dma("sync", "ds", gnwc[:], d_gnw, writes=[g("gnw")])
    dma("sync", "ds", gnbc[:], d_gnb, writes=[g("gnb")])
    dma("sync", "ds", posb[:], d_posb, writes=[g("posb")])
    dma("sync", "ds", bgc[:], d_bgc, writes=[g("bgc")])
    dma("gpsimd", "dg", pfl[:], d_pflag, writes=[g("pfl")])
    for n, blk in [("q", AUG["wq"]), ("k", AUG["wk"]), ("v", AUG["wv"])]:
        dma("gpsimd", "dg", aug_all[1:2, blk:blk + HID], d_b[n], writes=[g("augw" + n)])
    dma("gpsimd", "dg", aug_all[0:1, AUG["wo"]:AUG["wo"] + HID], d_b["o"],
        writes=[g("augwo")])
    for n in WN:
        for eh in range(8):
            dma("gpsimd", "dc", d_wbf[n][eh].rearrange("dc p m -> (dc p) m"),
                d_w[n][:, eh * 256:(eh + 1) * 256],
                writes=[g("scr_" + n)])
    dma("gpsimd", "dc", d_wgTbf.rearrange("dc p m -> (dc p) m"), d_wgT,
        writes=[g("scr_wgT")])
    sqt, _ = abf(T)
    e0, _ = abf(T)
    e1, _ = abf(T)
    e2, _ = abf(T)
    e3, _ = abf(T)
    gqs, gqs_h = abf(T)
    mtmp, mtmp_h = abf(T)
    for h in range(NH):
        for ss in range(4):
            B.step("gpsimd", [(lambda sv: (lambda e: e.iota(
                csc[:, 0:T], pattern=[[1, T]], base=-128 * sv,
                channel_multiplier=-1, allow_small_or_imprecise_dtypes=True)))(ss)],
                reads=[g("csc")], writes=[g("csc")], mark="gp")
            B.step("scalar", [
                (lambda hv: (lambda e: e.activation(
                    csn[:, 0:T], csc[:, 0:T], AF.Exp,
                    scale=cvals[:, CV_LNG + hv:CV_LNG + hv + 1])))(h),
                (lambda hv: (lambda e: e.activation(
                    csS[:, 0:T], csc[:, 0:T], AF.Exp,
                    scale=cvals[:, CV_LNG2 + hv:CV_LNG2 + hv + 1])))(h),
            ], reads=[g("csc"), g("cvals")], writes=[g("csn"), g("csS")],
                mark="se")
            B.step("vector", [lambda e: e.tensor_tensor(
                csn[:, 0:T], csn[:, 0:T], csS[:, 0:T], op=ALU.mult)],
                reads=[g("csS")], writes=[g("csn")], mark="ve")
            B.step("gpsimd", [(lambda sv: (lambda e: e.affine_select(
                mtmp, csn[:, 0:T], pattern=[[1, T]],
                compare_op=ALU.is_ge, fill=0.0, base=-128 * sv,
                channel_multiplier=-1)))(ss)],
                reads=[g("csn")], writes=[g("mtmp")], mark="gp")
            dma("sync", "ds", d_mask[h, ss], mtmp, reads=[g("mtmp")],
                writes=[g("d_mask")])


    def stats_mms(xb, xbbuf, W, bs, bq):
        ntg = W // 512
        for dc in range(16):
            sq_aps = [sqt[:, 0:512], e0[:, 0:512]]
            B.step("scalar", [(lambda d, t: (lambda e: e.activation(
                sq_aps[t], xb[:, d * W + t * 512:d * W + (t + 1) * 512],
                AF.Square)))(dc, tg) for tg in range(ntg)],
                reads=[xbbuf], writes=[g("sqt"), g("e0")], mark="se")
            ems = []
            for tg in range(ntg):
                ems.append((lambda d, t: (lambda e: e.matmul(
                    bk(bs + t, parts=1), ones_colbf[:],
                    xb[:, d * W + t * 512:d * W + (t + 1) * 512],
                    start=(d == 0), stop=(d == 15))))(dc, tg))
                ems.append((lambda d, t, sa=sq_aps[tg]: (lambda e: e.matmul(
                    bk(bq + t, parts=1), ones_colbf[:], sa,
                    start=(d == 0), stop=(d == 15))))(dc, tg))
            B.step("tensor", ems, reads=[xbbuf, g("sqt"), g("e0"), g("const1")],
                   writes=[pb[bs + t] for t in range(ntg)] +
                          [pb[bq + t] for t in range(ntg)], mark="te")

    def strip_chain(W, bs, bq, flagged=False, want_aug=True, div=HID,
                    r2_mul=False, r2_save=False, want_rcols=False):
        for tg in range(W // 512):
            c0 = tg * 512
            B.step("scalar", [
                (lambda t: (lambda e: e.copy(s_sum[0:1, :], bk(bs + t, parts=1))))(tg),
                (lambda t: (lambda e: e.copy(s_sq[0:1, :], bk(bq + t, parts=1))))(tg),
            ], reads=[pb[bs + tg], pb[bq + tg]], writes=[g("strip")], mark="se")
            B.step("vector", [lambda e: e.tensor_scalar(
                s_mean[0:1, :], s_sum[0:1, :], 1.0 / div, None, op0=ALU.mult)],
                reads=[g("strip")], writes=[g("strip")], mark="ve")
            B.step("scalar", [lambda e: e.activation(s_tmp[0:1, :], s_mean[0:1, :],
                                                     AF.Square)],
                   reads=[g("strip")], writes=[g("strip")], mark="se")
            ems = [lambda e: e.scalar_tensor_tensor(
                s_sq[0:1, :], s_sq[0:1, :], 1.0 / div, s_tmp[0:1, :],
                op0=ALU.mult, op1=ALU.subtract)]
            if r2_mul:
                ems.append((lambda c: (lambda e: e.tensor_tensor(
                    s_t3[0:1, :], r2row[0:1, c:c + 512], r2row[0:1, c:c + 512],
                    op=ALU.mult)))(c0))
                ems.append(lambda e: e.tensor_tensor(
                    s_sq[0:1, :], s_sq[0:1, :], s_t3[0:1, :], op=ALU.mult))
            B.step("vector", ems, reads=[g("strip"), g("r2row")],
                   writes=[g("strip")], mark="ve")
            B.step("scalar", [lambda e: e.activation(
                s_tmp[0:1, :], s_sq[0:1, :], AF.Sqrt,
                bias=cvals[0:1, CV_EPS:CV_EPS + 1])],
                reads=[g("strip"), g("cvals")], writes=[g("strip")], mark="se")
            B.step("vector", [lambda e: e.reciprocal(s_sum[0:1, :],
                                                     s_tmp[0:1, :])],
                   reads=[g("strip")], writes=[g("strip")], mark="ve")
            B.step("scalar", [lambda e: e.activation(s_r[0:1, :], s_sum[0:1, :],
                                                     AF.Identity, bias=0.0)],
                   reads=[g("strip")], writes=[g("strip")], mark="se")
            ems = []
            if want_aug:
                ems.append((lambda c: (lambda e: e.tensor_scalar(
                    mrows[0:1, c:c + 512], s_mean[0:1, :], -1.0, None,
                    op0=ALU.mult)))(c0))
                if flagged:
                    ems.append((lambda c: (lambda e: e.tensor_tensor(
                        s_t3[0:1, :], s_tmp[0:1, :], pfl[0:1, c:c + 512],
                        op=ALU.mult)))(c0))
                    ems.append((lambda c: (lambda e: e.tensor_tensor(
                        mrows[0:1, c:c + 512], mrows[0:1, c:c + 512],
                        pfl[0:1, c:c + 512], op=ALU.mult)))(c0))
                else:
                    ems.append(lambda e: e.tensor_copy(s_t3[0:1, :], s_tmp[0:1, :]))
            ems.append((lambda c: (lambda e: e.tensor_copy(
                bcm[0:1, c:c + 512], s_r[0:1, :])))(c0))
            ems.append((lambda c: (lambda e: e.tensor_scalar(
                bcm2[0:1, c:c + 512], s_r[0:1, :], 1.0 / 16.0, None,
                op0=ALU.mult)))(c0))
            if r2_mul:
                ems.append((lambda c: (lambda e: e.tensor_tensor(
                    s_r[0:1, :], s_r[0:1, :], r2row[0:1, c:c + 512],
                    op=ALU.mult)))(c0))
                ems.append((lambda c: (lambda e: e.scalar_tensor_tensor(
                    s_tmp[0:1, :], s_mean[0:1, :], -1.0, s_r[0:1, :],
                    op0=ALU.mult, op1=ALU.mult)))(c0))
                ems.append((lambda c: (lambda e: e.tensor_copy(
                    bcm[0:1, c:c + 512], s_r[0:1, :])))(c0))
                ems.append((lambda c: (lambda e: e.tensor_copy(
                    bcm2[0:1, c:c + 512], s_tmp[0:1, :])))(c0))
            B.step("vector", ems,
                   reads=[g("strip"), g("pfl"), g("r2sq"), g("r2row")],
                   writes=[g("strip"), g("mrows"), g("bcm"), g("bcm2")], mark="ve")
            if want_aug:
                dma("gpsimd", "dg", mrows[1:2, c0:c0 + 512], s_t3[0:1, :],
                    reads=[g("strip")], writes=[g("mrows")])
            if r2_save:
                B.step("vector", [(lambda c: (lambda e: e.tensor_copy(
                    r2row[0:1, c:c + 512], s_r[0:1, :])))(c0)],
                    reads=[g("strip")], writes=[g("r2row")], mark="ve")
            if want_rcols:
                for ts in range(4):
                    b = 4 + (ts % 3)
                    B.step("tensor", [(lambda t, bb: (lambda e: e.transpose(
                        bk(bb, width=1), s_r[0:1, t * 128:(t + 1) * 128],
                        identF[0:1, 0:1])))(ts, b)],
                        reads=[g("strip"), g("ident")], writes=[pb[b]], mark="te")
                    B.step("scalar", [(lambda t, bb, d0=tg * 4: (lambda e: e.copy(
                        rcols[:, d0 + t:d0 + t + 1], bk(bb, width=1))))(ts, b)],
                        reads=[pb[b]], writes=[g("rcols")], mark="se")

    def bcast(row_ap, row_buf, W, bankidx, out_ap, out_buf, parts_k=1, st_ap=None,
              st_buf=None):
        stat = st_ap if st_ap is not None else ones_bc[:]
        ems = []
        for tg in range(W // 512):
            ems.append((lambda t, sa=stat: (lambda e: e.matmul(
                bk(bankidx + t), sa, row_ap[0:parts_k, t * 512:(t + 1) * 512],
                start=True, stop=True)))(tg))
        B.step("tensor", ems,
               reads=[row_buf, g("const1")] + ([st_buf] if st_buf else []),
               writes=[pb[bankidx + t] for t in range(W // 512)], mark="te")
        B.step("scalar", [(lambda t: (lambda e: e.copy(
            out_ap[:, t * 512:(t + 1) * 512], bk(bankidx + t))))(tg)
            for tg in range(W // 512)],
            reads=[pb[bankidx + t] for t in range(W // 512)],
            writes=[out_buf], mark="se")

    def cs_tiles(dst, dstbuf, W, iota_base, use_posb, folds):
        for hf in range(2):
            for sub in range(W // 512):
                o = sub * 512
                B.step("gpsimd", [(lambda ib: (lambda e: e.iota(
                    csc[:, 0:512], pattern=[[1, 512]], base=ib, channel_multiplier=0,
                    allow_small_or_imprecise_dtypes=True)))(iota_base + o)],
                    reads=[g("csc")], writes=[g("csc")], mark="gp")
                ems = []
                if use_posb:
                    ems.append(lambda e: e.tensor_scalar(csc[:, 0:512], csc[:, 0:512],
                                                         posb[:, 0:1], None,
                                                         op0=ALU.add))
                ems += [
                    (lambda h2: (lambda e: e.tensor_scalar(
                        csc[:, 0:512], csc[:, 0:512], thcol[:, h2:h2 + 1], None,
                        op0=ALU.mult)))(hf),
                    lambda e: e.tensor_scalar(csn[:, 0:512], csc[:, 0:512], INV_2PI,
                                              None, op0=ALU.mult),
                    lambda e: e.tensor_copy(csi_t[:, 0:512], csn[:, 0:512]),
                    lambda e: e.tensor_copy(csn[:, 0:512], csi_t[:, 0:512]),
                    lambda e: e.scalar_tensor_tensor(csr[:, 0:512], csn[:, 0:512], -C1,
                                                     csc[:, 0:512], op0=ALU.mult,
                                                     op1=ALU.add),
                    lambda e: e.scalar_tensor_tensor(csr[:, 0:512], csn[:, 0:512], -C2,
                                                     csr[:, 0:512], op0=ALU.mult,
                                                     op1=ALU.add),
                    lambda e: e.scalar_tensor_tensor(csr[:, 0:512], csn[:, 0:512], -C3,
                                                     csr[:, 0:512], op0=ALU.mult,
                                                     op1=ALU.add),
                    lambda e: e.tensor_single_scalar(csc[:, 0:512], csr[:, 0:512], PI,
                                                     op=ALU.is_gt),
                    lambda e: e.scalar_tensor_tensor(csr[:, 0:512], csc[:, 0:512], N2PI,
                                                     csr[:, 0:512], op0=ALU.mult,
                                                     op1=ALU.add),
                    lambda e: e.tensor_single_scalar(csc[:, 0:512], csr[:, 0:512], -PI,
                                                     op=ALU.is_lt),
                    lambda e: e.scalar_tensor_tensor(csr[:, 0:512], csc[:, 0:512], -N2PI,
                                                     csr[:, 0:512], op0=ALU.mult,
                                                     op1=ALU.add),
                ]
                B.step("vector", ems, reads=[g("csc"), g("posb"), g("thcol")],
                       writes=[g("csc"), g("csn"), g("csr"), g("csi")], mark="ve")
                B.step("scalar", [lambda e: e.activation(csS[:, 0:512], csr[:, 0:512],
                                                         AF.Sin)],
                       reads=[g("csr")], writes=[g("csS")], mark="se")
                B.step("vector", [
                    lambda e: e.tensor_scalar(csn[:, 0:512], csr[:, 0:512], HALF_PI,
                                              None, op0=ALU.add),
                    lambda e: e.tensor_single_scalar(csc[:, 0:512], csn[:, 0:512], PI,
                                                     op=ALU.is_gt),
                    lambda e: e.scalar_tensor_tensor(csn[:, 0:512], csc[:, 0:512], N2PI,
                                                     csn[:, 0:512], op0=ALU.mult,
                                                     op1=ALU.add),
                    lambda e: e.tensor_scalar(csS[:, 0:512], csS[:, 0:512],
                                              altsign[:, 0:1], None, op0=ALU.mult),
                ], reads=[g("csr"), g("altsign")],
                    writes=[g("csn"), g("csc"), g("csS")], mark="ve")
                B.step("scalar", [lambda e: e.activation(csC[:, 0:512], csn[:, 0:512],
                                                         AF.Sin)],
                       reads=[g("csn")], writes=[g("csC")], mark="se")
                ems = []
                for fi, (tile, tilebuf) in enumerate(folds):
                    bb = fi * 4 * W + hf * 2 * W
                    ems.append((lambda b2, ta, o2=o: (lambda e: e.tensor_tensor(
                        dst[:, b2 + o2:b2 + o2 + 512], csC[:, 0:512],
                        ta[:, o2:o2 + 512], op=ALU.mult)))(bb, tile))
                    ems.append((lambda b2, ta, o2=o: (lambda e: e.tensor_tensor(
                        dst[:, b2 + W + o2:b2 + W + o2 + 512], csS[:, 0:512],
                        ta[:, o2:o2 + 512], op=ALU.mult)))(bb, tile))
                B.step("vector", ems,
                       reads=[g("csC"), g("csS")] + [fb for _, fb in folds],
                       writes=[dstbuf], mark="ve")

    def stream_w8(n, eh, slot_ap, slot_buf):
        return dma("sync", "ds", slot_ap.rearrange("p (dc m) -> p dc m", m=256),
                   d_wbf[n][eh].rearrange("dc p m -> p dc m"),
                   reads=[g("scr_" + n)], writes=[slot_buf])

    def wsum(n, eh, slot_ap, slot_buf, bankidx):
        ems = []
        for dc in range(16):
            ems.append((lambda d: (lambda e: e.matmul(
                bk(bankidx, parts=1, width=256), ones_colbf[:],
                slot_ap[:, d * 256:(d + 1) * 256],
                start=(d == 0), stop=(d == 15))))(dc))
        B.step("tensor", ems, reads=[slot_buf, g("const1")], writes=[pb[bankidx]],
               mark="te")
        B.step("scalar", [lambda e: e.copy(augw[0:1, 0:256],
                                           bk(bankidx, parts=1, width=256))],
               reads=[pb[bankidx]], writes=[g("augw")], mark="se")
        B.step("vector", [(lambda e2, nn: (lambda e: e.tensor_copy(
            aug_all[0:1, AUG[nn] + e2 * 256:AUG[nn] + (e2 + 1) * 256],
            augw[0:1, 0:256])))(eh, n)],
            reads=[g("augw")], writes=[g("augw" + n[1:])], mark="ve")

    def shift_epi(src_bank, Cap, Sap, csbuf, dst_ap, dstbuf):
        B.step("scalar", [lambda e: e.copy(e0[:, 0:T], bk(src_bank))],
               reads=[pb[src_bank]], writes=[g("e0")], mark="se")
        B.step("vector", [
            lambda e: e.stream_shuffle(e1[:, 0:T], e0[:, 0:T], mask=SHUF_MASK),
            (lambda ca: lambda e: e.tensor_tensor(e2[:, 0:T], e0[:, 0:T], ca,
                                                  op=ALU.mult))(Cap),
        ], reads=[g("e0"), csbuf], writes=[g("e1"), g("e2")], mark="ve")
        B.step("vector", [
            (lambda sa: lambda e: e.tensor_tensor(e3[:, 0:T], e1[:, 0:T], sa,
                                                  op=ALU.mult))(Sap),
            (lambda da: lambda e: e.tensor_tensor(da, e2[:, 0:T], e3[:, 0:T],
                                                  op=ALU.add))(dst_ap),
        ], reads=[g("e1"), g("e2"), csbuf], writes=[g("e3"), dstbuf], mark="ve")

    w8s = [abf(4096) for _ in range(2)]
    xbfP, xbfP_h = abf(16 * TP)
    cskP, cskP_h = abf(4 * TP)
    rkbP, rkbP_h = abf(TP)
    ktP, ktP_h = abf(2 * TP)
    ktmP, ktmP_h = abf(8 * 256)
    vtmP, vtmP_h = abf(8 * 256)

    for pc in range(NPC if stop_after >= 1 else 0):
        cols = slice(pc * TP, (pc + 1) * TP)
        for q4 in range(4):
            dma("gpsimd", "dg",
                xbfP.rearrange("p (dc w) -> p dc w", w=TP)[:, q4 * 4:(q4 + 1) * 4],
                d_xp[:, cols].rearrange("(dc p) w -> p dc w", p=128)[:, q4 * 4:(q4 + 1) * 4],
                reads=[g("xbfP")], writes=[g("xbfP")])
        if PH2_SUB < 1:
            continue
        stats_mms(xbfP, g("xbfP"), TP, 0, 2)
        strip_chain(TP, 0, 2, flagged=True, want_rcols=True)
        if PH2_SUB < 2:
            continue
        bcast(bcm, g("bcm"), TP, 4, rkbP, g("rkbP"))
        cs_tiles(cskP, g("cskP"), TP, pc * TP, False, [(rkbP, g("rkbP"))])
        if PH2_SUB < 3:
            continue
        for h in range(NH if PH2_SUB >= 3 else 0):
            slv = w8s[0]
            slk = w8s[1]
            stream_w8("wv", h, slv[0], g("w8_0"))
            if pc == 0:
                wsum("wv", h, slv[0], g("w8_0"), 6)
            for ts in range(8):
                b = ts % 4
                ems = []
                for dc in range(16):
                    ems.append((lambda d, t, oap=bk(b, width=256): (lambda e: e.matmul(
                        oap,
                        xbfP[:, d * TP + t * 128:d * TP + (t + 1) * 128],
                        slv[0][:, d * 256:(d + 1) * 256],
                        start=(d == 0), stop=False)))(dc, ts))
                ems.append((lambda t, hh, oap=bk(b, width=256): (lambda e: e.matmul(
                    oap, mrows[0:2, t * 128:(t + 1) * 128],
                    aug_all[0:2, AUG["wv"] + hh * 256:AUG["wv"] + (hh + 1) * 256],
                    start=False, stop=True)))(ts, h))
                B.step("tensor", ems,
                       reads=[g("xbfP"), g("w8_0"), g("mrows"), g("augwv")],
                       writes=[pb[b]], mark="te")
                B.step("scalar", [(lambda t, bb: (lambda e: e.activation(
                    vtmP[:, t * 256:(t + 1) * 256], bk(bb, width=256),
                    AF.Identity, bias=0.0, scale=rcols[:, t:t + 1])))(ts, b)],
                    reads=[pb[b], g("rcols")], writes=[g("vtmP")], mark="se")
            if PH2_SUB < 4:
                continue
            stream_w8("wk", h, slk[0], g("w8_1"))
            if pc == 0:
                wsum("wk", h, slk[0], g("w8_1"), 6)
            for m2 in range(2):
                bp = 3 if m2 == 0 else 5
                for tg in range(2):
                    ems = []
                    for dc in range(16):
                        ems.append((lambda d, t, m, oap=bk(bp + tg): (lambda e: e.matmul(
                            oap,
                            slk[0][:, d * 256 + m * 128:d * 256 + (m + 1) * 128],
                            xbfP[:, d * TP + t * 512:d * TP + (t + 1) * 512],
                            start=(d == 0), stop=False)))(dc, tg, m2))
                    ems.append((lambda t, m, hh, oap=bk(bp + tg): (lambda e: e.matmul(
                        oap,
                        aug_all[0:2, AUG["wk"] + hh * 256 + m * 128:
                                AUG["wk"] + hh * 256 + (m + 1) * 128],
                        mrows[0:2, t * 512:(t + 1) * 512],
                        start=False, stop=True)))(tg, m2, h))
                    B.step("tensor", ems,
                           reads=[g("xbfP"), g("w8_1"), g("mrows"), g("augwk")],
                           writes=[pb[bp + tg]], mark="te")
                    co = m2 * 2 * TP + tg * 512
                    shift_epi(bp + tg, cskP[:, co:co + 512],
                              cskP[:, co + TP:co + TP + 512], g("cskP"),
                              ktP[:, m2 * TP + tg * 512:m2 * TP + (tg + 1) * 512],
                              g("ktP"))
            if PH2_SUB < 5:
                continue
            for m2 in range(2):
                for ssb in range(8):
                    B.step("tensor", [(lambda m, s2: (lambda e: e.transpose(
                        psb[:, 0:128],
                        ktP[:, m * TP + s2 * 128:m * TP + (s2 + 1) * 128],
                        ident[:])))(m2, ssb)],
                        reads=[g("ktP"), g("ident")], writes=[g("psb")], mark="te")
                    B.step("scalar", [(lambda m, s2, hh: (lambda e: e.activation(
                        ktmP[:, s2 * 256 + m * 128:s2 * 256 + (m + 1) * 128],
                        psb[:, 0:128], AF.Identity, bias=0.0,
                        scale=kdecP[:, 8 * hh + s2:8 * hh + s2 + 1])))(m2, ssb, h)],
                        reads=[g("psb"), g("kdec")], writes=[g("ktmP")], mark="se")
            for dk2 in range(2):
                b = dk2
                ems = []
                for ssb in range(8):
                    ems.append((lambda s2, d2, oap=bk(b, width=256): (lambda e: e.matmul(
                        oap,
                        ktmP[:, s2 * 256 + d2 * 128:s2 * 256 + (d2 + 1) * 128],
                        vtmP[:, s2 * 256:(s2 + 1) * 256],
                        start=(s2 == 0), stop=(s2 == 7))))(ssb, dk2))
                B.step("tensor", ems, reads=[g("ktmP"), g("vtmP")],
                       writes=[pb[b]], mark="te")
                B.step("vector", [(lambda d2, bb, hh: (lambda e: e.scalar_tensor_tensor(
                    S_fp[:, hh * 512 + d2 * 256:hh * 512 + (d2 + 1) * 256],
                    S_fp[:, hh * 512 + d2 * 256:hh * 512 + (d2 + 1) * 256],
                    GTP[hh], bk(bb, width=256), op0=ALU.mult, op1=ALU.add)))(dk2, b, h)],
                    reads=[pb[b]], writes=[g("S_fp")], mark="ve")

    tap("ktmP", ktmP[:, 0:2048], [g("ktmP")])
    tap("vtmP", vtmP[:, 0:2048], [g("vtmP")])
    tap("ktP", ktP[:, 0:2048], [g("ktP")])
    barrier()
    for hh in (xbfP_h, cskP_h, rkbP_h, ktP_h, ktmP_h, vtmP_h):
        arb.freed(hh)
    tap("S_prior", S_fp[:], [g("S_fp")])

    xbf, xbf_h = abf(16 * T)
    csq, csq_h = abf(8 * T)
    rqb, rqb_h = abf(T)
    rkb, rkb_h = abf(T)
    r1bc, r1bc_h = af32(T)
    qt, qt_h = abf(2 * T)
    ktl, ktl_h = abf(2 * T)
    ktm, ktm_h = abf(4 * 256)
    vtm, vtm_h = abf(4 * 256)
    qhb, qhb_h = abf(2 * T)
    simb, simb_h = abf(4 * T)
    mskb, mskb_h = abf(4 * T)
    retb, retb_h = abf(16 * T)
    sgb, sgb_h = abf(16 * T)
    aob = sgb
    bct0, bct0_h = abf(T)
    bct1, bct1_h = abf(T)
    x32 = [af32(T) for _ in range(2)]
    yt32 = [af32(T) for _ in range(2)]

    def CSQ(m2, kind):
        return csq[:, (m2 * 2 + kind) * T:(m2 * 2 + kind + 1) * T]

    def CSK(m2, kind):
        return csq[:, 4 * T + (m2 * 2 + kind) * T:4 * T + (m2 * 2 + kind + 1) * T]

    n_ch = 0 if stop_after < 2 else (1 if stop_after == 2 else NCH)
    for c in range(n_ch):
        cols = slice(c * T, (c + 1) * T)
        for q4 in range(4):
            dma("gpsimd", "dg",
                xbf.rearrange("p (dc w) -> p dc w", w=T)[:, q4 * 4:(q4 + 1) * 4],
                d_x[:, cols].rearrange("(dc p) w -> p dc w", p=128)[:, q4 * 4:(q4 + 1) * 4],
                reads=[g("xbf")], writes=[g("xbf")])
        stats_mms(xbf, g("xbf"), T, 0, 1)
        strip_chain(T, 0, 1, want_rcols=True)
        bcast(bcm2, g("bcm2"), T, 4, rqb, g("rqb"))
        bcast(bcm, g("bcm"), T, 5, rkb, g("rkb"))
        B.step("tensor", [lambda e: e.matmul(bk(6), ones_bc[:], bcm[0:1, 0:T],
                                             start=True, stop=True)],
               reads=[g("bcm"), g("const1")], writes=[pb[6]], mark="te")
        B.step("scalar", [lambda e: e.copy(r1bc, bk(6))],
               reads=[pb[6]], writes=[g("r1bc")], mark="se")
        cs_tiles(csq, g("csq"), T, c * T, True, [(rqb, g("rqb")), (rkb, g("rkb"))])
        if c == 0:
            tap("csq", csq[:, 0:8 * T], [g("csq")])
            tap("thcol", thcol[:], [g("thcol")])
            tap("altsign", altsign[:], [g("altsign")])
            tap("csS", csS[:, 0:512], [g("csS")])
            tap("csC", csC[:, 0:512], [g("csC")])
            tap("rqb", rqb[:, 0:T], [g("rqb")])
            tap("rkb", rkb[:, 0:T], [g("rkb")])

        for h in range(NH):
            dma("sync", "ds",
                mskb[:, 0:4 * T].rearrange("p (s t) -> p s t", s=4),
                d_mask[h].rearrange("s p t -> p s t"),
                reads=[g("mskslot"), g("d_mask")], writes=[g("mskslot")])
            B.step("vector", [(lambda hh: (lambda e: e.tensor_copy(
                S_bf[:], S_fp[:, hh * 512:(hh + 1) * 512])))(h)],
                reads=[g("S_fp")], writes=[g("S_bf")], mark="ve")
            stream_w8("wv", h, w8s[0][0], g("w8_0"))
            for ts in range(4):
                b = ts
                ems = []
                for dc in range(16):
                    ems.append((lambda d, t, oap=bk(b, width=256): (lambda e: e.matmul(
                        oap,
                        xbf[:, d * T + t * 128:d * T + (t + 1) * 128],
                        w8s[0][0][:, d * 256:(d + 1) * 256],
                        start=(d == 0), stop=False)))(dc, ts))
                ems.append((lambda t, hh, oap=bk(b, width=256): (lambda e: e.matmul(
                    oap, mrows[0:2, t * 128:(t + 1) * 128],
                    aug_all[0:2, AUG["wv"] + hh * 256:AUG["wv"] + (hh + 1) * 256],
                    start=False, stop=True)))(ts, h))
                B.step("tensor", ems,
                       reads=[g("xbf"), g("w8_0"), g("mrows"), g("augwv")],
                       writes=[pb[b]], mark="te")
                B.step("scalar", [(lambda t, bb: (lambda e: e.activation(
                    vtm[:, t * 256:(t + 1) * 256], bk(bb, width=256),
                    AF.Identity, bias=0.0, scale=rcols[:, t:t + 1])))(ts, b)],
                    reads=[pb[b], g("rcols")], writes=[g("vtm")], mark="se")
            stream_w8("wq", h, w8s[1][0], g("w8_1"))
            if c == 0:
                wsum("wq", h, w8s[1][0], g("w8_1"), 4)
            for m2 in range(2):
                b = 3 + m2
                ems = []
                for dc in range(16):
                    ems.append((lambda d, m, oap=bk(b): (lambda e: e.matmul(
                        oap, w8s[1][0][:, d * 256 + m * 128:d * 256 + (m + 1) * 128],
                        xbf[:, d * T:(d + 1) * T],
                        start=(d == 0), stop=False)))(dc, m2))
                ems.append((lambda m, hh, oap=bk(b): (lambda e: e.matmul(
                    oap, aug_all[0:2, AUG["wq"] + hh * 256 + m * 128:
                                 AUG["wq"] + hh * 256 + (m + 1) * 128],
                    mrows[0:2, 0:T], start=False, stop=True)))(m2, h))
                B.step("tensor", ems,
                       reads=[g("xbf"), g("w8_1"), g("mrows"), g("augwq")],
                       writes=[pb[b]], mark="te")
                shift_epi(b, CSQ(m2, 0), CSQ(m2, 1), g("csq"),
                          qt[:, m2 * T:(m2 + 1) * T], g("qt"))
            stream_w8("wk", h, w8s[0][0], g("w8_0"))
            for m2 in range(2):
                b = 0 + m2
                ems = []
                for dc in range(16):
                    ems.append((lambda d, m, oap=bk(b): (lambda e: e.matmul(
                        oap, w8s[0][0][:, d * 256 + m * 128:d * 256 + (m + 1) * 128],
                        xbf[:, d * T:(d + 1) * T],
                        start=(d == 0), stop=False)))(dc, m2))
                ems.append((lambda m, hh, oap=bk(b): (lambda e: e.matmul(
                    oap, aug_all[0:2, AUG["wk"] + hh * 256 + m * 128:
                                 AUG["wk"] + hh * 256 + (m + 1) * 128],
                    mrows[0:2, 0:T], start=False, stop=True)))(m2, h))
                B.step("tensor", ems,
                       reads=[g("xbf"), g("w8_0"), g("mrows"), g("augwk")],
                       writes=[pb[b]], mark="te")
                shift_epi(b, CSK(m2, 0), CSK(m2, 1), g("csq"),
                          ktl[:, m2 * T:(m2 + 1) * T], g("ktl"))
            for m2 in range(2):
                for ssb in range(4):
                    B.step("tensor", [(lambda m, s2: (lambda e: e.transpose(
                        psb[:, 0:128],
                        ktl[:, m * T + s2 * 128:m * T + (s2 + 1) * 128],
                        ident[:])))(m2, ssb)],
                        reads=[g("ktl"), g("ident")], writes=[g("psb")], mark="te")
                    B.step("scalar", [(lambda m, s2, hh: (lambda e: e.activation(
                        ktm[:, s2 * 256 + m * 128:s2 * 256 + (m + 1) * 128],
                        psb[:, 0:128], AF.Identity, bias=0.0,
                        scale=kdec[:, 4 * hh + s2:4 * hh + s2 + 1])))(m2, ssb, h)],
                        reads=[g("psb"), g("kdec")], writes=[g("ktm")], mark="se")
            B.step("scalar", [(lambda hh: (lambda e: e.activation(
                gqs, gqb[:, 0:512], AF.Exp,
                scale=cvals[:, CV_LNG + hh:CV_LNG + hh + 1])))(h)],
                reads=[g("gqb"), g("cvals")], writes=[g("gqs")], mark="se")
            B.step("vector", [
                (lambda m: (lambda e: e.tensor_tensor(
                    qhb[:, m * T:(m + 1) * T], qt[:, m * T:(m + 1) * T],
                    gqs, op=ALU.mult)))(m2)
                for m2 in range(2)],
                reads=[g("qt"), g("gqs")], writes=[g("qhb")], mark="ve")
            for ssb in range(4):
                b = 3 + (ssb % 2)
                ems = []
                for dh2 in range(2):
                    ems.append((lambda s2, d2, oap=bk(b): (lambda e: e.matmul(
                        oap, ktl[:, d2 * T + s2 * 128:d2 * T + (s2 + 1) * 128],
                        qt[:, d2 * T:(d2 + 1) * T],
                        start=(d2 == 0), stop=(d2 == 1))))(ssb, dh2))
                B.step("tensor", ems, reads=[g("ktl"), g("qt")], writes=[pb[b]],
                       mark="te")
                B.step("vector", [(lambda s2, bb: (lambda e: e.tensor_tensor(
                    simb[:, s2 * T:(s2 + 1) * T], bk(bb),
                    mskb[:, s2 * T:(s2 + 1) * T],
                    op=ALU.mult)))(ssb, b)],
                    reads=[pb[b], g("mskslot")], writes=[g("simb")], mark="ve")
            for dvh in range(2):
                b = 0 if dvh == 0 else 2
                ems = []
                for ssb in range(4):
                    ems.append((lambda s2, dv, oap=bk(b): (lambda e: e.matmul(
                        oap, vtm[:, s2 * 256 + dv * 128:s2 * 256 + (dv + 1) * 128],
                        simb[:, s2 * T:(s2 + 1) * T],
                        start=(s2 == 0), stop=False)))(ssb, dvh))
                for dh2 in range(2):
                    ems.append((lambda d2, dv, oap=bk(b): (lambda e: e.matmul(
                        oap, S_bf[:, d2 * 256 + dv * 128:d2 * 256 + (dv + 1) * 128],
                        qhb[:, d2 * T:(d2 + 1) * T],
                        start=False, stop=(d2 == 1))))(dh2, dvh))
                B.step("tensor", ems, reads=[g("vtm"), g("simb"), g("S_bf"), g("qhb")],
                       writes=[pb[b]], mark="te")
            for dvh in range(2):
                b = 0 if dvh == 0 else 2
                B.step("scalar", [
                    (lambda bb, hh, dv: (lambda e: e.copy(
                        retb[:, (2 * hh + dv) * T:(2 * hh + dv + 1) * T], bk(bb))))(b, h, dvh),
                    (lambda bb: (lambda e: e.activation(
                        sqt[:, 0:T], bk(bb), AF.Square)))(b),
                ], reads=[pb[b]], writes=[g("retb"), g("sqt")], mark="se")
                B.step("tensor", [
                    (lambda hh, dv: (lambda e: e.matmul(
                        bk(5, parts=8), oneh8[:, hh * 8:hh * 8 + 8],
                        retb[:, (2 * hh + dv) * T:(2 * hh + dv + 1) * T],
                        start=(hh == 0 and dv == 0), stop=(hh == 7 and dv == 1))))(h, dvh),
                    (lambda hh, dv: (lambda e: e.matmul(
                        bk(6, parts=8), oneh8[:, hh * 8:hh * 8 + 8], sqt[:, 0:T],
                        start=(hh == 0 and dv == 0), stop=(hh == 7 and dv == 1))))(h, dvh),
                ], reads=[g("retb"), g("sqt"), g("oneh8")],
                    writes=[pb[5], pb[6]], mark="te")
            for dk2 in range(2):
                b = 1 if dk2 == 0 else 3
                ems = []
                for ssb in range(4):
                    ems.append((lambda s2, d2, oap=bk(b, width=256): (lambda e: e.matmul(
                        oap,
                        ktm[:, s2 * 256 + d2 * 128:s2 * 256 + (d2 + 1) * 128],
                        vtm[:, s2 * 256:(s2 + 1) * 256],
                        start=(s2 == 0), stop=(s2 == 3))))(ssb, dk2))
                B.step("tensor", ems, reads=[g("ktm"), g("vtm")], writes=[pb[b]],
                       mark="te")
                B.step("vector", [(lambda d2, bb, hh: (lambda e: e.scalar_tensor_tensor(
                    S_fp[:, hh * 512 + d2 * 256:hh * 512 + (d2 + 1) * 256],
                    S_fp[:, hh * 512 + d2 * 256:hh * 512 + (d2 + 1) * 256],
                    GT[hh], bk(bb, width=256), op0=ALU.mult, op1=ALU.add)))(dk2, b, h)],
                    reads=[pb[b]], writes=[g("S_fp")], mark="ve")
        if c == 0:
            tap("qt0", qt[:, 0:2 * T], [g("qt")])
            tap("vtm0", vtm[:, 0:1024], [g("vtm")])
            tap("retb", retb[:, 0:16 * T], [g("retb")])

        for eh in range(NH):
            stream_w8("wg", eh, w8s[0][0], g("w8_0"))
            if c == 0:
                wsum("wg", eh, w8s[0][0], g("w8_0"), 4)
            for m2 in range(2):
                m = 2 * eh + m2
                b = m2
                ems = []
                for dc in range(16):
                    ems.append((lambda d, mm2, oap=bk(b): (lambda e: e.matmul(
                        oap, w8s[0][0][:, d * 256 + mm2 * 128:d * 256 + (mm2 + 1) * 128],
                        xbf[:, d * T:(d + 1) * T],
                        start=(d == 0), stop=False)))(dc, m2))
                ems.append((lambda mm, oap=bk(b): (lambda e: e.matmul(
                    oap, aug_all[0:1, AUG["wg"] + mm * 128:AUG["wg"] + (mm + 1) * 128],
                    mrows[0:1, 0:T], start=False, stop=True)))(m))
                B.step("tensor", ems,
                       reads=[g("xbf"), g("w8_0"), g("mrows"), g("augwg")],
                       writes=[pb[b]], mark="te")
                B.step("vector", [(lambda bb: (lambda e: e.tensor_tensor(
                    csr[:, 0:T], bk(bb), r1bc, op=ALU.mult)))(b)],
                    reads=[pb[b], g("r1bc")], writes=[g("csr")], mark="ve")
                B.step("scalar", [(lambda mm: (lambda e: e.activation(
                    sgb[:, mm * T:(mm + 1) * T], csr[:, 0:T], AF.Silu,
                    bias=bgc[:, mm:mm + 1], scale=1.0)))(m)],
                    reads=[g("csr"), g("bgc")], writes=[g("sgb")], mark="se")

        GA = grows[0:8, 0:T]
        GB = grows[0:8, T:2 * T]
        GC = grows[0:8, 2 * T:3 * T]
        B.step("scalar", [
            lambda e: e.activation(GA, bk(5, parts=8), AF.Identity,
                                   bias=0.0, scale=cvals[0:8, CV_256:CV_256 + 1]),
            lambda e: e.activation(GB, bk(6, parts=8), AF.Identity,
                                   bias=0.0, scale=cvals[0:8, CV_256:CV_256 + 1]),
        ], reads=[pb[5], pb[6], g("cvals")], writes=[g("grows")], mark="se")
        B.step("scalar", [lambda e: e.activation(GC, GA, AF.Square)],
               reads=[g("grows")], writes=[g("grows")], mark="se")
        B.step("vector", [lambda e: e.tensor_tensor(GB, GB, GC, op=ALU.subtract)],
               reads=[g("grows")], writes=[g("grows")], mark="ve")
        B.step("scalar", [lambda e: e.activation(GC, GB, AF.Sqrt,
                                                 bias=cvals[0:8, CV_EPS:CV_EPS + 1])],
               reads=[g("grows"), g("cvals")], writes=[g("grows")], mark="se")
        B.step("vector", [
            lambda e: e.reciprocal(gtmp[0:8, 0:T], GC),
        ], reads=[g("grows"), g("gtmp")], writes=[g("gtmp")], mark="ve")
        B.step("scalar", [lambda e: e.activation(GB, gtmp[0:8, 0:T],
                                                 AF.Identity, bias=0.0)],
               reads=[g("gtmp")], writes=[g("grows")], mark="se")
        B.step("vector", [
            lambda e: e.scalar_tensor_tensor(GC, GA, -1.0, GB, op0=ALU.mult,
                                             op1=ALU.mult),
            lambda e: e.tensor_copy(gbf[0:8, 0:T], GB),
            lambda e: e.tensor_copy(gbf2[0:8, 0:T], GC),
        ], reads=[g("grows")], writes=[g("grows"), g("gbf")], mark="ve")
        for h in range(NH):
            B.step("tensor", [
                (lambda hh: (lambda e: e.matmul(
                    bk(4), oneh8T[0:8, hh * 128:hh * 128 + 128], gbf[0:8, 0:T],
                    start=True, stop=True)))(h),
                (lambda hh: (lambda e: e.matmul(
                    bk(5), oneh8T[0:8, hh * 128:hh * 128 + 128], gbf2[0:8, 0:T],
                    start=True, stop=True)))(h),
            ], reads=[g("gbf"), g("oneh8")], writes=[pb[4], pb[5]], mark="te")
            B.step("scalar", [lambda e: e.copy(bct0, bk(4)),
                              lambda e: e.copy(bct1, bk(5))],
                   reads=[pb[4], pb[5]], writes=[g("bct")], mark="se")
            for dvh in range(2):
                m = 2 * h + dvh
                B.step("vector", [
                    (lambda mm: (lambda e: e.tensor_tensor(
                        e2[:, 0:T], retb[:, mm * T:(mm + 1) * T], bct0,
                        op=ALU.mult)))(m),
                    (lambda mm: (lambda e: e.tensor_tensor(
                        e3[:, 0:T], e2[:, 0:T], bct1, op=ALU.add)))(m),
                    (lambda mm: (lambda e: e.tensor_scalar(
                        e2[:, 0:T], e3[:, 0:T], gnwc[:, mm:mm + 1],
                        gnbc[:, mm:mm + 1], op0=ALU.mult, op1=ALU.add)))(m),
                    (lambda mm: (lambda e: e.tensor_tensor(
                        retb[:, mm * T:(mm + 1) * T], e2[:, 0:T],
                        sgb[:, mm * T:(mm + 1) * T], op=ALU.mult)))(m),
                ], reads=[g("retb"), g("bct"), g("gnw"), g("gnb"), g("sgb")],
                    writes=[g("retb"), g("e2"), g("e3")], mark="ve")
        if c == 0:
            tap("gate", retb[:, 0:16 * T], [g("retb")])

        for eh in range(NH):
            stream_w8("wo", eh, w8s[1][0], g("w8_1"))
            for m2 in range(2):
                m = 2 * eh + m2
                b = m % 4
                ems = []
                for dc in range(16):
                    ems.append((lambda d, mm2, oap=bk(b): (lambda e: e.matmul(
                        oap, w8s[1][0][:, d * 256 + mm2 * 128:d * 256 + (mm2 + 1) * 128],
                        retb[:, d * T:(d + 1) * T],
                        start=(d == 0), stop=False)))(dc, m2))
                ems.append((lambda mm, oap=bk(b): (lambda e: e.matmul(
                    oap, aug_all[0:1, AUG["wo"] + mm * 128:AUG["wo"] + (mm + 1) * 128],
                    ones_bf[0:1, 0:T], start=False, stop=True)))(m))
                B.step("tensor", ems,
                       reads=[g("retb"), g("w8_1"), g("const1"), g("augwo")],
                       writes=[pb[b]], mark="te")
                B.step("scalar", [
                    (lambda bb, mm: (lambda e: e.copy(
                        aob[:, mm * T:(mm + 1) * T], bk(bb))))(b, m),
                    (lambda bb: (lambda e: e.activation(
                        sqt[:, 0:T], bk(bb), AF.Square)))(b),
                ], reads=[pb[b]], writes=[g("sgb"), g("sqt")], mark="se")
                B.step("tensor", [
                    (lambda mm: (lambda e: e.matmul(
                        bk(5, parts=1), ones_colbf[:], aob[:, mm * T:(mm + 1) * T],
                        start=(mm == 0), stop=(mm == 15))))(m),
                    (lambda mm: (lambda e: e.matmul(
                        bk(6, parts=1), ones_colbf[:], sqt[:, 0:T],
                        start=(mm == 0), stop=(mm == 15))))(m),
                ], reads=[g("sgb"), g("sqt"), g("const1")], writes=[pb[5], pb[6]],
                    mark="te")
        strip_chain(T, 5, 6, want_aug=False)
        B.step("vector", [
            lambda e: e.scalar_tensor_tensor(s_tmp[0:1, :], s_mean[0:1, :], -1.0,
                                             s_r[0:1, :], op0=ALU.mult, op1=ALU.mult),
            lambda e: e.tensor_copy(bcm2[0:1, 0:T], s_tmp[0:1, :]),
        ], reads=[g("strip")], writes=[g("strip"), g("bcm2")], mark="ve")
        bcast(bcm, g("bcm"), T, 4, bct0, g("bct"))
        bcast(bcm2, g("bcm2"), T, 5, bct1, g("bct"))
        for dc in range(16):
            sl = dc % 2
            dma("sync", "ds", x32[sl][0],
                d_x[dc * 128:(dc + 1) * 128, cols],
                reads=[g(f"x32_{sl}")], writes=[g(f"x32_{sl}")])
            B.step("vector", [
                (lambda d: (lambda e: e.tensor_tensor(
                    e2[:, 0:T], aob[:, d * T:(d + 1) * T], bct0, op=ALU.mult)))(dc),
                lambda e: e.tensor_tensor(e3[:, 0:T], e2[:, 0:T], bct1, op=ALU.add),
                (lambda s: (lambda e: e.tensor_tensor(
                    yt32[s][0], e3[:, 0:T], x32[s][0], op=ALU.add)))(sl),
            ], reads=[g("sgb"), g("bct"), g(f"x32_{sl}"), g(f"yt32_{sl}")],
                writes=[g("e2"), g("e3"), g(f"yt32_{sl}")], mark="ve")
            dma("sync", "ds", d_y[dc, :, cols], yt32[sl][0],
                reads=[g(f"yt32_{sl}")], writes=[g("d_y")])
        if c == 0:
            tap("y0", yt32[0][0], [g("yt32_0")])

    barrier()
    for hh in (xbf_h, csq_h, rqb_h, rkb_h, qt_h, ktl_h, ktm_h, vtm_h, qhb_h,
               simb_h, mskb_h, retb_h, sgb_h, bct0_h, bct1_h, gqs_h, mtmp_h):
        arb.freed(hh)
    arf.freed(r1bc_h)
    for a in x32 + yt32:
        arf.freed(a[1])

    ybf, ybf_h = abf(16 * TP)
    unb, unb_h = abf(16 * TP)
    ffns = [abf(TP) for _ in range(2)]
    ffnr = ffns
    rwbt, rwbt_h = abf(TP)
    pbt, pbt_h = abf(TP)
    bbt, bbt_h = abf(TP)
    fA, fA_h = abf(TP)
    fB, fB_h = abf(TP)
    wgt8, wgt8_h = abf(128)
    y32 = [af32(512) for _ in range(2)]
    o32 = [af32(512) for _ in range(2)]

    for hf4 in range(2 if stop_after >= 4 else 0):
        cols = slice(hf4 * TP, (hf4 + 1) * TP)
        for q4 in range(4):
            dma("gpsimd", "dg",
                ybf.rearrange("p (dc w) -> p dc w", w=TP)[:, q4 * 4:(q4 + 1) * 4],
                d_y[:, :, cols].rearrange("dc p w -> p dc w")[:, q4 * 4:(q4 + 1) * 4],
                reads=[g("ybf"), g("d_y")], writes=[g("ybf")])
        stats_mms(ybf, g("ybf"), TP, 0, 2)
        strip_chain(TP, 0, 2, r2_save=True, want_rcols=True)
        dma("sync", "ds", wgt8.rearrange("p (dc m) -> p dc m", m=8),
            d_wgTbf.rearrange("dc p m -> p dc m"),
            reads=[g("scr_wgT")], writes=[g("wgt8")])
        if hf4 == 0:
            ems = []
            for dc in range(16):
                ems.append((lambda d: (lambda e: e.matmul(
                    bk(2, parts=1, width=8), ones_colbf[:], wgt8[:, d * 8:(d + 1) * 8],
                    start=(d == 0), stop=(d == 15))))(dc))
            B.step("tensor", ems, reads=[g("wgt8"), g("const1")], writes=[pb[2]],
                   mark="te")
            B.step("scalar", [lambda e: e.copy(augw[0:1, 0:8],
                                               bk(2, parts=1, width=8))],
                   reads=[pb[2]], writes=[g("augw")], mark="se")
            B.step("vector", [lambda e: e.tensor_copy(aug8[0:1, 0:64],
                                                      augw[0:1, 0:64])],
                   reads=[g("augw")], writes=[g("aug8")], mark="ve")
        for tg in range(2):
            b = 2 + tg
            ems = []
            for dc in range(16):
                ems.append((lambda d, t, oap=bk(b, parts=8): (lambda e: e.matmul(
                    oap, wgt8[:, d * 8:(d + 1) * 8],
                    ybf[:, d * TP + t * 512:d * TP + (t + 1) * 512],
                    start=(d == 0), stop=False)))(dc, tg))
            ems.append((lambda t, oap=bk(b, parts=8): (lambda e: e.matmul(
                oap, aug8[0:1, 0:8], mrows[0:1, t * 512:(t + 1) * 512],
                start=False, stop=True)))(tg))
            B.step("tensor", ems, reads=[g("ybf"), g("wgt8"), g("aug8"), g("mrows")],
                   writes=[pb[b]], mark="te")
            B.step("scalar", [(lambda t, bb: (lambda e: e.copy(
                lgsb[0:8, t * 512:(t + 1) * 512], bk(bb, parts=8))))(tg, b)],
                reads=[pb[b]], writes=[g("lgsb")], mark="se")
        for tb in range(8):
            B.step("tensor", [(lambda t: (lambda e: e.transpose(
                psb[:, 0:8], lgsb[0:8, t * 128:(t + 1) * 128],
                ident[0:8, 0:8])))(tb)],
                reads=[g("lgsb"), g("ident")], writes=[g("psb")], mark="te")
            B.step("scalar", [(lambda t: (lambda e: e.copy(
                lgt[:, t * 8:(t + 1) * 8], psb[:, 0:8])))(tb)],
                reads=[g("psb")], writes=[g("lgt")], mark="se")
        if tb == 7:
            pass
        B.step("vector", [lambda e: e.tensor_copy(lgtf[:], lgt[:])],
               reads=[g("lgt")], writes=[g("lgtf")], mark="ve")
        for tb in range(8):
            B.step("vector", [
                (lambda t: (lambda e: e.max(out=m8t[:], in_=lgtf[:, t * 8:(t + 1) * 8])))(tb),
                lambda e: e.tensor_tensor(m8t[:, 2:3], m8t[:, 1:2], m8t[:, 0:1],
                                          op=ALU.subtract),
            ], reads=[g("lgtf")], writes=[g("m8t")], mark="ve")
            B.step("scalar", [
                (lambda t: (lambda e: e.activation(
                    m8t[:, 3:4], m8t[:, 2:3], AF.Exp, scale=rcols[:, t:t + 1])))(tb),
                lambda e: e.activation(m8t[:, 4:5], m8t[:, 3:4], AF.Identity,
                                       bias=cvals[:, CV_ONE:CV_ONE + 1]),
            ], reads=[g("m8t"), g("rcols"), g("cvals")], writes=[g("m8t")],
                mark="se")
            B.step("vector", [
                lambda e: e.reciprocal(m8t[:, 5:6], m8t[:, 4:5]),
            ], reads=[g("m8t")], writes=[g("m8t")], mark="ve")
            B.step("scalar", [(lambda t: (lambda e: e.activation(
                rwc[:, t:t + 1], m8t[:, 4:5], AF.Identity, bias=0.0,
                scale=m8t[:, 5:6])))(tb)],
                reads=[g("m8t")], writes=[g("rwc")], mark="se")
        B.step("tensor", [lambda e: e.transpose(bk(2, parts=8, width=128),
                                                rwc[:, 0:8], identF[:])],
               reads=[g("rwc"), g("ident")], writes=[pb[2]], mark="te")
        B.step("scalar", [lambda e: e.copy(rw8[0:8, 0:128],
                                           bk(2, parts=8, width=128))],
               reads=[pb[2]], writes=[g("rw8")], mark="se")
        B.step("vector", [lambda e: e.tensor_copy(lgsb[0:8, 0:128], rw8[0:8, 0:128])],
               reads=[g("rw8")], writes=[g("lgsb")], mark="ve")
        for tb in range(8):
            B.step("tensor", [(lambda t: (lambda e: e.matmul(
                bk(2, width=128), oneh8T[0:8, t * 128:t * 128 + 128],
                lgsb[0:8, 0:128], start=True, stop=True)))(tb)],
                reads=[g("lgsb"), g("oneh8")], writes=[pb[2]], mark="te")
            B.step("scalar", [(lambda t: (lambda e: e.copy(
                rwbt[:, t * 128:(t + 1) * 128], bk(2, width=128))))(tb)],
                reads=[pb[2]], writes=[g("rwbt")], mark="se")
        for eh in range(NH):
            stream_w8("wupT", eh, w8s[0][0], g("w8_0"))
            if hf4 == 0:
                wsum("wupT", eh, w8s[0][0], g("w8_0"), 2)
            for m2 in range(2):
                m = 2 * eh + m2
                bb0 = 3
                ems = []
                for dc in range(16):
                    for tg in range(2):
                        ems.append((lambda d, t, mm2: (lambda e: e.matmul(
                            bk(bb0 + t),
                            w8s[0][0][:, d * 256 + mm2 * 128:d * 256 + (mm2 + 1) * 128],
                            ybf[:, d * TP + t * 512:d * TP + (t + 1) * 512],
                            start=(d == 0), stop=False)))(dc, tg, m2))
                for tg in range(2):
                    ems.append((lambda t, mm: (lambda e: e.matmul(
                        bk(bb0 + t),
                        aug_all[0:1, AUG["wupT"] + mm * 128:AUG["wupT"] + (mm + 1) * 128],
                        mrows[0:1, t * 512:(t + 1) * 512],
                        start=False, stop=True)))(tg, m))
                B.step("tensor", ems,
                       reads=[g("ybf"), g("w8_0"), g("mrows"), g("augwupT")],
                       writes=[pb[bb0], pb[bb0 + 1]], mark="te")
                B.step("scalar", [(lambda t, mm, bb: (lambda e: e.activation(
                    unb[:, mm * TP + t * 512:mm * TP + (t + 1) * 512],
                    bk(bb + t), AF.Relu)))(tg, m, bb0) for tg in range(2)],
                    reads=[pb[bb0], pb[bb0 + 1]], writes=[g("unb")], mark="se")
        for dc in range(16):
            B.step("vector", [(lambda d: (lambda e: e.tensor_tensor(
                ybf[:, d * TP:(d + 1) * TP], ybf[:, d * TP:(d + 1) * TP],
                rwbt, op=ALU.mult)))(dc)],
                reads=[g("ybf"), g("rwbt")], writes=[g("ybf")], mark="ve")
        B.step("vector", [lambda e: e.tensor_tensor(mrows[0:1, 0:TP], mrows[0:1, 0:TP],
                                                    rwbt[0:1, 0:TP], op=ALU.mult)],
               reads=[g("mrows"), g("rwbt")], writes=[g("mrows")], mark="ve")
        for eh in range(NH):
            stream_w8("wdnT", eh, w8s[0][0], g("w8_0"))
            stream_w8("we", eh, w8s[1][0], g("w8_1"))
            if hf4 == 0:
                wsum("we", eh, w8s[1][0], g("w8_1"), 2)
            for m2 in range(2):
                m = 2 * eh + m2
                bb0 = 3
                ems = []
                for oc in range(16):
                    for tg in range(2):
                        ems.append((lambda d, t, mm2: (lambda e: e.matmul(
                            bk(bb0 + t),
                            w8s[0][0][:, d * 256 + mm2 * 128:d * 256 + (mm2 + 1) * 128],
                            unb[:, d * TP + t * 512:d * TP + (t + 1) * 512],
                            start=(d == 0), stop=False)))(oc, tg, m2))
                for dc in range(16):
                    for tg in range(2):
                        ems.append((lambda d, t, mm2: (lambda e: e.matmul(
                            bk(bb0 + t),
                            w8s[1][0][:, d * 256 + mm2 * 128:d * 256 + (mm2 + 1) * 128],
                            ybf[:, d * TP + t * 512:d * TP + (t + 1) * 512],
                            start=False, stop=False)))(dc, tg, m2))
                for tg in range(2):
                    ems.append((lambda t, mm: (lambda e: e.matmul(
                        bk(bb0 + t),
                        aug_all[0:1, AUG["we"] + mm * 128:AUG["we"] + (mm + 1) * 128],
                        mrows[0:1, t * 512:(t + 1) * 512],
                        start=False, stop=True)))(tg, m))
                B.step("tensor", ems,
                       reads=[g("ybf"), g("unb"), g("w8_0"), g("w8_1"), g("mrows"),
                              g("augwe")],
                       writes=[pb[bb0], pb[bb0 + 1]], mark="te")
                fsl = m % 2
                B.step("scalar", [
                    (lambda t, bb, s: (lambda e: e.copy(
                        ffns[s][0][:, t * 512:(t + 1) * 512],
                        bk(bb + t))))(0, bb0, fsl),
                    (lambda bb: (lambda e: e.activation(
                        sqt[:, 0:T], bk(bb), AF.Square)))(bb0),
                    (lambda t, bb, s: (lambda e: e.copy(
                        ffns[s][0][:, t * 512:(t + 1) * 512],
                        bk(bb + t))))(1, bb0, fsl),
                    (lambda bb: (lambda e: e.activation(
                        e0[:, 0:T], bk(bb + 1), AF.Square)))(bb0),
                ], reads=[pb[bb0], pb[bb0 + 1]]
                    + [g(f"ffns{fsl}")],
                    writes=[g(f"ffns{fsl}"), g("sqt"), g("e0")], mark="se")
                dma("sync", "ds", d_ffn[m, :, cols], ffns[fsl][0],
                    reads=[g(f"ffns{fsl}")], writes=[g("d_ffn")])
                B.step("tensor", [
                    (lambda mm, s: (lambda e: e.matmul(
                        bk(0, parts=1), ones_colbf[:], ffns[s][0][:, 0:512],
                        start=(mm == 0), stop=(mm == 15))))(m, fsl),
                    (lambda mm: (lambda e: e.matmul(
                        bk(5, parts=1), ones_colbf[:], sqt[:, 0:T],
                        start=(mm == 0), stop=(mm == 15))))(m),
                    (lambda mm, s: (lambda e: e.matmul(
                        bk(1, parts=1), ones_colbf[:], ffns[s][0][:, 512:1024],
                        start=(mm == 0), stop=(mm == 15))))(m, fsl),
                    (lambda mm: (lambda e: e.matmul(
                        bk(6, parts=1), ones_colbf[:], e0[:, 0:T],
                        start=(mm == 0), stop=(mm == 15))))(m),
                ], reads=[g(f"ffns{fsl}"), g("sqt"), g("e0"), g("const1")],
                    writes=[pb[0], pb[1], pb[5], pb[6]], mark="te")
        strip_chain(TP, 0, 5, want_aug=False, r2_mul=True)
        bcast(bcm, g("bcm"), TP, 2, pbt, g("pbt"))
        bcast(bcm2, g("bcm2"), TP, 2, bbt, g("bbt"))
        for dc in range(16):
            for tg in range(2):
                sl = (2 * dc + tg) % 2
                c2 = slice(hf4 * TP + tg * 512, hf4 * TP + (tg + 1) * 512)
                dma("sync", "ds", y32[sl][0], d_y[dc, :, c2],
                    reads=[g(f"y32_{sl}"), g("d_y")], writes=[g(f"y32_{sl}")])
                dma("sync", "ds", ffnr[sl][0][:, 0:512], d_ffn[dc, :, c2],
                    reads=[g(f"ffns{sl}"), g("d_ffn")], writes=[g(f"ffns{sl}")])
                B.step("vector", [
                    (lambda s, t: (lambda e: e.tensor_tensor(
                        fA[:, 0:512], ffnr[s][0][:, 0:512],
                        pbt[:, t * 512:(t + 1) * 512], op=ALU.mult)))(sl, tg),
                    (lambda t: (lambda e: e.tensor_tensor(
                        fB[:, 0:512], fA[:, 0:512],
                        bbt[:, t * 512:(t + 1) * 512], op=ALU.add)))(tg),
                    (lambda s: (lambda e: e.tensor_tensor(
                        o32[s][0], fB[:, 0:512], y32[s][0], op=ALU.add)))(sl),
                ], reads=[g(f"ffns{sl}"), g("pbt"), g("bbt"), g(f"y32_{sl}"),
                          g(f"o32_{sl}")],
                    writes=[g("fA"), g("fB"), g(f"o32_{sl}")], mark="ve")
                dma("sync", "ds", d_out[dc, :, c2], o32[sl][0],
                    reads=[g(f"o32_{sl}")])

    if stop_after >= 4:
        tap("ybf4", ybf[:, 0:2 * TP], [g("ybf")])
        tap("lgtf", lgtf[:], [g("lgtf")])
        tap("rwc", rwc[:, 0:8], [g("rwc")])
        tap("rwbt", rwbt[:], [g("rwbt")])
        tap("pbt", pbt[:], [g("pbt")])
        tap("bbt", bbt[:], [g("bbt")])
        tap("r2row", r2row[0:1, :], [g("r2row")])
        tap("unb4", unb[:, 0:2 * TP], [g("unb")])
        tap("mrows4", mrows[0:2, 0:TP], [g("mrows")])
        tap("o32l", o32[1][0][:], [g("o32_1")])

    B.step("sync", [lambda e: e.nop()],
           extra=[(s, v, None) for s, v in B.cnt.items()
                  if s[:2] in ("ds", "dg", "dc") and v > 0])

    globals()["LAST_B"] = B
    B.check()
    with nc.Block() as block:
        B.emit_all(block)
    ctx.close()
    return nc


_CACHE = {}


STOP_AFTER = 4
PH2SUB = 5


def _get_nc():
    key = ("nc", STOP_AFTER, PH2SUB)
    if key not in _CACHE:
        _CACHE[key] = build_program(STOP_AFTER, PH2SUB)
    return _CACHE[key]


def kernel(**inputs):
    x = np.ascontiguousarray(np.asarray(inputs["L0_attn_x"], dtype=np.float32))
    w = {k: np.ascontiguousarray(np.asarray(inputs[k], dtype=np.float32))
         for k in inputs}
    nc = _get_nc()

    wq = w["L0_attn_w_q"]; wk = w["L0_attn_w_k"]; wv = w["L0_attn_w_v"]
    wg = w["L0_attn_w_g"]; wo = w["L0_attn_w_o"]
    wupT = np.ascontiguousarray(w["L0_ffn_W_up"].T)
    wdnT = np.ascontiguousarray(w["L0_ffn_W_down"].T)
    wgT = np.ascontiguousarray(w["L0_ffn_W_gate"].T)
    we = w["L0_ffn_W_expert"]
    colify = lambda v: np.ascontiguousarray(v.reshape(16, 128).T)
    common = {
        "wq": wq, "wk": wk, "wv": wv, "wg": wg, "wo": wo,
        "wupT": wupT, "wdnT": wdnT, "we": we, "wgT": wgT,
        "bq": w["L0_attn_b_q"].reshape(1, HID),
        "bk": w["L0_attn_b_k"].reshape(1, HID),
        "bv": w["L0_attn_b_v"].reshape(1, HID),
        "bg": w["L0_attn_b_g"].reshape(1, HID),
        "bo": w["L0_attn_b_o"].reshape(1, HID),
        "gnw": colify(w["L0_attn_gn_weight"]),
        "gnb": colify(w["L0_attn_gn_bias"]),
        "bgcol": colify(w["L0_attn_b_g"]),
    }
    zeros_fm = np.zeros((HID, SPAN), dtype=np.float32)
    in_maps = []
    for core in range(8):
        b, hf = core // 2, core % 2
        xs = np.ascontiguousarray(x[b, hf * SPAN:(hf + 1) * SPAN].T)
        xp = (np.ascontiguousarray(x[b, 0:SPAN].T) if hf == 1 else zeros_fm)
        im = dict(common)
        im["x_fm"] = xs
        im["xp_fm"] = xp
        im["pos_base"] = np.full((128, 1), float(hf * SPAN), dtype=np.float32)
        im["pflag"] = np.full((1, TP), 1.0 if hf == 1 else 0.0, dtype=np.float32)
        in_maps.append(im)

    res = run_bass_kernel_spmd(nc, in_maps, core_ids=list(range(8)))
    out = np.empty((4, 4096, HID), dtype=np.float32)
    for core in range(8):
        b, hf = core // 2, core % 2
        o = np.asarray(res.results[core]["out_fm"])
        out[b, hf * SPAN:(hf + 1) * SPAN] = o.reshape(HID, SPAN).T
    return out
